# version 08 (final)
# baseline (speedup 1.0000x reference)
from collections import defaultdict
import contextlib

import numpy as np
import ml_dtypes

import concourse.bass as bass
import concourse.mybir as mybir
from concourse.bass_utils import run_bass_kernel_spmd
from concourse.bass import IndirectOffsetOnAxis

F32 = mybir.dt.float32
BF16 = mybir.dt.bfloat16
I32 = mybir.dt.int32
AF = mybir.ActivationFunctionType
ALU = mybir.AluOpType
AXX = mybir.AxisListType.X

NCORE = 8
L = 8192
D = 2048
H = 64
HP = H // NCORE
HPP = HP // 2
P = 64
N = 128
TC = 128
NCH = L // TC
TPC = L // NCORE
NTT = TPC // 128
GRP = 16
NGRP = NCH // GRP
E = 16
EPC = E // NCORE
CAP = 1024
I_FF = 1408
IS_FF = 2816
EPS_RMS = 1e-6
EPS_LN = 1e-5
BIGNEG = -1e30
SENT = 20000.0
TWO_PI = float(2.0 * np.pi)
INV_2PI = float(1.0 / (2.0 * np.pi))
MAGIC = float(1.5 * 2.0 ** 23)
CB = 0
CC = HPP * N
CX = 2 * HPP * N
CZ = CX + HPP * P
CT = CZ + HPP * P
CD = CT + HPP * (N // 2)
CL = CD + HPP
PASSW = CL + HPP

ENGS = ("sy", "pe", "dv", "sc", "gp")
SEMS = ("Ssy", "Spe", "Sdv", "Ssc", "Sgp", "Sgw", "Scc", "Sou", "Sx0", "Sx1",
        "SlA0", "SlA1", "SlB0", "SlB1", "Sxc", "Syc", "Swo0", "Swo1", "Ssh0",
        "Ssh1", "Sxw0", "Sxw1", "Sg0", "Sg1", "Seo", "Scl", "Sx2", "Sgt0", "Sgt1", "Scnt", "Scn2", "Seo1", "Syc1")


class Plan:
    def __init__(self, cnt):
        self.streams = {e: [] for e in ENGS}
        self.cnt = cnt

    def op(self, eng, fn, waits=(), inc=None, amt=1):
        ev = None
        if inc is not None:
            self.cnt[inc] += amt
            ev = (inc, self.cnt[inc])
        ws = tuple(w for w in waits if w is not None)
        self.streams[eng].append((fn, ws, (inc, amt) if inc else None))
        return ev

    def barrier(self):
        return [(s, self.cnt[s]) for s in SEMS if self.cnt[s] > 0]


def gate_all(pl, evs):
    for g_ in ENGS:
        pl.op(g_, lambda e: None, waits=evs)


def replay(nc, plan, sems):
    with nc.Block() as block:
        def mk(ename):
            def body(eng):
                seen = {}
                for fn, waits, inc in plan.streams[ename]:
                    for (s, v) in waits:
                        if seen.get(s, -1) < v:
                            eng.wait_ge(sems[s], v)
                            seen[s] = v
                    inst = fn(eng)
                    if inc is not None:
                        if inst is None:
                            inst = eng.nop()
                        inst.then_inc(sems[inc[0]], inc[1])
            return body
        for ename, attr in (("sy", "sync"), ("pe", "tensor"), ("dv", "vector"),
                            ("sc", "scalar"), ("gp", "gpsimd")):
            getattr(block, attr)(mk(ename))



def _perm_cols():
    return np.arange(N).reshape(N // 2, 2).T.reshape(-1)


def host_prepare(inputs):
    f32 = np.float32
    g = {k: np.asarray(v) for k, v in inputs.items()}
    x = np.ascontiguousarray(np.asarray(g["L0_attn_x"]).reshape(L, D).astype(f32, copy=False))
    perm = _perm_cols()

    WxT = g["L0_attn_Wx"].astype(f32).T
    WzT = g["L0_attn_Wz"].astype(f32).T
    WBT = g["L0_attn_WB"].astype(f32).T
    WCT = g["L0_attn_WC"].astype(f32).T
    WthT = g["L0_attn_Wth"].astype(f32).T
    WdtT = g["L0_attn_Wdt"].astype(f32).T
    WlamT = g["L0_attn_Wlam"].astype(f32).T
    WoutT = g["L0_attn_Wout"].astype(f32).T
    WoutP = np.ascontiguousarray(
        WoutT.reshape(32, 128, 4, 512).transpose(2, 0, 1, 3).reshape(4 * 4096, 512))
    A_log = g["L0_attn_A_log"].astype(f32)
    Bn_w = g["L0_attn_Bn_w"].astype(f32)[perm]
    Cn_w = g["L0_attn_Cn_w"].astype(f32)[perm]
    B_bias = g["L0_attn_B_bias"].astype(f32)[:, perm]
    C_bias = g["L0_attn_C_bias"].astype(f32)[:, perm]

    gate_wT = np.ascontiguousarray(g["L0_ffn_gate_w"].astype(f32).T)
    w_gate = g["L0_ffn_w_gate"].astype(f32)
    w_up = g["L0_ffn_w_up"].astype(f32)
    w_down = g["L0_ffn_w_down"].astype(f32)
    swg = np.ascontiguousarray(
        g["L0_ffn_sw_gate"].astype(f32).reshape(D, 22, 128).transpose(1, 0, 2).reshape(22 * D, 128))
    swu = np.ascontiguousarray(
        g["L0_ffn_sw_up"].astype(f32).reshape(D, 22, 128).transpose(1, 0, 2).reshape(22 * D, 128))
    swd = np.ascontiguousarray(
        g["L0_ffn_sw_down"].astype(f32).reshape(22, 128, 4, 512).transpose(2, 0, 1, 3).reshape(4 * IS_FF, 512))

    ident_f = np.eye(128, dtype=f32)
    kk, tt = np.meshgrid(np.arange(128), np.arange(128), indexing="ij")
    consts = dict(
        ident_f=ident_f,
        ident_b=ident_f.astype(ml_dtypes.bfloat16),
        L1incl=np.triu(np.ones((128, 128), dtype=f32)),
        L0strict=np.triu(np.ones((128, 128), dtype=f32), 1),
        Ushift=np.eye(128, k=-1, dtype=f32),
        onehot127=(np.arange(128) == 127).astype(f32).reshape(128, 1),
        onehot127b=(np.arange(128) == 127).astype(ml_dtypes.bfloat16).reshape(128, 1),
        ones_col=np.ones((128, 1), dtype=f32),
        halfpi=np.full((128, 1), np.pi / 2, dtype=f32),
        ones_row=np.ones((1, 128), dtype=f32),
        MBincl=np.where(kk <= tt, 0.0, BIGNEG).astype(f32),
        MBstrict=np.where(kk < tt, 0.0, BIGNEG).astype(f32),
        M127col=np.where(np.arange(128) == 127, BIGNEG, 0.0).astype(f32).reshape(128, 1),
        IOTA16=np.tile(np.arange(16, dtype=f32), (128, 1)),
        RIOTA16=np.tile(np.arange(15, -1, -1, dtype=f32), (128, 1)),
    )
    oh = np.zeros((HPP, HPP * 128), dtype=f32)
    for h in range(HPP):
        oh[h, h * 128:(h + 1) * 128] = 1.0
    consts["onehotHPP"] = oh

    per_core = []
    for c in range(NCORE):
        d = dict(consts)
        d["x_slice"] = np.ascontiguousarray(x[c * TPC:(c + 1) * TPC])
        blocks = []
        for p in range(2):
            hs = c * HP + p * HPP
            he = hs + HPP
            Bc = WBT[:, hs * N:he * N].reshape(D, HPP, N)[:, :, perm].reshape(D, HPP * N)
            Cc = WCT[:, hs * N:he * N].reshape(D, HPP, N)[:, :, perm].reshape(D, HPP * N)
            blocks.append(np.concatenate(
                [Bc, Cc, WxT[:, hs * P:he * P], WzT[:, hs * P:he * P],
                 WthT[:, hs * (N // 2):he * (N // 2)], WdtT[:, hs:he],
                 WlamT[:, hs:he]], axis=1))
        d["attnW"] = np.ascontiguousarray(np.concatenate(blocks, axis=0))
        d["WoutT"] = WoutP
        d["A_log_col"] = np.ascontiguousarray(A_log[c * HP:(c + 1) * HP].reshape(HP, 1))
        d["BCnw"] = np.ascontiguousarray(
            np.concatenate([np.tile(Bn_w, (128, HPP)), np.tile(Cn_w, (128, HPP))], axis=1))
        bc = []
        for p in range(2):
            hs = c * HP + p * HPP
            bc.append(np.tile(np.concatenate(
                [B_bias[hs:hs + HPP].reshape(HPP * N), C_bias[hs:hs + HPP].reshape(HPP * N)]
            ).reshape(1, 2 * HPP * N), (128, 1)))
        d["BCbias"] = np.ascontiguousarray(np.concatenate(bc, axis=1))
        d["gate_wT"] = gate_wT
        d["swg"] = swg
        d["swu"] = swu
        d["swd"] = swd
        d["wg2"] = np.ascontiguousarray(
            w_gate[2 * c:2 * c + 2].reshape(2, D, 11, 128).transpose(0, 2, 1, 3).reshape(2, 11 * D, 128))
        d["wu2"] = np.ascontiguousarray(
            w_up[2 * c:2 * c + 2].reshape(2, D, 11, 128).transpose(0, 2, 1, 3).reshape(2, 11 * D, 128))
        d["wd2"] = np.ascontiguousarray(
            w_down[2 * c:2 * c + 2].reshape(2, 11, 128, 4, 512).transpose(0, 3, 1, 2, 4).reshape(2, 4 * I_FF, 512))
        d["tokids"] = np.ascontiguousarray(
            (np.arange(TPC, dtype=f32) + c * TPC).reshape(NTT, 128).T)
        p_ = np.arange(128)
        d["ygidx"] = np.ascontiguousarray(
            (((c // 2) * 4096 + np.arange(32)[None, :] * 128 + p_[:, None]) * 2048
             + (c % 2) * TPC).astype(np.int32))
        ch_ = np.arange(16)
        d["clmidx"] = np.ascontiguousarray(
            ((((ch_ // 8 + 2 * c) * CAP) + (ch_ % 8) * 128)[None, :] + p_[:, None]
             ).astype(np.int32) * 2)
        d["coremask"] = np.ascontiguousarray(
            (np.arange(NCORE) < c).astype(f32).reshape(NCORE, 1))
        d["NEG16"] = np.full((128, 16), -1e30, dtype=f32)
        per_core.append(d)
    return per_core



def build_program(shapes_dtypes, upto=99, debug=(), nchunks_attn=NCH, two_pass=True):
    nc = bass.Bass("TRN2", target_bir_lowering=False, debug=False)
    din = {}
    for k, (shp, dt) in shapes_dtypes.items():
        if dt == ml_dtypes.bfloat16:
            bdt = BF16
        elif np.dtype(dt) == np.int32:
            bdt = I32
        else:
            bdt = F32
        din[k] = nc.dram_tensor(k, list(shp), bdt, kind="ExternalInput").ap()

    d_out = nc.dram_tensor("out", [TPC, D], F32, kind="ExternalOutput").ap()
    dbg_outs = {}

    def dbg_decl(name, shape, dt=F32):
        if name in debug:
            dbg_outs[name] = nc.dram_tensor("dbg_" + name, list(shp := shape), dt,
                                            kind="ExternalOutput").ap()
            return dbg_outs[name]
        return None

    xln_c = nc.dram_tensor("xln_c", [2, D, TPC // 2], BF16)
    xln_g = nc.dram_tensor("xln_g", [2, NCORE * D, TPC // 2], BF16)
    y_c = nc.dram_tensor("y_c", [4, HP * P, L // 4], BF16)
    y_g = nc.dram_tensor("y_g", [4, NCORE * HP * P, L // 4], BF16)
    x2ln_c = nc.dram_tensor("x2ln_c", [TPC, D], BF16)
    x2ln_g = nc.dram_tensor("x2ln_g", [NCORE * TPC, D], BF16)
    cnt_c = nc.dram_tensor("cnt_c", [1, E], F32)
    cnt_g = nc.dram_tensor("cnt_g", [NCORE, E], F32)
    clm_c = nc.dram_tensor("clm_c", [E * CAP, 2], F32)
    clm_g = nc.dram_tensor("clm_g", [E * CAP, 2], F32)
    eout_c = nc.dram_tensor("eout_c", [EPC * CAP, D], BF16)
    x2_d = nc.dram_tensor("x2_d", [TPC, D], F32)
    shps_d = nc.dram_tensor("shps_d", [TPC, D], F32)
    eout_g = nc.dram_tensor("eout_g", [NCORE * EPC * CAP, D], BF16)

    cnt = defaultdict(int)
    outer = contextlib.ExitStack()
    EO = outer.enter_context
    sems = {s: EO(nc.semaphore(s)) for s in SEMS}
    rg = [list(range(NCORE))]

    CONST_SPECS = [
        ("ident_f", [128, 128], F32), ("ident_b", [128, 128], BF16),
        ("onehot127b", [128, 1], BF16),
        ("L1incl", [128, 128], F32), ("L0strict", [128, 128], F32),
        ("Ushift", [128, 128], F32), ("onehot127", [128, 1], F32),
        ("ones_col", [128, 1], F32), ("ones_row", [1, 128], F32),
        ("halfpi", [128, 1], F32),
        ("MBincl", [128, 128], F32), ("MBstrict", [128, 128], F32),
        ("M127col", [128, 1], F32), ("onehotHPP", [HPP, HPP * 128], F32),
        ("IOTA16", [128, 16], F32), ("RIOTA16", [128, 16], F32),
        ("A_log_col", [HP, 1], F32), ("BCnw", [128, 2 * HPP * N], F32),
        ("BCbias", [128, 4 * HPP * N], F32), ("tokids", [128, NTT], F32),
        ("ygidx", [128, 32], I32), ("clmidx", [128, 16], I32),
        ("coremask", [NCORE, 1], F32), ("NEG16", [128, 16], F32),
    ]
    C = {nm: EO(nc.sbuf_tensor("c_" + nm, shp, dt)) for nm, shp, dt in CONST_SPECS}
    A_row = EO(nc.sbuf_tensor("A_row", [HP, 1], F32))
    A_rowT = EO(nc.sbuf_tensor("A_rowT", [1, HP], F32))
    A_bc = EO(nc.sbuf_tensor("A_bc", [128, HP], F32))
    pb = [EO(nc.psum_tensor(f"pb{i}", [128, 512], F32)) for i in range(5)]
    pbT = EO(nc.psum_tensor("pbT", [128, 2048], BF16))
    pb7 = EO(nc.psum_tensor("pb7", [128, 512], F32))

    idf = C["ident_f"]
    idb = C["ident_b"]
    rbnd = nc.gpsimd.alloc_register()

    pl = Plan(cnt)
    ev_c = None
    for nm in C:
        ev_c = pl.op("sy", lambda e, nm=nm: e.dma_start(out=C[nm][:], in_=din[nm]),
                     inc="Ssy", amt=16)
    ev = pl.op("sc", lambda e: e.activation(A_row[:], C["A_log_col"][:], AF.Exp),
               waits=[ev_c], inc="Ssc")
    ev = pl.op("dv", lambda e: e.tensor_scalar(A_row[:], A_row[:], -1.0, None, ALU.mult),
               waits=[ev], inc="Sdv")
    ev = pl.op("dv", lambda e: e.drain(), waits=[ev], inc="Sdv")
    evt = pl.op("pe", lambda e: e.transpose(pb[0][0:1, 0:HP], A_row[:], idf[0:HP, 0:HP]),
                waits=[ev], inc="Spe")
    ev = pl.op("dv", lambda e: e.tensor_copy(A_rowT[:], pb[0][0:1, 0:HP]),
               waits=[evt], inc="Sdv")
    evt = pl.op("pe", lambda e: e.matmul(pb[0][:, 128:128 + HP], C["ones_row"][:],
                                         A_rowT[:], start=True, stop=True),
                waits=[ev], inc="Spe")
    ev_abc = pl.op("dv", lambda e: e.tensor_copy(A_bc[:], pb[0][:, 128:128 + HP]),
                   waits=[evt], inc="Sdv")
    PH0 = pl.barrier()
    replay(nc, pl, sems)

    pl = Plan(cnt)
    ph1 = contextlib.ExitStack()
    EP = ph1.enter_context
    xt = [EP(nc.sbuf_tensor(f"p1xt{i}", [128, D], F32)) for i in range(2)]
    scr = EP(nc.sbuf_tensor("p1scr", [128, D], F32))
    scr2 = EP(nc.sbuf_tensor("p1scr2", [128, D], F32))
    lncol = EP(nc.sbuf_tensor("p1col", [128, 8], F32))
    xlnb = EP(nc.sbuf_tensor("p1xlnb", [128, D], BF16))
    fmst = EP(nc.sbuf_tensor("p1fm", [128, 16 * TPC], BF16))

    def emit_ln_tile(pl, src_tile_ap_fn, out_bf, i, ev_ld, scr, scr2, lncol,
                     out_free_ev=None):
        ev = pl.op("dv", lambda e: e.tensor_reduce(lncol[:, 0:1], src_tile_ap_fn(), AXX, ALU.add),
                   waits=[ev_ld], inc="Sdv")
        ev = pl.op("dv", lambda e: (e.drain(), e.tensor_scalar(
            lncol[:, 1:2], lncol[:, 0:1], 1.0 / D, None, ALU.mult))[1],
            waits=[ev], inc="Sdv")
        ev = pl.op("dv", lambda e: (e.drain(), e.tensor_scalar(
            scr[:], src_tile_ap_fn(), lncol[:, 1:2], None, ALU.subtract))[1],
            waits=[ev], inc="Sdv")
        ev = pl.op("sc", lambda e: e.activation(scr2[:], scr[:], AF.Square,
                                                accum_out=lncol[:, 2:3]),
                   waits=[ev], inc="Ssc")
        ev = pl.op("dv", lambda e: e.tensor_scalar(lncol[:, 3:4], lncol[:, 2:3],
                                                   1.0 / D, EPS_LN, ALU.mult, ALU.add),
                   waits=[ev], inc="Sdv")
        ev = pl.op("sc", lambda e: e.activation(lncol[:, 4:5], lncol[:, 3:4], AF.Ln),
                   waits=[ev], inc="Ssc")
        ev = pl.op("sc", lambda e: (e.drain(), e.activation(
            lncol[:, 5:6], lncol[:, 4:5], AF.Exp, scale=-0.5))[1],
            waits=[ev], inc="Ssc")
        ofe = out_free_ev if isinstance(out_free_ev, list) else [out_free_ev]
        ev = pl.op("dv", lambda e: e.tensor_scalar(out_bf[:], scr[:], lncol[:, 5:6],
                                                   None, ALU.mult),
                   waits=[ev] + list(ofe), inc="Sdv")
        return ev

    ev_xt_free = [None, None]
    ev_xlnb_free = None
    evT_slot = [None, None]
    last_evac = None
    for i in range(NTT):
        ev_ld = pl.op("sy", lambda e, i=i: e.dma_start(
            out=xt[i % 2][:], in_=din["x_slice"][i * 128:(i + 1) * 128, :]),
            waits=[PH0[0], ev_xt_free[i % 2]], inc="Sx0" if i % 2 == 0 else "Sx1", amt=16)
        ev_bf = emit_ln_tile(pl, (lambda i=i: xt[i % 2][:]), xlnb, i, ev_ld,
                             scr, scr2, lncol, out_free_ev=ev_xlnb_free)
        ev_xt_free[i % 2] = ev_bf
        for kc in range(16):
            bank = kc % 2
            slot = bank * 8 + (kc // 2) % 8
            evt = pl.op("pe", lambda e, kc=kc, slot=slot: e.transpose(
                pbT[:, slot * 128:(slot + 1) * 128],
                xlnb[:, kc * 128:(kc + 1) * 128], idb[:]),
                waits=[ev_bf, evT_slot[bank]], inc="Spe")
            evv = pl.op("dv", lambda e, kc=kc, i=i, slot=slot: e.tensor_copy(
                fmst[:, kc * TPC + i * 128: kc * TPC + (i + 1) * 128],
                pbT[:, slot * 128:(slot + 1) * 128]),
                waits=[evt], inc="Sdv")
            evT_slot[bank] = evv
            last_evac = evv
        ev_xlnb_free = evt

    ev_ag1h = [None, None]
    for hf in range(2):
        ev_fmdma = None
        for kc in range(16):
            ev_fmdma = pl.op("sy", lambda e, kc=kc, hf=hf: e.dma_start(
                out=xln_c[hf, kc * 128:(kc + 1) * 128, :],
                in_=fmst[:, kc * TPC + hf * 512: kc * TPC + hf * 512 + 512]),
                waits=[last_evac], inc="Sxc", amt=16)
        ev_ag1h[hf] = pl.op("gp", lambda e, hf=hf: e.collective_compute(
            "AllGather", ALU.bypass, rg,
            ins=[xln_c.ap()[hf].opt()], outs=[xln_g.ap()[hf].opt()]),
            waits=[ev_fmdma], inc="Scc")
    ev_ag1 = ev_ag1h[1]
    PH1 = pl.barrier()
    replay(nc, pl, sems)
    ph1.close()

    if "xln_g" in debug:
        pl = Plan(cnt)
        ph = contextlib.ExitStack()
        t = ph.enter_context(nc.sbuf_tensor("dbgt", [128, 4096], BF16))
        dbg_outs["xln_g"] = nc.dram_tensor("dbg_xln_g", [128, 4096], BF16,
                                           kind="ExternalOutput").ap()
        ev = pl.op("sy", lambda e: e.dma_start(out=t[:, 0:512],
                                               in_=xln_g[0, 0:128, :]),
                   waits=[ev_ag1], inc="Ssy", amt=16)
        ev = pl.op("sy", lambda e: e.dma_start(
            out=t[:, 2048:2560], in_=xln_g[0, 2048 * 3 + 128:2048 * 3 + 256, :]),
            waits=[ev_ag1], inc="Ssy", amt=16)
        ev = pl.op("sy", lambda e: e.dma_start(out=dbg_outs["xln_g"], in_=t[:]),
                   waits=[ev], inc="Ssy", amt=16)
        pl.op("sy", lambda e: None, waits=pl.barrier())
        replay(nc, pl, sems)
        ph.close()

    def finish_dummy():
        pl = Plan(cnt)
        ph = contextlib.ExitStack()
        z = ph.enter_context(nc.sbuf_tensor("zz", [128, D], F32))
        ev = pl.op("dv", lambda e: e.memset(z[:], 0.0), waits=pl.barrier(), inc="Sdv")
        evs = []
        for i in range(NTT):
            evs.append(pl.op("sy", lambda e, i=i: e.dma_start(
                out=d_out[i * 128:(i + 1) * 128, :], in_=z[:]),
                waits=[ev], inc="Sou", amt=16))
        pl.op("sy", lambda e: None, waits=[evs[-1]])
        replay(nc, pl, sems)
        ph.close()
        outer.close()
        return nc, dbg_outs

    if upto <= 1:
        return finish_dummy()

    nchunks = min(nchunks_attn, NCH)
    ph2 = contextlib.ExitStack()
    EP = ph2.enter_context
    aW = EP(nc.sbuf_tensor("aW", [128, 16 * PASSW], BF16))
    lhsA = EP(nc.sbuf_tensor("lhsA", [128, 2 * 16 * 256], BF16))
    lhsB = EP(nc.sbuf_tensor("lhsB", [128, 2 * 16 * 256], BF16))
    smalls = EP(nc.sbuf_tensor("smalls", [128, GRP * 48], F32))
    rb4 = EP(nc.sbuf_tensor("rb4", [128, 512], F32))
    acdtJs = EP(nc.sbuf_tensor("acdtJs", [1, 2 * 512], F32))
    thred = EP(nc.sbuf_tensor("thred", [128, 8 * 256], F32))
    csb = EP(nc.sbuf_tensor("csb", [128, GRP * 512], BF16))
    offrow = EP(nc.sbuf_tensor("offrow", [1, 256], F32))
    albr = EP(nc.sbuf_tensor("albr", [1, 32], F32))
    sw = EP(nc.sbuf_tensor("sw", [128, 9 * 32], F32))
    thx = EP(nc.sbuf_tensor("thx", [128, 8 * 264], F32))
    dtc = EP(nc.sbuf_tensor("dtc", [128, 32], F32))
    lamc = EP(nc.sbuf_tensor("lamc", [128, 32], F32))
    w0 = EP(nc.sbuf_tensor("w0", [128, 256], F32))
    tb0 = EP(nc.sbuf_tensor("tb0", [128, 256], F32))
    gw0 = EP(nc.sbuf_tensor("gw0", [128, 256], F32))
    gw1 = EP(nc.sbuf_tensor("gw1", [128, 256], F32))
    mixf = EP(nc.sbuf_tensor("mixf", [128, 2 * 1024], F32))
    scrB = EP(nc.sbuf_tensor("scrB", [128, 1024], F32))
    msq = EP(nc.sbuf_tensor("msq", [128, 32], F32))
    mixb = EP(nc.sbuf_tensor("mixb", [128, 2 * 1024], BF16))
    xhb = EP(nc.sbuf_tensor("xhb", [128, 2 * 256], BF16))
    zb = EP(nc.sbuf_tensor("zb", [128, 2 * 256], F32))
    rt = [EP(nc.sbuf_tensor(f"rt{i}", [128, 512], F32)) for i in range(4)]
    ex1 = rt[0]
    ex2 = rt[1]
    Bfm = EP(nc.sbuf_tensor("Bfm", [128, 2 * HPP * 128], BF16))
    Cfm = EP(nc.sbuf_tensor("Cfm", [128, HPP * 128], BF16))
    mcomb = EP(nc.sbuf_tensor("mcomb", [128, HPP * 128], BF16))
    lhsTw = EP(nc.sbuf_tensor("lhsTw", [128, HPP * 128], BF16))
    bndw = EP(nc.sbuf_tensor("bndw", [1, HPP * 128], BF16))
    bexp = EP(nc.sbuf_tensor("bexp", [1, HPP * 128], F32))
    w2b1 = EP(nc.sbuf_tensor("w2b1", [1, 4], F32))
    hbrow = EP(nc.sbuf_tensor("hbrow", [1, HPP * 128], BF16))
    hst = EP(nc.sbuf_tensor("hst", [128, 2 * HPP * P], BF16))
    y_tm = EP(nc.sbuf_tensor("y_tm", [128, HPP * P], F32))
    ygb = EP(nc.sbuf_tensor("ygb", [128, HPP * P], BF16))
    yfmg = EP(nc.sbuf_tensor("yfmg", [128, 2 * GRP * 128], BF16))
    bndst = EP(nc.sbuf_tensor("bndst", [1, 2 * 768], BF16))
    if nchunks < NCH:
        plz = Plan(cnt)
        zc = mixb
        evz0 = plz.op("dv", lambda e: e.memset(mixb[:, 0:2048], 0.0), inc="Sdv")
        evzd = None
        for rb in range(4):
            for cq in range(4):
                evzd = plz.op("sy", lambda e, rb=rb, cq=cq: e.dma_start(
                    out=y_c[cq, rb * 128:(rb + 1) * 128, :],
                    in_=mixb[:, 0:2048]), waits=[evz0], inc="Ssy", amt=16)
        plz.op("dv", lambda e: e.memset(mixb[:, 0:2048], 0.0), waits=[evzd], inc="Sdv")
        replay(nc, plz, sems)

    S_DT, S_ACDT, S_UG, S_UB, S_UBS, S_EA, S_AL, S_SE, S_W2, S_ADT = range(10)

    def scol(ci, q):
        return q * 64 + (ci % GRP) * 4

    evb_w = {}
    evb_r = {}
    evT_w = {}
    evT_r = {}
    BANK_T0, BANK_T1 = "TA", "TB"

    def pe_bank(pl, bank, fn, extra=(), start_new=True):
        ev = pl.op("pe", fn, waits=[evb_r.get(bank)] + list(extra) if start_new
                   else list(extra), inc="Spe")
        evb_w[bank] = ev
        return ev

    def rd_bank(pl, eng, bank, fn, extra=(), sem=None):
        ev = pl.op(eng, fn, waits=[evb_w.get(bank)] + list(extra),
                   inc=sem or ("Sdv" if eng == "dv" else "Ssc"))
        evb_r[bank] = ev
        return ev

    W_DT = [None]

    ev_aw = None
    PASS_HEADS = HPP

    def emit_pass(p, nch, ev_aw_free, ev_yfmg_free0=None, gate=None, p0_flush=None):
        flush_evs = []
        ag_evs = []
        ev_aw = None
        ev_lhsA_free = [None, None]
        ev_lhsB_free = [None, None]
        ev_lhsA_ld = {}
        ev_lhsB_ld = {}
        ev_off_w = None
        ev_smalls = {}
        ev_aarg = {}
        ev_cs = {}
        ev_mixb_rd = [[], []]
        ev_mixf_rd = [[], []]
        ev_xh_rd = [[], []]
        ev_z_rd = [None, None]
        ev_Bfm_rd = [[], []]
        ev_Cfm_rd = [[] for _ in range(HPP)]
        ev_h_w = {}
        ev_h_rd = {}
        ev_bnd_w = [None, None]
        ev_bnd_rd = [[], []]
        ev_gw_rd = [None]
        ev_thx_w = {}
        ev_thx_free = {}
        ev_ths_j = {}
        ev_dlc_free = []
        ev_q_pe_rd = [None]
        ev_ex_rd = [None]
        ev_bexp_rd = [None]
        ev_hbrow_rd = [None]
        ev_pb2bnd_rd = [None]
        ev_lhsTw_rd = [None]
        ev_mcomb_rd = [None]
        ev_yfmg_free = ev_yfmg_free0
        ev_yfm_last = None
        ev_proj_last = None
        ev_slot_free = {}
        ev_thred_rd = {}
        ev_csb_rd = {}
        ev_tsj_rd = None
        ev_r4_rd = [None]
        ev_aJ_rd = {}
        ev_scrB_rd = [None]
        evTb_r = [None, None]
        tcounter = [0, 0]

        def tslot(bank):
            if bank == 0:
                s = tcounter[0] % 8
            else:
                s = 8 + tcounter[1] % 2
            tcounter[bank] += 1
            return s

        def do_transpose(pl, src_ap_fn, waits):
            bank = (tcounter[0] + tcounter[1]) % 2
            s = tslot(bank)
            ev = pl.op("pe", lambda e, s=s: e.transpose(
                pbT[:, s * 128:(s + 1) * 128], src_ap_fn(), idb[:]),
                waits=list(waits) + [evTb_r[bank]], inc="Spe")
            return s, ev, bank

        def evac_T(pl, s, bank, ev, out_ap_fn, extra=()):
            evv = pl.op("dv", lambda e, s=s: e.tensor_copy(
                out_ap_fn(), pbT[:, s * 128:(s + 1) * 128]),
                waits=[ev] + list(extra), inc="Sdv")
            evTb_r[bank] = evv
            return evv

        def evac_T_sc(pl, s, bank, ev, out_ap_fn, extra=()):
            evv = pl.op("sc", lambda e, s=s: e.activation(
                out_ap_fn(), pbT[:, s * 128:(s + 1) * 128], AF.Copy),
                waits=[ev] + list(extra), inc="Ssc")
            evTb_r[bank] = evv
            return evv

        def stA(pl, ci):
            nonlocal ev_proj_last
            j = ci % GRP
            g2 = (ci // 2) % 2
            if ci % 2 == 0:
                c0 = (ci * TC) // TPC
                tcol = (ci * TC) % TPC
                ev_lhsA_ld[ci // 2] = pl.op("sy", lambda e, c0=c0, tcol=tcol, g2=g2: e.dma_start(
                    out=lhsA[:, g2 * 4096:(g2 + 1) * 4096].rearrange("p (kc t) -> p kc t", kc=16),
                    in_=xln_g[tcol // 512, c0 * D:(c0 + 1) * D,
                              tcol % 512:tcol % 512 + 256].rearrange(
                        "(kc q) t -> q kc t", q=128)),
                    waits=[ev_ag1h[tcol // 512], ev_lhsA_free[g2]],
                    inc="SlA0" if g2 == 0 else "SlA1", amt=16)
            ev_ld = ev_lhsA_ld[ci // 2]
            sub = ci % 2
            evp = None
            for kc in range(16):
                evp = pe_bank(pl, 2, lambda e, kc=kc, g2=g2, sub=sub: e.matmul(
                    pb[2][:, 0:264],
                    lhsA[:, g2 * 4096 + kc * 256 + sub * 128: g2 * 4096 + kc * 256 + (sub + 1) * 128],
                    aW[:, kc * PASSW + CT: kc * PASSW + CT + 264],
                    start=(kc == 0), stop=(kc == 15)),
                    extra=[ev_ld, ev_aw], start_new=(kc == 0))
            ev_lhsA_free[g2] = evp
            ev_proj_last = evp
            js = j % 8
            ev_thx_w[j] = rd_bank(pl, "dv", 2, lambda e, js=js: e.tensor_copy(
                thx[:, js * 264:(js + 1) * 264], pb[2][:, 0:264]),
                extra=[ev_thx_free.get(js)])

        def smalls_batch(pl, j0, j1):
            n = j1 - j0
            X = n * 4
            q = lambda k: smalls[:, k * 64 + j0 * 4: k * 64 + j1 * 4]
            sws = lambda k: sw[:, k * 32: k * 32 + X]
            thv = thx[:].rearrange("p (j c) -> p j c", j=8)
            evw = [ev_thx_w[j] for j in range(j0, j1)]
            gate = [ev_slot_free.get(j) for j in range(j0, j1)]
            Ab = A_bc[:, p * HPP:(p + 1) * HPP]
            Ab3 = Ab.unsqueeze(1).broadcast_to([128, n, HPP])
            j3 = lambda ap: ap.rearrange("p (j h) -> p j h", j=n)
            ev = pl.op("dv", lambda e: e.tensor_copy(
                j3(dtc[:, 0:X]), thv[:, 0:n, 256:260]),
                waits=evw + list(ev_dlc_free), inc="Sdv")
            ev = pl.op("dv", lambda e: (e.drain(), e.tensor_copy(
                j3(lamc[:, 0:X]), thv[:, 0:n, 260:264]))[1], waits=[ev], inc="Sdv")
            ev = pl.op("sc", lambda e: e.activation(sws(0), dtc[:, 0:X], AF.Abs), waits=[ev], inc="Ssc")
            ev = pl.op("sc", lambda e: (e.drain(), e.activation(sws(1), sws(0), AF.Exp, scale=-1.0))[1], waits=[ev], inc="Ssc")
            ev = pl.op("sc", lambda e: (e.drain(), e.activation(sws(2), sws(1), AF.Ln, bias=1.0))[1], waits=[ev], inc="Ssc")
            ev = pl.op("dv", lambda e: e.tensor_scalar(sws(3), dtc[:, 0:X], 0.0, None, ALU.max), waits=[ev], inc="Sdv")
            ev_dt = pl.op("dv", lambda e: (e.drain(), e.tensor_tensor(q(S_DT), sws(3), sws(2), ALU.add))[1],
                          waits=[ev] + gate + [ev_q_pe_rd[0]], inc="Sdv")
            ev_ths = pl.op("dv", lambda e: (e.drain(), e.tensor_tensor(
                thv[:, 0:n, 0:256].rearrange("p j (h i) -> p j h i", h=HPP),
                thv[:, 0:n, 0:256].rearrange("p j (h i) -> p j h i", h=HPP),
                j3(q(S_DT)).unsqueeze(3).broadcast_to([128, n, HPP, 64]), ALU.mult))[1],
                waits=[ev_dt], inc="Sdv")
            for j in range(j0, j1):
                ev_ths_j[j] = ev_ths
            ev_labs = pl.op("sc", lambda e: e.activation(sws(0), lamc[:, 0:X], AF.Abs), waits=[ev_dt], inc="Ssc")
            ev = pl.op("sc", lambda e: (e.drain(), e.activation(sws(1), sws(0), AF.Exp, scale=-1.0))[1], waits=[ev_labs], inc="Ssc")
            ev = pl.op("sc", lambda e: (e.drain(), e.activation(sws(2), sws(1), AF.Ln, bias=1.0))[1], waits=[ev], inc="Ssc")
            ev = pl.op("dv", lambda e: (e.drain(), e.tensor_scalar(sws(3), lamc[:, 0:X], 0.0, None, ALU.max))[1], waits=[ev], inc="Sdv")
            ev = pl.op("dv", lambda e: (e.drain(), e.tensor_tensor(sws(4), sws(3), sws(2), ALU.add))[1], waits=[ev], inc="Sdv")
            ev = pl.op("dv", lambda e: (e.drain(), e.tensor_scalar(sws(5), lamc[:, 0:X], -1.0, 0.0, ALU.mult, ALU.max))[1], waits=[ev], inc="Sdv")
            ev_dlc = pl.op("dv", lambda e: (e.drain(), e.tensor_tensor(sws(6), sws(5), sws(2), ALU.add))[1], waits=[ev], inc="Sdv")
            ev_dlc_free[:] = [ev_dlc, ev_labs]
            ev = pl.op("sc", lambda e: e.activation(sws(0), q(S_DT), AF.Ln), waits=[ev_dlc, ev_dt], inc="Ssc")
            ev_adt = pl.op("dv", lambda e: e.tensor_tensor(j3(q(S_ADT)), j3(q(S_DT)), Ab3, ALU.mult),
                           waits=[ev_dt, ev_abc], inc="Sdv")
            ev = pl.op("dv", lambda e: (e.drain(), e.tensor_tensor(sws(2), sws(0), sws(6), ALU.subtract))[1], waits=[ev, ev_adt], inc="Sdv")
            ev = pl.op("dv", lambda e: (e.drain(), e.tensor_tensor(sws(3), sws(0), sws(4), ALU.subtract))[1], waits=[ev], inc="Sdv")
            ev_lgb = pl.op("dv", lambda e: (e.drain(), e.tensor_tensor(sws(4), sws(3), q(S_ADT), ALU.add))[1], waits=[ev], inc="Sdv")
            ev_cdt = pe_bank(pl, 2, lambda e: e.matmul(pb[2][:, 264:264 + X], C["L1incl"][:], q(S_DT), start=True, stop=True), extra=[ev_dt])
            ev_acdt = rd_bank(pl, "dv", 2, lambda e: e.tensor_tensor(j3(q(S_ACDT)), j3(pb[2][:, 264:264 + X]), Ab3, ALU.mult))
            ev_ug = pl.op("dv", lambda e: (e.drain(), e.tensor_tensor(q(S_UG), sws(2), q(S_ACDT), ALU.subtract))[1], waits=[ev_lgb, ev_acdt], inc="Sdv")
            ev_ub = pl.op("dv", lambda e: (e.drain(), e.tensor_tensor(q(S_UB), sws(4), q(S_ACDT), ALU.subtract))[1], waits=[ev_ug], inc="Sdv")
            ev_sh = pe_bank(pl, 2, lambda e: e.matmul(pb[2][:, 296:296 + X], C["Ushift"][:], q(S_UB), start=True, stop=True), extra=[ev_ub])
            ev_ubs = rd_bank(pl, "dv", 2, lambda e: e.tensor_tensor(
                q(S_UBS), pb[2][:, 296:296 + X], C["M127col"][:, 0:1].broadcast_to([128, X]), ALU.add))
            ev_ea = pl.op("sc", lambda e: e.activation(q(S_EA), q(S_ACDT), AF.Exp),
                          waits=[ev_acdt] + gate, inc="Ssc")
            ev = pe_bank(pl, 2, lambda e: e.matmul(pb[2][0:1, 328:328 + X], C["onehot127"][:], q(S_ACDT), start=True, stop=True), extra=[ev_acdt])
            ev = rd_bank(pl, "dv", 2, lambda e: e.tensor_copy(albr[:, 0:X], pb[2][0:1, 328:328 + X]))
            ev = pe_bank(pl, 2, lambda e: e.matmul(pb[2][:, 360:360 + X], C["ones_row"][:], albr[:, 0:X], start=True, stop=True), extra=[ev])
            ev_q_pe_rd[0] = ev
            ev_al = rd_bank(pl, "dv", 2, lambda e: e.tensor_copy(q(S_AL), pb[2][:, 360:360 + X]))
            ev_se = pl.op("sc", lambda e: e.activation(q(S_SE), q(S_AL), AF.Exp),
                          waits=[ev_al] + gate, inc="Ssc")
            ev = pl.op("dv", lambda e: e.tensor_tensor(sws(5), q(S_UG), q(S_AL), ALU.add), waits=[ev_ug, ev_al], inc="Sdv")
            ev = pl.op("sc", lambda e: (e.drain(), e.activation(sws(6), sws(5), AF.Exp))[1], waits=[ev], inc="Ssc")
            ev2 = pl.op("dv", lambda e: e.tensor_tensor(sws(7), q(S_UBS), q(S_AL), ALU.add), waits=[ev_ubs, ev_al], inc="Sdv")
            ev2 = pl.op("sc", lambda e: (e.drain(), e.activation(sws(8), sws(7), AF.Exp))[1], waits=[ev2], inc="Ssc")
            ev_w2 = pl.op("dv", lambda e: e.tensor_tensor(q(S_W2), sws(6), sws(8), ALU.add), waits=[ev, ev2], inc="Sdv")
            evq = dict(w2=ev_w2, se=ev_se, ea=ev_ea, ubs=ev_ubs,
                       ug=ev_ug, ub=ev_ub, al=ev_al)
            for j in range(j0, j1):
                ev_smalls[j] = evq

        def ang_chunk(pl, ci):
            nonlocal ev_off_w
            j = ci % GRP
            js = j % 8
            tv = thx[:, js * 264: js * 264 + 256]
            ev_ths = ev_ths_j[j]
            ev = pe_bank(pl, 7, lambda e, tv=tv: e.matmul(pb7[:, 0:256], C["L1incl"][:], tv, start=True, stop=False), extra=[ev_ths])
            ev_ang = pe_bank(pl, 7, lambda e: e.matmul(pb7[:, 0:256], C["ones_row"][:], offrow[:], start=False, stop=True), extra=[ev_off_w], start_new=False)
            ev_osum = pe_bank(pl, 7, lambda e, tv=tv: e.matmul(
                pb7[0:1, 256:512], C["ones_col"][:], tv, start=True, stop=True),
                extra=[ev_ths], start_new=False)
            ev_thx_free[js] = ev_osum
            ev_off_w = rd_bank(pl, "dv", 7, lambda e: e.tensor_tensor(
                offrow[:], offrow[:], pb7[0:1, 256:512], ALU.add), extra=[ev_ang, ev_off_w])
            cbase = js * 256
            ev = rd_bank(pl, "dv", 7, lambda e: e.tensor_scalar(w0[:], pb7[:, 0:256], INV_2PI, MAGIC, ALU.mult, ALU.add))
            ev = pl.op("dv", lambda e: (e.drain(), e.tensor_scalar(w0[:], w0[:], MAGIC, None, ALU.subtract))[1], waits=[ev], inc="Sdv")
            ev_sa = rd_bank(pl, "dv", 7, lambda e, cbase=cbase: e.scalar_tensor_tensor(
                thred[:, cbase:cbase + 256], w0[:], -TWO_PI, pb7[:, 0:256], ALU.mult, ALU.add),
                extra=[ev, ev_thred_rd.get(js)])
            ev_sa = pl.op("dv", lambda e, cbase=cbase: (e.drain(), e.tensor_scalar(
                thred[:, cbase:cbase + 256], thred[:, cbase:cbase + 256],
                float(np.pi), -float(np.pi), ALU.min, ALU.max))[1],
                waits=[ev_sa], inc="Sdv")
            ev_aarg[j] = ev_sa

        def trig_half(pl, j0, j1):
            for j in range(j0, j1):
                sa = ev_aarg[j]
                js = j % 8
                ev_s = pl.op("sc", lambda e, j=j, js=js: e.activation(
                    csb[:, j * 512:j * 512 + 256], thred[:, js * 256:(js + 1) * 256], AF.Sin),
                    waits=[sa, ev_csb_rd.get(j)], inc="Ssc")
                ev = pl.op("sc", lambda e, js=js: (e.drain(), e.activation(
                    tb0[:], thred[:, js * 256:(js + 1) * 256], AF.Abs))[1],
                    waits=[sa], inc="Ssc")
                ev_c = pl.op("sc", lambda e, j=j: (e.drain(), e.activation(
                    csb[:, j * 512 + 256:j * 512 + 512], tb0[:], AF.Sin,
                    scale=-1.0, bias=C["halfpi"][:]))[1],
                    waits=[ev], inc="Ssc")
                ev_cs[j] = (ev_s, ev_c)
                ev_thred_rd[js] = ev_c

        def stA_tail(pl, cin, nxt_lo, nxt_hi):
            li = cin - nxt_lo
            n = nxt_hi - nxt_lo
            h1 = min(8, n)
            if li == h1 - 1:
                smalls_batch(pl, 0, h1)
                for cj in range(nxt_lo, nxt_lo + h1):
                    ang_chunk(pl, cj)
                trig_half(pl, 0, h1)
            if n > h1 and li == n - 1:
                smalls_batch(pl, 8, n)
                for cj in range(nxt_lo + 8, nxt_hi):
                    ang_chunk(pl, cj)
                trig_half(pl, 8, n)

        def stB_proj(pl, ci):
            nonlocal ev_proj_last
            par = ci % 2
            g2 = (ci // 2) % 2
            if ci % 2 == 0:
                c0 = (ci * TC) // TPC
                tcol = (ci * TC) % TPC
                ev_lhsB_ld[ci // 2] = pl.op("sy", lambda e, c0=c0, tcol=tcol, g2=g2: e.dma_start(
                    out=lhsB[:, g2 * 4096:(g2 + 1) * 4096].rearrange("p (kc t) -> p kc t", kc=16),
                    in_=xln_g[tcol // 512, c0 * D:(c0 + 1) * D,
                              tcol % 512:tcol % 512 + 256].rearrange(
                        "(kc q) t -> q kc t", q=128)),
                    waits=[ev_ag1h[tcol // 512], ev_lhsB_free[g2]],
                    inc="SlB0" if g2 == 0 else "SlB1", amt=16)
            ev_ld = ev_lhsB_ld[ci // 2]
            sub = ci % 2
            for bk2, coff in ((2, CX), (0, CB), (1, CC)):
                evx = None
                for kc in range(16):
                    evx = pe_bank(pl, bk2, lambda e, kc=kc, g2=g2, sub=sub, bk2=bk2, coff=coff: e.matmul(
                        pb[bk2][:, 0:512],
                        lhsB[:, g2 * 4096 + kc * 256 + sub * 128: g2 * 4096 + kc * 256 + (sub + 1) * 128],
                        aW[:, kc * PASSW + coff: kc * PASSW + coff + 512],
                        start=(kc == 0), stop=(kc == 15)),
                        extra=[ev_ld, ev_aw], start_new=(kc == 0))
            ev_lhsB_free[g2] = evx
            ev_proj_last = evx
            return par

        def stB_evac_xz(pl, ci):
            par = ci % 2
            evxh = rd_bank(pl, "sc", 2, lambda e, par=par: e.activation(
                xhb[:, par * 256:(par + 1) * 256], pb[2][:, 0:256], AF.Copy),
                extra=ev_xh_rd[par])
            evz = rd_bank(pl, "sc", 2, lambda e, par=par: e.activation(
                zb[:, par * 256:(par + 1) * 256], pb[2][:, 256:512], AF.Copy),
                extra=[ev_z_rd[par]])
            ev_xh_rd[par] = []
            return dict(evxh=evxh, evz=evz)

        def stB_evac_mix(pl, ci):
            par = ci % 2
            evB = rd_bank(pl, "sc", 0, lambda e, par=par: e.activation(
                mixf[:, par * 1024:par * 1024 + 512], pb[0][:, 0:512], AF.Copy),
                extra=ev_mixf_rd[par])
            evC = rd_bank(pl, "sc", 1, lambda e, par=par: e.activation(
                mixf[:, par * 1024 + 512:par * 1024 + 1024], pb[1][:, 0:512], AF.Copy))
            ev_mixf_rd[par] = []
            return dict(evB=evB, evC=evC)

        def stB_rest(pl, ci, dep):
            nonlocal ev_yfm_last
            par = ci % 2
            j = ci % GRP
            evB, evC, evxh, evz = dep["evB"], dep["evC"], dep["evxh"], dep["evz"]
            mo = par * 1024
            ev_sq = None
            for jj in range(8):
                ev_sq = pl.op("sc", lambda e, jj=jj, mo=mo: e.activation(
                    scrB[:, jj * 128:(jj + 1) * 128], mixf[:, mo + jj * 128: mo + (jj + 1) * 128],
                    AF.Square, accum_out=msq[:, jj:jj + 1]),
                    waits=[evB, evC, ev_scrB_rd[0]], inc="Ssc")
            ev_mixf_rd[par].append(ev_sq)
            ev = pl.op("dv", lambda e: e.tensor_scalar(msq[:, 8:16], msq[:, 0:8], 1.0 / N, EPS_RMS, ALU.mult, ALU.add), waits=[ev_sq], inc="Sdv")
            ev = pl.op("sc", lambda e: (e.drain(), e.activation(msq[:, 16:24], msq[:, 8:16], AF.Ln))[1], waits=[ev], inc="Ssc")
            ev = pl.op("sc", lambda e: (e.drain(), e.activation(msq[:, 24:32], msq[:, 16:24], AF.Exp, scale=-0.5))[1], waits=[ev], inc="Ssc")
            ev = pl.op("dv", lambda e, mo=mo: e.tensor_tensor(
                scrB[:].rearrange("p (jj i) -> p jj i", jj=8),
                mixf[:, mo:mo + 1024].rearrange("p (jj i) -> p jj i", jj=8),
                msq[:, 24:32].unsqueeze(2).broadcast_to([128, 8, 128]), ALU.mult), waits=[ev], inc="Sdv")
            ev_mixf_rd[par].append(ev)
            ev = pl.op("dv", lambda e: (e.drain(), e.tensor_tensor(scrB[:], scrB[:], C["BCnw"][:], ALU.mult))[1], waits=[ev], inc="Sdv")
            ev_pre = pl.op("dv", lambda e, p=p: (e.drain(), e.tensor_tensor(
                scrB[:], scrB[:], C["BCbias"][:, p * 1024:(p + 1) * 1024], ALU.add))[1], waits=[ev], inc="Sdv")
            evs_cs = ev_cs[j]
            x5 = lambda eo: scrB[:].rearrange("p (bc h eo i) -> p bc h eo i", bc=2, h=HPP, eo=2)[:, :, :, eo, :]
            cbc = lambda off: csb[:, j * 512 + off: j * 512 + off + 256].rearrange(
                "p (h i) -> p h i", h=HPP).unsqueeze(1).broadcast_to([128, 2, HPP, 64])
            m5 = lambda eo: mixb[:, par * 1024:(par + 1) * 1024].rearrange(
                "p (bc h eo i) -> p bc h eo i", bc=2, h=HPP, eo=2)[:, :, :, eo, :]
            r4v = lambda t: rt[t][:].rearrange("p (bc h i) -> p bc h i", bc=2, h=HPP)
            ev = pl.op("dv", lambda e: e.tensor_tensor(r4v(0), x5(0), cbc(256), ALU.mult), waits=[ev_pre, evs_cs[1]], inc="Sdv")
            ev = pl.op("dv", lambda e: e.tensor_tensor(r4v(1), x5(0), cbc(0), ALU.mult), waits=[ev, evs_cs[0]], inc="Sdv")
            ev = pl.op("dv", lambda e: e.tensor_tensor(r4v(2), x5(1), cbc(0), ALU.mult), waits=[ev], inc="Sdv")
            ev = pl.op("dv", lambda e: e.tensor_tensor(r4v(3), x5(1), cbc(256), ALU.mult), waits=[ev], inc="Sdv")
            ev = pl.op("dv", lambda e: (e.drain(), e.tensor_tensor(
                m5(0), r4v(0), r4v(2), ALU.subtract))[1],
                waits=[ev] + ev_mixb_rd[par], inc="Sdv")
            ev_mix = pl.op("dv", lambda e: (e.drain(), e.tensor_tensor(
                m5(1), r4v(1), r4v(3), ALU.add))[1], waits=[ev], inc="Sdv")
            ev_mixb_rd[par] = []
            ev_scrB_rd[0] = ev_mix
            ev_csb_rd[j] = ev_mix
            SDc = lambda off: smalls[:, scol(ci, off):scol(ci, off) + 4]
            sm = ev_smalls[j]
            ev_r4 = pl.op("dv", lambda e: e.tensor_tensor(
                rb4[:].rearrange("p (h t) -> p h t", h=HPP),
                SDc(S_ADT).unsqueeze(2).broadcast_to([128, HPP, 128]),
                C["L1incl"][:].unsqueeze(1).broadcast_to([128, HPP, 128]), ALU.mult),
                waits=[sm["ug"], ev_r4_rd[0]], inc="Sdv")
            evj = pe_bank(pl, 7, lambda e: e.matmul(pb7[0:1, 0:512], C["ones_col"][:], rb4[:], start=True, stop=True), extra=[ev_r4])
            ev_r4_rd[0] = evj
            ev_aJ = rd_bank(pl, "dv", 7, lambda e, par=par: e.tensor_copy(
                acdtJs[:, par * 512:(par + 1) * 512], pb7[0:1, 0:512]),
                extra=[ev_aJ_rd.get(par)])
            ev_bnd_reads = []
            ev_pb2bnd_rd_prev = ev_pb2bnd_rd[0]
            evB2s, evC2s, ev3s = [], [], []
            for h in range(HPP):
                sB, evtB, bB = do_transpose(pl, (lambda h=h, par=par: mixb[:, par * 1024 + h * 128: par * 1024 + (h + 1) * 128]), [ev_mix])
                ev_mixb_rd[par].append(evtB)
                evB2 = evac_T(pl, sB, bB, evtB, (lambda h=h, par=par: Bfm[:, (par * HPP + h) * 128:(par * HPP + h + 1) * 128]), extra=ev_Bfm_rd[par] if h == 0 else [])
                if h == 0:
                    ev_Bfm_rd[par] = []
                evB2s.append(evB2)
                sC, evtC, bC = do_transpose(pl, (lambda h=h, par=par: mixb[:, par * 1024 + 512 + h * 128: par * 1024 + 512 + (h + 1) * 128]), [ev_mix])
                ev_mixb_rd[par].append(evtC)
                evC2 = evac_T_sc(pl, sC, bC, evtC, (lambda h=h: Cfm[:, h * 128:(h + 1) * 128]), extra=ev_Cfm_rd[h])
                ev_Cfm_rd[h] = []
                evC2s.append(evC2)
                ev3 = pe_bank(pl, 2, lambda e, h=h, par=par: e.matmul(
                    pb[2][0:1, h * 128:(h + 1) * 128],
                    Bfm[:, ((1 - par) * HPP + h) * 128 + 127: ((1 - par) * HPP + h) * 128 + 128],
                    Cfm[:, h * 128:(h + 1) * 128],
                    start=True, stop=True), extra=[evC2, ev_pb2bnd_rd_prev])
                ev_Bfm_rd[1 - par].append(ev3)
                ev_Cfm_rd[h].append(ev3)
                ev3s.append(ev3)
            ev1 = pe_bank(pl, 4, lambda e, par=par: e.matmul(
                pb[4][:, 0:512], C["ones_row"][:],
                acdtJs[0:1, par * 512:(par + 1) * 512],
                start=True, stop=True), extra=[ev_aJ])
            ev2l = None
            for h in range(HPP):
                ev2l = pe_bank(pl, 3, lambda e, h=h, par=par: e.matmul(
                    pb[3][:, h * 128:(h + 1) * 128],
                    Bfm[:, (par * HPP + h) * 128:(par * HPP + h + 1) * 128],
                    Cfm[:, h * 128:(h + 1) * 128],
                    start=True, stop=True), extra=[evB2s[h], evC2s[h]],
                    start_new=(h == 0))
                ev_Bfm_rd[par].append(ev2l)
                ev_Cfm_rd[h].append(ev2l)
            h4v = lambda t: t.rearrange("p (h i) -> p h i", h=HPP)
            col4 = lambda off: SDc(off).unsqueeze(2).broadcast_to([128, HPP, 128])
            mb4 = lambda nm: C[nm][:].unsqueeze(1).broadcast_to([128, HPP, 128])
            evd1 = rd_bank(pl, "dv", 4, lambda e: e.tensor_tensor(
                h4v(ex1[:]), h4v(pb[4][:, 0:512]), col4(S_UG), ALU.add),
                extra=[sm["ug"], ev1, ev_ex_rd[0]])
            evd1 = pl.op("dv", lambda e: (e.drain(), e.tensor_tensor(
                h4v(ex1[:]), h4v(ex1[:]), mb4("MBincl"), ALU.add))[1], waits=[evd1], inc="Sdv")
            eva1 = pl.op("sc", lambda e: e.activation(ex1[:], ex1[:], AF.Exp), waits=[evd1], inc="Ssc")
            evd2 = rd_bank(pl, "dv", 4, lambda e: e.tensor_tensor(
                h4v(ex2[:]), h4v(pb[4][:, 0:512]), col4(S_UBS), ALU.add),
                extra=[sm["ubs"], ev_ex_rd[0]])
            evd2 = pl.op("dv", lambda e: (e.drain(), e.tensor_tensor(
                h4v(ex2[:]), h4v(ex2[:]), mb4("MBstrict"), ALU.add))[1], waits=[evd2], inc="Sdv")
            eva2 = pl.op("sc", lambda e: (e.drain(), e.activation(ex2[:], ex2[:], AF.Exp))[1], waits=[evd2], inc="Ssc")
            evw = pl.op("dv", lambda e: e.tensor_tensor(ex1[:], ex1[:], ex2[:], ALU.add), waits=[eva1, eva2], inc="Sdv")
            evmc = rd_bank(pl, "dv", 3, lambda e: e.tensor_tensor(
                mcomb[:], ex1[:], pb[3][:, 0:512], ALU.mult),
                extra=[evw, ev2l, ev_mcomb_rd[0]])
            ev_ex_rd[0] = evmc
            evbx_dv = pl.op("dv", lambda e, par=par: e.tensor_tensor(
                bexp[:].rearrange("o (h t) -> o h t", h=HPP),
                acdtJs[0:1, par * 512:(par + 1) * 512].rearrange("o (h t) -> o h t", h=HPP),
                SDc(S_UB)[0:1, :].unsqueeze(2).broadcast_to([1, HPP, 128]), ALU.add),
                waits=[ev_aJ, sm["ub"], ev_bexp_rd[0]], inc="Sdv")
            evbx = pl.op("sc", lambda e: e.activation(bexp[:], bexp[:], AF.Exp), waits=[evbx_dv], inc="Ssc")
            evbw = rd_bank(pl, "dv", 2, lambda e: e.tensor_tensor(
                bndw[:], bexp[:], pb[2][0:1, 0:512], ALU.mult),
                extra=[evbx] + ev3s)
            ev_pb2bnd_rd[0] = evbw
            evlw = pl.op("dv", lambda e, par=par: e.tensor_tensor(
                h4v(lhsTw[:]), mixb[:, par * 1024:par * 1024 + 512].rearrange("p (h i) -> p h i", h=HPP),
                col4(S_W2), ALU.mult),
                waits=[ev_mix, sm["w2"], ev_lhsTw_rd[0]], inc="Sdv")
            ev_mixb_rd[par].append(evlw)
            evw2b = pl.op("dv", lambda e: e.tensor_tensor(
                w2b1[:], SDc(S_UB)[0:1, :], SDc(S_AL)[0:1, :], ALU.add),
                waits=[sm["ub"], sm["al"]], inc="Sdv")
            evw2b = pl.op("sc", lambda e: (e.drain(), e.activation(w2b1[:], w2b1[:], AF.Exp))[1], waits=[evw2b], inc="Ssc")
            evhb = pl.op("dv", lambda e, par=par: e.tensor_tensor(
                hbrow[:].rearrange("o (h t) -> o h t", h=HPP),
                bndst[0:1, par * 768:par * 768 + 512].rearrange("o (h t) -> o h t", h=HPP),
                w2b1[0:1, :].unsqueeze(2).broadcast_to([1, HPP, 128]), ALU.mult),
                waits=[evw2b, ev_bnd_w[par], ev_hbrow_rd[0]], inc="Sdv")
            ev_bnd_reads.append(evhb)
            evy = None
            evh = None
            for h in range(HPP):
                evy = pe_bank(pl, 3, lambda e, h=h, par=par: e.matmul(
                    pb[3][:, h * 64:(h + 1) * 64], mcomb[:, h * 128:(h + 1) * 128],
                    xhb[:, par * 256 + h * 64: par * 256 + (h + 1) * 64],
                    start=True, stop=False), extra=[evmc, evxh],
                    start_new=(h == 0))
                evy = pe_bank(pl, 3, lambda e, h=h, par=par: e.matmul(
                    pb[3][:, h * 64:(h + 1) * 64], bndw[0:1, h * 128:(h + 1) * 128],
                    bndst[0:1, par * 768 + 512 + h * 64:par * 768 + 512 + (h + 1) * 64],
                    start=False, stop=True), extra=[evbw, ev_bnd_w[par]], start_new=False)
                ev_bnd_reads.append(evy)
                evh = pe_bank(pl, 3, lambda e, h=h, par=par: e.matmul(
                    pb[3][:, 256 + h * 64: 256 + (h + 1) * 64], lhsTw[:, h * 128:(h + 1) * 128],
                    xhb[:, par * 256 + h * 64: par * 256 + (h + 1) * 64],
                    start=True, stop=False), extra=[evlw], start_new=False)
                evh = pe_bank(pl, 3, lambda e, h=h, par=par: e.matmul(
                    pb[3][:, 256 + h * 64: 256 + (h + 1) * 64], hbrow[0:1, h * 128:(h + 1) * 128],
                    bndst[0:1, par * 768 + 512 + h * 64:par * 768 + 512 + (h + 1) * 64],
                    start=False, stop=True), extra=[evhb], start_new=False)
                ev_bnd_reads.append(evh)
            ev_mcomb_rd[0] = evy
            ev_lhsTw_rd[0] = evh
            ev_xh_rd[par].extend([evy, evh])
            ev_bexp_rd[0] = evy
            ev_hbrow_rd[0] = evh
            evy2 = None
            for h in range(HPP):
                evy2 = pe_bank(pl, 4, lambda e, h=h, par=par: e.matmul(
                    pb[4][:, h * 64:(h + 1) * 64], Cfm[:, h * 128:(h + 1) * 128],
                    hst[:, par * 256 + h * P:par * 256 + (h + 1) * P],
                    start=True, stop=True), extra=[evC2s[h], ev_h_w.get(par)],
                    start_new=(h == 0))
                ev_Cfm_rd[h].append(evy2)
            ev_h_rd.setdefault(par, []).append(evy2)
            ysc4 = scrB[:, 0:256]
            hupd = scrB[:, 256:512]
            evye = rd_bank(pl, "dv", 4, lambda e: e.tensor_tensor(
                ysc4.rearrange("p (h i) -> p h i", h=HPP),
                pb[4][:, 0:256].rearrange("p (h i) -> p h i", h=HPP),
                SDc(S_EA).unsqueeze(2).broadcast_to([128, HPP, 64]), ALU.mult),
                extra=[evy2, sm["ea"]])
            evye2 = rd_bank(pl, "dv", 3, lambda e: e.tensor_tensor(
                y_tm[:], ysc4, pb[3][:, 0:256], ALU.add),
                extra=[evye, evy])
            ev_y_evac = [evye2]
            evhu = pl.op("dv", lambda e, par=par: e.tensor_tensor(
                hupd.rearrange("p (h i) -> p h i", h=HPP),
                hst[:, par * 256:(par + 1) * 256].rearrange("p (h i) -> p h i", h=HPP),
                SDc(S_SE).unsqueeze(2).broadcast_to([128, HPP, P]), ALU.mult),
                waits=[sm["se"], ev_h_w.get(par), evye2], inc="Sdv")
            evhe = rd_bank(pl, "dv", 3, lambda e, par=par: e.tensor_tensor(
                hst[:, (1 - par) * 256:(2 - par) * 256], hupd, pb[3][:, 256:512], ALU.add),
                extra=[evhu, evh, ev_h_w.get(1 - par)] + ev_h_rd.get(1 - par, []))
            ev_h_w[1 - par] = evhe
            ev_h_rd[1 - par] = []
            ev_h_rd.setdefault(par, []).append(evhu)
            ev_scrB_rd[0] = evhe
            ev_aJ_rd[par] = ev1
            evsc_g = pl.op("sc", lambda e, par=par: e.activation(gw0[:], zb[:, par * 256:(par + 1) * 256], AF.Exp, scale=-1.0), waits=[evz, ev_gw_rd[0]], inc="Ssc")
            ev = pl.op("dv", lambda e: e.tensor_scalar(gw0[:], gw0[:], 1.0, None, ALU.add), waits=[evsc_g], inc="Sdv")
            ev = pl.op("dv", lambda e: (e.drain(), e.reciprocal(gw1[:], gw0[:]))[1], waits=[ev], inc="Sdv")
            ev = pl.op("dv", lambda e, par=par: (e.drain(), e.tensor_tensor(gw0[:], zb[:, par * 256:(par + 1) * 256], gw1[:], ALU.mult))[1], waits=[ev], inc="Sdv")
            ev_z_rd[par] = ev
            evg = pl.op("dv", lambda e: (e.drain(), e.tensor_tensor(ygb[:], y_tm[:], gw0[:], ALU.mult))[1], waits=[ev] + ev_y_evac, inc="Sdv")
            ev_gw_rd[0] = evg
            ev_slot_free[j] = evg
            cl = j
            for ft in range(2):
                s, evt, bkt = do_transpose(pl, (lambda ft=ft: ygb[:, ft * 128:(ft + 1) * 128]), [evg])
                evv = evac_T_sc(pl, s, bkt, evt, (lambda ft=ft, cl=cl: yfmg[:, ft * GRP * 128 + cl * 128: ft * GRP * 128 + (cl + 1) * 128]), extra=[ev_yfmg_free])
                ev_yfm_last = evv
            evbe1 = pe_bank(pl, 2, lambda e, par=par: e.matmul(
                pb[2][0:1, 0:512], C["onehot127b"][:],
                mixb[:, par * 1024:par * 1024 + 512], start=True, stop=True),
                extra=[ev_mix, ev_pb2bnd_rd[0]])
            ev_mixb_rd[par].append(evbe1)
            evbe2 = pe_bank(pl, 7, lambda e, par=par: e.matmul(
                pb7[0:1, 0:256], C["onehot127b"][:],
                xhb[:, par * 256:(par + 1) * 256], start=True, stop=True),
                extra=[evxh])
            ev_xh_rd[par].append(evbe2)
            evbc = rd_bank(pl, "dv", 2, lambda e, par=par: e.tensor_copy(
                bndst[0:1, (1 - par) * 768:(1 - par) * 768 + 512], pb[2][0:1, 0:512]),
                extra=ev_bnd_rd[1 - par] + [ev_bnd_w[1 - par]])
            evbc = rd_bank(pl, "dv", 7, lambda e, par=par: e.tensor_copy(
                bndst[0:1, (1 - par) * 768 + 512:(1 - par) * 768 + 768], pb7[0:1, 0:256]),
                extra=[evbc])
            ev_bnd_w[1 - par] = evbc
            ev_bnd_rd[1 - par] = []
            ev_bnd_rd[par] = ev_bnd_reads

        plw = Plan(cnt)
        gate_all(plw, PH1 if gate is None else gate)
        evw = plw.op("gp", lambda e, p=p: e.dma_start(
            out=aW[:].rearrange("p (kc c) -> p kc c", kc=16),
            in_=din["attnW"][p * D:(p + 1) * D, :].rearrange("(kc q) c -> q kc c", q=128)),
            waits=[PH0[0], ev_aw_free], inc="Sgw", amt=16)
        ev_aw = evw
        for t in (hst, Bfm, bndst, smalls, sw):
            plw.op("gp", lambda e, t=t: e.memset(t[:], 0.0), inc="Sgp")
        plw.op("dv", lambda e: e.memset(offrow[:], 0.0), waits=[ev_off_w] if ev_off_w else [], inc="Sdv")
        ev_ms = plw.op("gp", lambda e: e.nop(), inc="Sgp")
        replay(nc, plw, sems)
        for par0 in range(2):
            ev_bnd_w[par0] = ev_ms
            ev_h_w[par0] = ev_ms
        ev_Bfm_rd[0] = [ev_ms]
        ev_Bfm_rd[1] = [ev_ms]

        ngr = (nch + GRP - 1) // GRP
        pl = Plan(cnt)
        n0 = min(GRP, nch)
        for ci in range(0, n0):
            stA(pl, ci)
            stA_tail(pl, ci, 0, n0)
        replay(nc, pl, sems)
        for g in range(ngr):
            pl = Plan(cnt)
            lo, hi = g * GRP, min((g + 1) * GRP, nch)
            nxt_lo, nxt_hi = (g + 1) * GRP, min((g + 2) * GRP, nch)
            stB_proj(pl, lo)
            pend = dict(stB_evac_xz(pl, lo), **stB_evac_mix(pl, lo))
            for ci in range(lo, hi):
                nxt = {}
                if ci + 1 < hi:
                    stB_proj(pl, ci + 1)
                    nxt.update(stB_evac_xz(pl, ci + 1))
                stB_rest(pl, ci, pend)
                if ci + 1 < hi:
                    nxt.update(stB_evac_mix(pl, ci + 1))
                pend = nxt
                cin = nxt_lo + (ci - lo)
                if cin < nxt_hi:
                    stA(pl, cin)
                    stA_tail(pl, cin, nxt_lo, nxt_hi)
            ev_fl = None
            for ft in range(2):
                ev_fl = pl.op("sy", lambda e, ft=ft, g=g, p=p: e.dma_start(
                    out=y_c[g, (p * 256 + ft * 128):(p * 256 + (ft + 1) * 128),
                            0:(hi - lo) * 128],
                    in_=yfmg[:, ft * GRP * 128:ft * GRP * 128 + (hi - lo) * 128]),
                    waits=[ev_yfm_last], inc="Syc", amt=16)
            ev_yfmg_free = ev_fl
            flush_evs.append(ev_fl)
            if p0_flush is not None and g < len(p0_flush):
                ag_evs.append(pl.op("gp", lambda e, g=g: e.collective_compute(
                    "AllGather", ALU.bypass, rg,
                    ins=[y_c.ap()[g].opt()], outs=[y_g.ap()[g].opt()]),
                    waits=[ev_fl, p0_flush[g]], inc="Scc"))
            replay(nc, pl, sems)
        return ev_yfmg_free, ev_proj_last, flush_evs, ag_evs

    ev_f0, ev_pj0, fl0, _ = emit_pass(0, nchunks, None)
    if two_pass:
        PB1 = [(s, cnt[s]) for s in SEMS if cnt[s] > 0]
        ev_f1, _, fl1, ags = emit_pass(1, nchunks, ev_pj0, ev_yfmg_free0=ev_f0,
                                       gate=PB1, p0_flush=fl0)
    else:
        ev_f1 = ev_f0
        ags = []

    pl = Plan(cnt)
    if ags:
        ev_ag2 = ags[-1]
    else:
        for g4 in range(4):
            ev_ag2 = pl.op("gp", lambda e, g4=g4: e.collective_compute(
                "AllGather", ALU.bypass, rg,
                ins=[y_c.ap()[g4].opt()], outs=[y_g.ap()[g4].opt()]),
                waits=[ev_f0, ev_f1], inc="Scc")
    if "mix" in debug:
        ng = min(GRP, nchunks)
        for nm, t, pp, w in (("mixb", mixb, 128, 2048), ("xhb", xhb, 128, 512),
                             ("zbv", zb, 128, 512), ("csb", csb, 128, ng * 512),
                             ("smalls", smalls, 128, ng * 48),
                             ("thred", thred, 128, ng * 256),
                             ("ytm", y_tm, 128, HPP * P), ("tsb", thx, 128, 256),
                             ("offrow", offrow, 1, 256), ("Bfm", Bfm, 128, 1024)):
            dt_ = t[:].dtype
            dbg_outs[nm] = nc.dram_tensor("dbg_" + nm, [pp, w], dt_,
                                          kind="ExternalOutput").ap()
            pl.op("sy", lambda e, nm=nm, t=t, pp=pp, w=w: e.dma_start(
                out=dbg_outs[nm], in_=t[0:pp, 0:w]), waits=pl.barrier(), inc="Ssy", amt=16)
        pl.op("sy", lambda e: None, waits=pl.barrier())
    PH2 = pl.barrier()
    replay(nc, pl, sems)
    ph2.close()

    if "y_g" in debug:
        pl = Plan(cnt)
        ph = contextlib.ExitStack()
        t = ph.enter_context(nc.sbuf_tensor("dbgy", [128, 8192], BF16))
        dbg_outs["y_g"] = nc.dram_tensor("dbg_y_g", [128, 8192], BF16,
                                         kind="ExternalOutput").ap()
        ev = pl.op("sy", lambda e: e.dma_start(out=t[:, 0:2048], in_=y_g[0, 0:128, :]),
                   waits=[ev_ag2], inc="Ssy", amt=16)
        ev = pl.op("sy", lambda e: e.dma_start(out=t[:, 4096:6144], in_=y_g[0, 256:384, :]),
                   waits=[ev_ag2, ev], inc="Ssy", amt=16)
        ev = pl.op("sy", lambda e: e.dma_start(out=dbg_outs["y_g"], in_=t[:]),
                   waits=[ev], inc="Ssy", amt=16)
        pl.op("sy", lambda e: None, waits=pl.barrier())
        replay(nc, pl, sems)
        ph.close()

    if upto <= 2:
        return finish_dummy()

    ph34 = contextlib.ExitStack()
    x2fm = ph34.enter_context(nc.sbuf_tensor("x2fm", [128, 16 * TPC], BF16))
    gslots = ph34.enter_context(nc.sbuf_tensor("gslots", [128, NTT * 4], I32))
    ph3 = contextlib.ExitStack()
    EP = ph3.enter_context
    ygsb = EP(nc.sbuf_tensor("ygsb", [128, 32 * TPC], BF16))
    wosb = EP(nc.sbuf_tensor("wosb", [128, 32 * 512], BF16))
    xts = [EP(nc.sbuf_tensor(f"p3x{i}", [128, D], F32)) for i in range(2)]
    x2t = EP(nc.sbuf_tensor("x2t", [128, D], F32))
    scr3 = EP(nc.sbuf_tensor("scr3", [128, D], F32))
    scr3b = EP(nc.sbuf_tensor("scr3b", [128, D], F32))
    ln3 = EP(nc.sbuf_tensor("ln3", [128, 8], F32))
    x2lnb = EP(nc.sbuf_tensor("x2lnb", [128, D], BF16))
    gwb = EP(nc.sbuf_tensor("gwb", [128, 16 * 16], BF16))
    glog = EP(nc.sbuf_tensor("glog", [128, 16], F32))
    mx4 = EP(nc.sbuf_tensor("mx4", [128, 8], F32))
    idx4 = EP(nc.sbuf_tensor("idx4", [128, 4], F32))
    w4 = EP(nc.sbuf_tensor("w4", [128, 8], F32))
    rnk4 = EP(nc.sbuf_tensor("rnk4", [128, 4], F32))
    tt16 = EP(nc.sbuf_tensor("tt16", [128, 16], F32))
    tt16b = EP(nc.sbuf_tensor("tt16b", [128, 16], F32))
    macc = EP(nc.sbuf_tensor("macc", [128, 16], F32))
    mu8 = EP(nc.sbuf_tensor("mu8", [128, 16], mybir.dt.uint8))
    ku8 = EP(nc.sbuf_tensor("ku8", [128, 4], mybir.dt.uint8))
    ohall = EP(nc.sbuf_tensor("ohall", [128, NTT * 64], F32))
    ohSall = EP(nc.sbuf_tensor("ohSall", [128, NTT * 16], F32))
    gslf = EP(nc.sbuf_tensor("gslf", [128, 4], F32))
    cntsb = EP(nc.sbuf_tensor("cntsb", [NCORE, E], F32))
    base_r = EP(nc.sbuf_tensor("base_r", [1, E], F32))
    offs_r = EP(nc.sbuf_tensor("offs_r", [1, E], F32))
    snt = EP(nc.sbuf_tensor("snt", [128, 256], F32))
    clsrc = EP(nc.sbuf_tensor("clsrc", [128, 8], F32))

    evT3b = [None, None]

    pl = Plan(cnt)
    gate_all(pl, PH2)
    ev_yg = None
    yg_flat = y_g.ap().flatten().unsqueeze(1)
    for kc in range(32):
        ev_yg = pl.op("gp", lambda e, kc=kc: e.indirect_dma_start(
            ygsb[:, kc * TPC:(kc + 1) * TPC], None,
            yg_flat, IndirectOffsetOnAxis(C["ygidx"][:, kc:kc + 1], 0),
            oob_is_err=True),
            waits=[ev_ag2], inc="Sg0", amt=16)
    ev_wo_used = [None]
    ev_x_used = [None, None]
    ev_xij = {}
    ev_x2dw = None
    for jg in range(4):
        sl = 0
        evw_h = []
        for hf, eng in ((0, "gp"), (1, "gp")):
            evw_h.append(pl.op(eng, lambda e, jg=jg, hf=hf: e.dma_start(
                out=wosb[:, hf * 8192:(hf + 1) * 8192].rearrange("p (kc c) -> p kc c", kc=16),
                in_=din["WoutT"][jg * 4096 + hf * 2048: jg * 4096 + (hf + 1) * 2048, :]
                .rearrange("(kc q) c -> q kc c", q=128)),
                waits=[ev_wo_used[0]], inc="Swo0" if jg % 2 == 0 else "Swo1", amt=16))
        ev_wold = evw_h
        evm_last = None
        for i in range(NTT):
            xsl = (jg * NTT + i) % 2
            ev_xij[(jg, i)] = pl.op("sy", lambda e, i=i, jg=jg, xsl=xsl: e.dma_start(
                out=xts[xsl][:, 0:512],
                in_=din["x_slice"][i * 128:(i + 1) * 128, jg * 512:(jg + 1) * 512]),
                waits=[ev_x_used[xsl]], inc="Sx0" if xsl == 0 else "Sx1", amt=16)
            bnk = (jg * NTT + i) % 3
            evm = None
            for kc in range(32):
                evm = pe_bank(pl, bnk, lambda e, i=i, jg=jg, kc=kc, bnk=bnk: e.matmul(
                    pb[bnk][:, 0:512],
                    ygsb[:, kc * TPC + i * 128: kc * TPC + (i + 1) * 128],
                    wosb[:, kc * 512:(kc + 1) * 512],
                    start=(kc == 0), stop=(kc == 31)),
                    extra=[ev_yg] + ev_wold, start_new=(kc == 0))
            evm_last = evm
            ev = rd_bank(pl, "dv", bnk, lambda e, i=i, jg=jg, xsl=xsl, bnk=bnk: e.tensor_tensor(
                x2t[:, 0:512], pb[bnk][:, 0:512], xts[xsl][:, 0:512], ALU.add),
                extra=[ev_xij[(jg, i)], ev_x2dw])
            ev_x_used[xsl] = ev
            ev_x2dw = pl.op("sy", lambda e, i=i, jg=jg: e.dma_start(
                out=x2_d[i * 128:(i + 1) * 128, jg * 512:(jg + 1) * 512],
                in_=x2t[:, 0:512]), waits=[ev], inc="Sx2", amt=16)
        ev_wo_used[0] = evm_last
    ev_x2r = [ev_x_used[0], ev_x_used[1]]
    ev_x2fm = {}
    ev_x2c = None
    ev_bf_prev = None
    ev_x2fm_last = None
    for i in range(NTT):
        xsl = i % 2
        ev_ld = pl.op("sy", lambda e, i=i, xsl=xsl: e.dma_start(
            out=xts[xsl][:], in_=x2_d[i * 128:(i + 1) * 128, :]),
            waits=[ev_x2r[xsl], (("Sx2", cnt["Sx2"]) if i == 0 else None)],
            inc="Sxw0" if xsl == 0 else "Sxw1", amt=16)
        ev_bf = emit_ln_tile(pl, (lambda xsl=xsl: xts[xsl][:]), x2lnb, i, ev_ld,
                             scr3, scr3b, ln3,
                             out_free_ev=[ev_x2fm.get(i - 1), ev_x2c])
        ev_x2r[xsl] = ev_bf
        ev_x2c = pl.op("sy", lambda e, i=i: e.dma_start(
            out=x2ln_c[i * 128:(i + 1) * 128, :], in_=x2lnb[:]),
            waits=[ev_bf, ev_x2c], inc="Sxc", amt=16)
        last_t = None
        for kc in range(16):
            bank = kc % 2
            slot = bank * 8 + (kc // 2) % 8
            evt = pl.op("pe", lambda e, kc=kc, slot=slot: e.transpose(
                pbT[:, slot * 128:(slot + 1) * 128],
                x2lnb[:, kc * 128:(kc + 1) * 128], idb[:]),
                waits=[ev_bf, evT3b[bank]], inc="Spe")
            evv = pl.op("dv", lambda e, kc=kc, i=i, slot=slot: e.tensor_copy(
                x2fm[:, kc * TPC + i * 128: kc * TPC + (i + 1) * 128],
                pbT[:, slot * 128:(slot + 1) * 128]),
                waits=[evt], inc="Sdv")
            evT3b[bank] = evv
            last_t = evt
            ev_x2fm_last = evv
        ev_x2fm[i] = last_t
    for i in range(NTT):
        ev_x2fm[i] = ev_x2fm_last
    ev_ag3 = pl.op("gp", lambda e: e.collective_compute(
        "AllGather", ALU.bypass, rg, ins=[x2ln_c.ap().opt()], outs=[x2ln_g.ap().opt()]),
        waits=[ev_x2c], inc="Scc")
    replay(nc, pl, sems)

    def sntbig4():
        return snt[:, 0:4]

    pl = Plan(cnt)
    ev_gsl = {}
    ev_ogl = {}
    ev_gw = pl.op("gp", lambda e, bnk=bnk: e.dma_start(
        out=gwb[:].rearrange("p (kc e) -> p kc e", kc=16),
        in_=din["gate_wT"].rearrange("(kc p) e -> p kc e", p=128)),
        inc="Sgw", amt=16)
    ev_cnt_mm = {}
    for i in range(NTT):
        bnk = 3 + i % 2
        evm = None
        for kc in range(16):
            evm = pe_bank(pl, bnk, lambda e, i=i, kc=kc, bnk=bnk: e.matmul(
                pb[bnk][:, 0:16],
                x2fm[:, kc * TPC + i * 128: kc * TPC + (i + 1) * 128],
                gwb[:, kc * 16:(kc + 1) * 16],
                start=(kc == 0), stop=(kc == 15)),
                extra=[ev_gw, ev_x2fm[i]], start_new=(kc == 0))
        ev = rd_bank(pl, "dv", bnk, lambda e, bnk=bnk: e.tensor_copy(glog[:], pb[bnk][:, 0:16]),
                     extra=[ev_gsl.get(i - 2)])
        for k in range(4):
            ev = pl.op("dv", lambda e, k=k: (e.drain(), e.tensor_reduce(
                mx4[:, k:k + 1], glog[:], AXX, ALU.max))[1], waits=[ev], inc="Sdv")
            ev = pl.op("dv", lambda e, k=k: (e.drain(), e.tensor_scalar(
                tt16[:], glog[:], mx4[:, k:k + 1], None, ALU.is_ge))[1], waits=[ev], inc="Sdv")
            ev = pl.op("dv", lambda e: (e.drain(), e.tensor_tensor(
                tt16b[:], tt16[:], C["RIOTA16"][:], ALU.mult))[1], waits=[ev], inc="Sdv")
            ev = pl.op("dv", lambda e, k=k: (e.drain(), e.tensor_reduce(
                idx4[:, k:k + 1], tt16b[:], AXX, ALU.max))[1], waits=[ev], inc="Sdv")
            ev = pl.op("dv", lambda e, k=k: (e.drain(), e.tensor_scalar(
                idx4[:, k:k + 1], idx4[:, k:k + 1], -1.0, 15.0, ALU.mult, ALU.add))[1],
                waits=[ev], inc="Sdv")
            if k < 3:
                ev = pl.op("dv", lambda e, k=k: (e.drain(), e.tensor_scalar(
                    mu8[:], glog[:], mx4[:, k:k + 1], None, ALU.is_ge))[1], waits=[ev], inc="Sdv")
                ev = pl.op("dv", lambda e: (e.drain(), e.copy_predicated(
                    glog[:], mu8[:], C["NEG16"][:]))[1], waits=[ev], inc="Sdv")
        ev = pl.op("dv", lambda e: (e.drain(), e.tensor_scalar(
            w4[:, 0:4], mx4[:, 0:4], mx4[:, 0:1], None, ALU.subtract))[1], waits=[ev], inc="Sdv")
        ev = pl.op("sc", lambda e: e.activation(w4[:, 4:8], w4[:, 0:4], AF.Exp), waits=[ev], inc="Ssc")
        ev = pl.op("dv", lambda e: e.tensor_reduce(w4[:, 0:1], w4[:, 4:8], AXX, ALU.add), waits=[ev], inc="Sdv")
        ev = pl.op("dv", lambda e: (e.drain(), e.reciprocal(w4[:, 1:2], w4[:, 0:1]))[1], waits=[ev], inc="Sdv")
        ev = pl.op("dv", lambda e: (e.drain(), e.tensor_scalar(
            w4[:, 4:8], w4[:, 4:8], w4[:, 1:2], None, ALU.mult))[1], waits=[ev], inc="Sdv")
        ev_w4 = ev
        ev = pl.op("dv", lambda e, i=i: (e.drain(), e.tensor_tensor(
            ohall[:, i * 64:(i + 1) * 64].rearrange("p (k e) -> p k e", k=4),
            idx4[:].unsqueeze(2).broadcast_to([128, 4, 16]),
            C["IOTA16"][:].unsqueeze(1).broadcast_to([128, 4, 16]), ALU.is_equal))[1],
            waits=[ev], inc="Sdv")
        ev = pl.op("dv", lambda e, i=i: (e.drain(), e.tensor_reduce(
            ohSall[:, i * 16:(i + 1) * 16].unsqueeze(2),
            ohall[:, i * 64:(i + 1) * 64].rearrange("p (k e) -> p e k", k=4),
            AXX, ALU.add))[1], waits=[ev], inc="Sdv")
        ev_ogl[i] = ev
        ev_gsl[i] = ev
    evc = None
    for i in range(NTT):
        evc = pe_bank(pl, 7, lambda e, i=i: e.matmul(
            pb7[0:1, 0:16], C["ones_col"][:], ohSall[:, i * 16:(i + 1) * 16],
            start=(i == 0), stop=(i == NTT - 1)), extra=[ev_ogl[i]], start_new=(i == 0))
    evc = rd_bank(pl, "dv", 7, lambda e: e.tensor_copy(base_r[:], pb7[0:1, 0:16]))
    evc = pl.op("sy", lambda e: e.dma_start(out=cnt_c.ap(), in_=base_r[:]),
                waits=[evc], inc="Scnt", amt=16)
    ev_ag4 = pl.op("gp", lambda e: e.collective_compute(
        "AllGather", ALU.bypass, rg, ins=[cnt_c.ap().opt()], outs=[cnt_g.ap().opt()]),
        waits=[evc], inc="Scc")
    pl.op("gp", lambda e: e.reg_mov(rbnd, E * CAP - 1))
    evsn0 = pl.op("dv", lambda e: e.memset(snt[:], SENT), inc="Sdv")
    evsn = pl.op("sy", lambda e: e.dma_start(
        out=clm_c.ap().rearrange("(a b) c -> a (b c)", a=128), in_=snt[:]),
        waits=[evsn0], inc="Ssy", amt=16)
    evb = pl.op("sy", lambda e: e.dma_start(out=cntsb[:], in_=cnt_g.ap()),
                waits=[ev_ag4], inc="Scn2", amt=16)
    evb = pe_bank(pl, 2, lambda e, bnk=bnk: e.matmul(pb[2][0:1, 0:16], C["coremask"][:],
                                            cntsb[:], start=True, stop=True),
                  extra=[evb])
    evb = rd_bank(pl, "dv", 2, lambda e, bnk=bnk: e.tensor_copy(offs_r[:], pb[2][0:1, 0:16]))
    ev_offs = evb
    ev_clsc = []
    for i in range(NTT):
        bnk = 3 + i % 2
        ev1 = pe_bank(pl, bnk, lambda e, i=i, bnk=bnk: e.matmul(
            pb[bnk][:, 32:48], C["L0strict"][:], ohSall[:, i * 16:(i + 1) * 16],
            start=True, stop=False), extra=[ev_ogl[i]])
        ev1 = pe_bank(pl, bnk, lambda e, bnk=bnk: e.matmul(
            pb[bnk][:, 32:48], C["ones_row"][:], offs_r[:],
            start=False, stop=True), extra=[ev_offs], start_new=False)
        evcm = rd_bank(pl, "dv", bnk, lambda e, bnk=bnk: e.tensor_copy(macc[:], pb[bnk][:, 32:48]))
        ev2 = pe_bank(pl, bnk, lambda e, i=i, bnk=bnk: e.matmul(
            pb[bnk][0:1, 48:64], C["ones_col"][:], ohSall[:, i * 16:(i + 1) * 16],
            start=True, stop=True), extra=[evcm], start_new=False)
        ev_offs = rd_bank(pl, "dv", bnk, lambda e, bnk=bnk: e.tensor_tensor(
            offs_r[:], offs_r[:], pb[bnk][0:1, 48:64], ALU.add), extra=[ev_offs, ev2])
        ev = evcm
        for k in range(4):
            ohk = lambda kk, ii: ohall[:, ii * 64 + kk * 16: ii * 64 + (kk + 1) * 16]
            ev = pl.op("dv", lambda e, k=k, i=i: (e.drain(), e.tensor_tensor(
                tt16[:], ohk(k, i), macc[:], ALU.mult))[1], waits=[ev], inc="Sdv")
            ev = pl.op("dv", lambda e, k=k: (e.drain(), e.tensor_reduce(
                rnk4[:, k:k + 1], tt16[:], AXX, ALU.add))[1], waits=[ev], inc="Sdv")
            if k < 3:
                ev = pl.op("dv", lambda e, k=k, i=i: (e.drain(), e.tensor_tensor(
                    macc[:], macc[:], ohk(k, i), ALU.add))[1], waits=[ev], inc="Sdv")
        ev = pl.op("dv", lambda e, i=i: (e.drain(), e.tensor_tensor(
            ohall[:, i * 64:(i + 1) * 64].rearrange("p (k e) -> p k e", k=4),
            ohall[:, i * 64:(i + 1) * 64].rearrange("p (k e) -> p k e", k=4),
            C["IOTA16"][:].unsqueeze(1).broadcast_to([128, 4, 16]), ALU.mult))[1],
            waits=[ev], inc="Sdv")
        ev = pl.op("dv", lambda e, i=i: (e.drain(), e.tensor_reduce(
            idx4[:].unsqueeze(2), ohall[:, i * 64:(i + 1) * 64].rearrange("p (k e) -> p k e", k=4),
            AXX, ALU.add))[1], waits=[ev], inc="Sdv")
        ev = pl.op("dv", lambda e: (e.drain(), e.scalar_tensor_tensor(
            gslf[:], idx4[:], 1024.0, rnk4[:], ALU.mult, ALU.add))[1], waits=[ev], inc="Sdv")
        ev = pl.op("dv", lambda e: (e.drain(), e.tensor_scalar(
            ku8[:], rnk4[:], float(CAP), None, ALU.is_ge))[1], waits=[ev], inc="Sdv")
        ev = pl.op("dv", lambda e: (e.drain(), e.copy_predicated(
            gslf[:], ku8[:], sntbig4()))[1],
            waits=[ev, evsn0], inc="Sdv")
        ev = pl.op("dv", lambda e, i=i: (e.drain(), e.tensor_copy(
            gslots[:, i * 4:(i + 1) * 4], gslf[:]))[1], waits=[ev], inc="Sdv")
        ev_gsl[i] = ev
        ev = pl.op("dv", lambda e, i=i: (e.drain(), e.tensor_copy(
            clsrc[:].rearrange("p (k two) -> p two k", two=2)[:, 0, :],
            C["tokids"][:, i:i + 1].broadcast_to([128, 4])))[1],
            waits=[ev, ev_w4, (ev_clsc[-1] if ev_clsc else None)], inc="Sdv")
        ev = pl.op("dv", lambda e: (e.drain(), e.tensor_copy(
            clsrc[:].rearrange("p (k two) -> p two k", two=2)[:, 1, :], w4[:, 4:8]))[1],
            waits=[ev], inc="Sdv")
        for k in range(4):
            evs = pl.op("gp", lambda e, i=i, k=k: e.indirect_dma_start(
                clm_c.ap(), IndirectOffsetOnAxis(gslots[:, i * 4 + k:i * 4 + k + 1], 0),
                clsrc[:, 2 * k:2 * k + 2], None,
                bounds_check=rbnd, oob_is_err=False),
                waits=[ev, ev_gsl[i], evsn], inc="Scl", amt=16)
            ev_clsc.append(evs)
    ev_ar = pl.op("gp", lambda e: e.collective_compute(
        "AllReduce", ALU.min, rg, ins=[clm_c.ap().opt()], outs=[clm_g.ap().opt()]),
        waits=[evsn] + ev_clsc[-1:], inc="Scc")
    PH3 = pl.barrier()
    replay(nc, pl, sems)

    if upto <= 3:
        ph3.close()
        return finish_dummy()

    B3 = pl.barrier()
    ph3.close()
    gate5 = [(s, v) for (s, v) in B3 if s != "Scc"]
    ph5 = contextlib.ExitStack()
    EP = ph5.enter_context
    clmt = EP(nc.sbuf_tensor("clmt", [128, 32], F32))
    toki = EP(nc.sbuf_tensor("toki", [128, 16], I32))
    wbufc = EP(nc.sbuf_tensor("wbufc", [128, 16], F32))
    xg_tm = EP(nc.sbuf_tensor("xg_tm", [128, 8 * D], BF16))
    xg_fm = EP(nc.sbuf_tensor("xg_fm", [128, 16 * CAP], BF16))
    ew1 = EP(nc.sbuf_tensor("ew1", [128, 2 * 16 * 128], BF16))
    ew2 = EP(nc.sbuf_tensor("ew2", [128, 2 * 16 * 128], BF16))
    midE = EP(nc.sbuf_tensor("midE", [128, 11 * CAP], BF16))
    ewd = EP(nc.sbuf_tensor("ewd", [128, 2 * 11 * 512], BF16))
    ge_sb = EP(nc.sbuf_tensor("ge_sb", [128, CAP], F32))
    ee_sb = EP(nc.sbuf_tensor("ee_sb", [128, CAP], F32))
    re_sb = EP(nc.sbuf_tensor("re_sb", [128, CAP], F32))
    eo_sb = EP(nc.sbuf_tensor("eo_sb", [128, 2 * 512], BF16))

    pl = Plan(cnt)
    gate_all(pl, gate5)
    clm_flat = clm_g.ap().flatten().unsqueeze(1)
    ev_clm = None
    for j in range(16):
        ev_clm = pl.op("gp", lambda e, j=j: e.indirect_dma_start(
            clmt[:, 2 * j:2 * j + 2], None,
            clm_flat, IndirectOffsetOnAxis(C["clmidx"][:, j:j + 1], 0),
            oob_is_err=True),
            waits=[ev_ar], inc="Sg0", amt=16)
    ev = pl.op("dv", lambda e: e.tensor_copy(
        toki[:], clmt[:].rearrange("p (j two) -> p two j", two=2)[:, 0, :]),
        waits=[ev_clm], inc="Sdv")
    ev_tok = pl.op("dv", lambda e: (e.drain(), e.tensor_copy(
        wbufc[:], clmt[:].rearrange("p (j two) -> p two j", two=2)[:, 1, :]))[1],
        waits=[ev], inc="Sdv")
    pl.op("gp", lambda e: e.reg_mov(rbnd, L - 1))
    ev_eoutw = None
    ev_xg_lastT = None
    ev_xgfm_rd = None
    ev_midE_rd = None
    ev_ewd = [None, None]
    ev_eo_used = [None, None]
    for el in range(EPC):
        ev_ms5 = pl.op("gp", lambda e: e.memset(xg_tm[:], 0.0),
                       waits=[ev_xg_lastT], inc="Sgp")
        ev_xg = None
        for ch in range(8):
            ev_xg = pl.op("gp", lambda e, el=el, ch=ch: e.indirect_dma_start(
                xg_tm[:, ch * D:(ch + 1) * D], None,
                x2ln_g.ap(), IndirectOffsetOnAxis(toki[:, el * 8 + ch: el * 8 + ch + 1], 0),
                bounds_check=rbnd, oob_is_err=False),
                waits=[ev_tok, ev_ms5, ev_ag3], inc="Sg1", amt=16)
        ev_fm = None
        for ch in range(8):
            for kc in range(16):
                bank = (ch * 16 + kc) % 2
                slot = bank * 8 + ((ch * 16 + kc) // 2) % 8
                evt = pl.op("pe", lambda e, ch=ch, kc=kc, slot=slot: e.transpose(
                    pbT[:, slot * 128:(slot + 1) * 128],
                    xg_tm[:, ch * D + kc * 128: ch * D + (kc + 1) * 128], idb[:]),
                    waits=[ev_xg, evT3b[bank]], inc="Spe")
                evv = pl.op("dv", lambda e, ch=ch, kc=kc, slot=slot: e.tensor_copy(
                    xg_fm[:, kc * CAP + ch * 128: kc * CAP + (ch + 1) * 128],
                    pbT[:, slot * 128:(slot + 1) * 128]),
                    waits=[evt, ev_xgfm_rd], inc="Sdv")
                evT3b[bank] = evv
                ev_fm = evv
                ev_xg_lastT = evt
        ev_ew1u = [None, None]
        ev_ew2u = [None, None]
        ev_resb_rd = [None, None]
        ev_midl = None
        for it in range(11):
            sl = it % 2
            evg = pl.op("gp", lambda e, el=el, it=it, sl=sl: e.dma_start(
                out=ew1[:, sl * 2048:(sl + 1) * 2048].rearrange("p (kc c) -> p kc c", kc=16),
                in_=din["wg2"][el, it * D:(it + 1) * D, :].rearrange("(kc q) c -> q kc c", q=128)),
                waits=[ev_ew1u[sl]], inc="Ssh0" if sl == 0 else "Ssh1", amt=16)
            evu = pl.op("gp", lambda e, el=el, it=it, sl=sl: e.dma_start(
                out=ew2[:, sl * 2048:(sl + 1) * 2048].rearrange("p (kc c) -> p kc c", kc=16),
                in_=din["wu2"][el, it * D:(it + 1) * D, :].rearrange("(kc q) c -> q kc c", q=128)),
                waits=[ev_ew2u[sl]], inc="Sgt0" if sl == 0 else "Sgt1", amt=16)
            for jt in range(2):
                bgate = jt % 2
                bup = 2 + jt % 2
                for kc in range(16):
                    pe_bank(pl, bgate, lambda e, it=it, jt=jt, kc=kc, sl=sl, bgate=bgate, bup=bup: e.matmul(
                        pb[bgate][:, 0:512],
                        ew1[:, sl * 2048 + kc * 128: sl * 2048 + (kc + 1) * 128],
                        xg_fm[:, kc * CAP + jt * 512: kc * CAP + (jt + 1) * 512],
                        start=(kc == 0), stop=(kc == 15)),
                        extra=[evg, ev_fm], start_new=(kc == 0))
                for kc in range(16):
                    pe_bank(pl, bup, lambda e, it=it, jt=jt, kc=kc, sl=sl, bup=bup: e.matmul(
                        pb[bup][:, 0:512],
                        ew2[:, sl * 2048 + kc * 128: sl * 2048 + (kc + 1) * 128],
                        xg_fm[:, kc * CAP + jt * 512: kc * CAP + (jt + 1) * 512],
                        start=(kc == 0), stop=(kc == 15)),
                        extra=[evu, ev_fm], start_new=(kc == 0))
                co = jt * 512
                ev = rd_bank(pl, "sc", bgate, lambda e, co=co, bgate=bgate: e.activation(
                    re_sb[:, co:co + 512], pb[bgate][:, 0:512], AF.Silu),
                    extra=[ev_resb_rd[jt]])
                ev = rd_bank(pl, "dv", bup, lambda e, it=it, co=co, bup=bup: e.tensor_tensor(
                    midE[:, it * CAP + co: it * CAP + co + 512], re_sb[:, co:co + 512],
                    pb[bup][:, 0:512], ALU.mult), extra=[ev, ev_midE_rd])
                ev_resb_rd[jt] = ev
                ev_midl = ev
            ev_ew1u[sl] = ev_midl
            ev_ew2u[sl] = ev_midl
        ev_xgfm_rd = ev_midl
        for jg in range(4):
            sl = jg % 2
            evw = pl.op("gp", lambda e, el=el, jg=jg, sl=sl: e.dma_start(
                out=ewd[:, sl * 5632:(sl + 1) * 5632].rearrange("p (ki c) -> p ki c", ki=11),
                in_=din["wd2"][el, jg * I_FF:(jg + 1) * I_FF, :].rearrange("(ki q) c -> q ki c", q=128)),
                waits=[ev_ewd[sl]], inc="Seo" if sl == 0 else "Seo1", amt=16)
            evm_last = None
            for i in range(8):
                bnk = (jg * 8 + i) % 3
                evm = None
                for ki in range(11):
                    evm = pe_bank(pl, bnk, lambda e, i=i, jg=jg, ki=ki, sl=sl, bnk=bnk: e.matmul(
                        pb[bnk][:, 0:512],
                        midE[:, ki * CAP + i * 128: ki * CAP + (i + 1) * 128],
                        ewd[:, sl * 5632 + ki * 512: sl * 5632 + (ki + 1) * 512],
                        start=(ki == 0), stop=(ki == 10)),
                        extra=[evw, ev_midl], start_new=(ki == 0))
                evm_last = evm
                esl = (jg * 8 + i) % 2
                ev = rd_bank(pl, "dv", bnk, lambda e, el=el, i=i, esl=esl, bnk=bnk: e.tensor_scalar(
                    eo_sb[:, esl * 512:(esl + 1) * 512], pb[bnk][:, 0:512],
                    wbufc[:, el * 8 + i: el * 8 + i + 1], None, ALU.mult),
                    extra=[ev_tok, ev_eo_used[esl]])
                ev_eoutw = pl.op("sy", lambda e, el=el, i=i, jg=jg, esl=esl: e.dma_start(
                    out=eout_c.ap()[el * CAP + i * 128: el * CAP + (i + 1) * 128,
                                    jg * 512:(jg + 1) * 512],
                    in_=eo_sb[:, esl * 512:(esl + 1) * 512]),
                    waits=[ev], inc="Syc" if esl == 0 else "Syc1", amt=16)
                ev_eo_used[esl] = ev_eoutw
            ev_ewd[sl] = evm_last
        ev_midE_rd = evm_last
    scc_pre_ag5 = cnt["Scc"]
    ev_ag5 = pl.op("gp", lambda e: e.collective_compute(
        "AllGather", ALU.bypass, rg, ins=[eout_c.ap().opt()], outs=[eout_g.ap().opt()]),
        waits=[ev_eo_used[0], ev_eo_used[1]], inc="Scc")
    B5 = pl.barrier()
    replay(nc, pl, sems)
    ph5.close()

    gate4 = ([(s, v) for (s, v) in B5 if s != "Scc"]
             + ([("Scc", scc_pre_ag5)] if scc_pre_ag5 > 0 else []))
    ph4 = contextlib.ExitStack()
    EP = ph4.enter_context
    swt = EP(nc.sbuf_tensor("swt", [128, 2 * 16 * 128], BF16))
    swt2 = EP(nc.sbuf_tensor("swt2", [128, 2 * 16 * 128], BF16))
    mid2 = EP(nc.sbuf_tensor("mid2", [128, 22 * TPC], BF16))
    g_sb = EP(nc.sbuf_tensor("g_sb", [128, TPC], F32))
    e_sb = EP(nc.sbuf_tensor("e_sb", [128, TPC], F32))
    r_sb = EP(nc.sbuf_tensor("r_sb", [128, TPC], F32))
    swd_sb = EP(nc.sbuf_tensor("swd_sb", [128, 2 * 22 * 512], BF16))

    NIT = IS_FF // 128
    pl = Plan(cnt)
    gate_all(pl, gate4)
    ev_swu = [None, None]
    ev_swu2 = [None, None]
    ev_rsb_rd = [None, None]
    ev_mid = {}
    for it in range(NIT):
        sl = it % 2
        evg = pl.op("gp", lambda e, it=it, sl=sl: e.dma_start(
            out=swt[:, sl * 2048:(sl + 1) * 2048].rearrange("p (kc c) -> p kc c", kc=16),
            in_=din["swg"][it * D:(it + 1) * D, :].rearrange("(kc q) c -> q kc c", q=128)),
            waits=[ev_swu[sl]], inc="Ssh0" if sl == 0 else "Ssh1", amt=16)
        evu = pl.op("gp", lambda e, it=it, sl=sl: e.dma_start(
            out=swt2[:, sl * 2048:(sl + 1) * 2048].rearrange("p (kc c) -> p kc c", kc=16),
            in_=din["swu"][it * D:(it + 1) * D, :].rearrange("(kc q) c -> q kc c", q=128)),
            waits=[ev_swu2[sl]], inc="Sgt0" if sl == 0 else "Sgt1", amt=16)
        evac_done = []
        for jt in range(2):
            bg = (2 * it + jt) % 2
            bgate = (2 * it + jt) % 2
            bup = 2 + (2 * it + jt) % 2
            evmg = None
            evmu = None
            for kc in range(16):
                evmg = pe_bank(pl, bgate, lambda e, it=it, jt=jt, kc=kc, sl=sl, bgate=bgate, bup=bup: e.matmul(
                    pb[bgate][:, 0:512],
                    swt[:, sl * 2048 + kc * 128: sl * 2048 + (kc + 1) * 128],
                    x2fm[:, kc * TPC + jt * 512: kc * TPC + (jt + 1) * 512],
                    start=(kc == 0), stop=(kc == 15)),
                    extra=[evg, ev_x2fm_last], start_new=(kc == 0))
            for kc in range(16):
                evmu = pe_bank(pl, bup, lambda e, it=it, jt=jt, kc=kc, sl=sl, bup=bup: e.matmul(
                    pb[bup][:, 0:512],
                    swt2[:, sl * 2048 + kc * 128: sl * 2048 + (kc + 1) * 128],
                    x2fm[:, kc * TPC + jt * 512: kc * TPC + (jt + 1) * 512],
                    start=(kc == 0), stop=(kc == 15)),
                    extra=[evu, ev_x2fm_last], start_new=(kc == 0))
            co = jt * 512
            ev = rd_bank(pl, "sc", bgate, lambda e, co=co, bgate=bgate: e.activation(
                r_sb[:, co:co + 512], pb[bgate][:, 0:512], AF.Silu),
                extra=[ev_rsb_rd[jt]])
            ev = rd_bank(pl, "dv", bup, lambda e, it=it, co=co, bup=bup: e.tensor_tensor(
                mid2[:, it * TPC + co: it * TPC + co + 512], r_sb[:, co:co + 512],
                pb[bup][:, 0:512], ALU.mult), extra=[ev])
            ev_rsb_rd[jt] = ev
            evac_done.append(ev)
        ev_swu[sl] = evac_done[-1]
        ev_swu2[sl] = evac_done[-1]
        ev_mid[it] = evac_done[-1]
    ev_mid_all = ev_mid[NIT - 1]
    replay(nc, pl, sems)

    pl = Plan(cnt)
    ev_swd = [None, None]
    shstage = g_sb
    x2sb = e_sb
    ev_x2u = [None, None]
    ev_shw = None
    for jg in range(4):
        sl = jg % 2
        evw = pl.op("gp", lambda e, jg=jg, sl=sl: e.dma_start(
            out=swd_sb[:, sl * 11264:(sl + 1) * 11264].rearrange("p (ki c) -> p ki c", ki=NIT),
            in_=din["swd"][jg * IS_FF:(jg + 1) * IS_FF, :].rearrange("(ki q) c -> q ki c", q=128)),
            waits=[ev_swd[sl]], inc="Ssh0" if sl == 0 else "Ssh1", amt=16)
        evm_last = None
        for i in range(NTT):
            bnk = (jg * NTT + i) % 3
            sl2 = (jg * NTT + i) % 2
            ev_x2ld = pl.op("sy", lambda e, i=i, jg=jg, sl2=sl2: e.dma_start(
                out=x2sb[:, sl2 * 512:(sl2 + 1) * 512],
                in_=x2_d[i * 128:(i + 1) * 128, jg * 512:(jg + 1) * 512]),
                waits=[ev_x2u[sl2]], inc="Sxw0" if sl2 == 0 else "Sxw1", amt=16)
            evm = None
            for ki in range(NIT):
                evm = pe_bank(pl, bnk, lambda e, i=i, jg=jg, ki=ki, sl=sl, bnk=bnk: e.matmul(
                    pb[bnk][:, 0:512],
                    mid2[:, ki * TPC + i * 128: ki * TPC + (i + 1) * 128],
                    swd_sb[:, sl * 11264 + ki * 512: sl * 11264 + (ki + 1) * 512],
                    start=(ki == 0), stop=(ki == NIT - 1)),
                    extra=[evw, ev_mid_all], start_new=(ki == 0))
            evm_last = evm
            ev = rd_bank(pl, "dv", bnk, lambda e, bnk=bnk, sl2=sl2: e.tensor_tensor(
                shstage[:, 0:512], pb[bnk][:, 0:512], x2sb[:, sl2 * 512:(sl2 + 1) * 512],
                ALU.add), extra=[ev_shw, ev_x2ld])
            ev_x2u[sl2] = ev
            ev_shw = pl.op("sy", lambda e, i=i, jg=jg: e.dma_start(
                out=shps_d[i * 128:(i + 1) * 128, jg * 512:(jg + 1) * 512],
                in_=shstage[:, 0:512]), waits=[ev], inc="Sx2", amt=16)
        ev_swd[sl] = evm_last
    EV_SH_DONE = ("Sx2", cnt["Sx2"])
    B4 = pl.barrier()
    replay(nc, pl, sems)
    ph4.close()

    ph6 = contextlib.ExitStack()
    EP = ph6.enter_context
    gth = [EP(nc.sbuf_tensor(f"gth{k}", [128, D], BF16)) for k in range(8)]
    shrt = [EP(nc.sbuf_tensor(f"shrt{k}", [128, D], F32)) for k in range(2)]
    s1t = [EP(nc.sbuf_tensor(f"s1t{k}", [128, D], F32)) for k in range(2)]
    s2t = [EP(nc.sbuf_tensor(f"s2t{k}", [128, D], F32)) for k in range(2)]
    outt = [EP(nc.sbuf_tensor(f"outt{k}", [128, D], F32)) for k in range(2)]

    pl = Plan(cnt)
    gate_all(pl, B4)
    pl.op("gp", lambda e: e.reg_mov(rbnd, NCORE * EPC * CAP - 1))
    ev_out_dma = None
    ev_prev = {}
    for i in range(NTT):
        b = i % 2
        base = b * 4
        evs = [None] * 4
        for k in range(4):
            ev_msk = pl.op("dv", lambda e, kk=base + k: e.memset(gth[kk][:], 0.0),
                           waits=[ev_prev.get(f"g{base + k}")], inc="Sdv")
            evs[k] = pl.op("gp", lambda e, i=i, k=k, kk=base + k: e.indirect_dma_start(
                gth[kk][:], None, eout_g.ap(),
                IndirectOffsetOnAxis(gslots[:, i * 4 + k: i * 4 + k + 1], 0),
                bounds_check=rbnd, oob_is_err=False),
                waits=[ev_msk, ev_ag5, ev_gsl[i]],
                inc=["Sg1", "Swo0", "Swo1", "Seo"][k], amt=16)
        ev_shl = pl.op("sy", lambda e, i=i, b=b: e.dma_start(
            out=shrt[b][:], in_=shps_d[i * 128:(i + 1) * 128, :]),
            waits=[ev_prev.get(f"sh{b}")], inc="Sxw1", amt=16)
        ev = pl.op("dv", lambda e, b=b: e.tensor_tensor(
            s1t[b][:], gth[base][:], gth[base + 1][:], ALU.add),
            waits=[evs[0], evs[1], ev_prev.get(f"s1{b}")], inc="Sdv")
        ev_prev[f"g{base}"] = ev
        ev_prev[f"g{base + 1}"] = ev
        ev2 = pl.op("dv", lambda e, b=b: e.tensor_tensor(
            s2t[b][:], gth[base + 2][:], gth[base + 3][:], ALU.add),
            waits=[evs[2], evs[3], ev_prev.get(f"s2{b}")], inc="Sdv")
        ev_prev[f"g{base + 2}"] = ev2
        ev_prev[f"g{base + 3}"] = ev2
        ev3 = pl.op("dv", lambda e, b=b: (e.drain(), e.tensor_tensor(
            s1t[b][:], s1t[b][:], s2t[b][:], ALU.add))[1],
            waits=[ev, ev2], inc="Sdv")
        ev_prev[f"s2{b}"] = ev3
        ev5 = pl.op("dv", lambda e, b=b: e.tensor_tensor(
            outt[b][:], s1t[b][:], shrt[b][:], ALU.add),
            waits=[ev3, ev_shl, ev_prev.get(f"out{b}")], inc="Sdv")
        ev_prev[f"s1{b}"] = ev5
        ev_prev[f"sh{b}"] = ev5
        ev_out_dma = pl.op("sy", lambda e, i=i, b=b: e.dma_start(
            out=d_out[i * 128:(i + 1) * 128, :], in_=outt[b][:]),
            waits=[ev5], inc="Sou", amt=16)
        ev_prev[f"out{b}"] = ev_out_dma
    pl.op("sy", lambda e: None, waits=[ev_out_dma])
    replay(nc, pl, sems)
    ph6.close()
    ph34.close()
    outer.close()
    return nc, dbg_outs



def kernel(**inputs):
    per_core = host_prepare(inputs)
    shapes = {k: (v.shape, v.dtype) for k, v in per_core[0].items()}
    nc, _ = build_program(shapes)
    res = run_bass_kernel_spmd(nc, per_core, core_ids=list(range(NCORE)))
    outs = [np.asarray(r["out"]) for r in res.results]
    return np.concatenate(outs, axis=0).reshape(1, L, D)


# version 10
# speedup vs baseline: 1.0483x; 1.0483x over previous
from collections import defaultdict
import contextlib

import numpy as np
import ml_dtypes

import concourse.bass as bass
import concourse.mybir as mybir
from concourse.bass_utils import run_bass_kernel_spmd
from concourse.bass import IndirectOffsetOnAxis

F32 = mybir.dt.float32
BF16 = mybir.dt.bfloat16
I32 = mybir.dt.int32
AF = mybir.ActivationFunctionType
ALU = mybir.AluOpType
AXX = mybir.AxisListType.X

NCORE = 8
L = 8192
D = 2048
H = 64
HP = H // NCORE
HPP = HP // 2
P = 64
N = 128
TC = 128
NCH = L // TC
TPC = L // NCORE
NTT = TPC // 128
GRP = 16
NGRP = NCH // GRP
E = 16
EPC = E // NCORE
CAP = 1024
I_FF = 1408
IS_FF = 2816
EPS_RMS = 1e-6
EPS_LN = 1e-5
BIGNEG = -1e30
SENT = 20000.0
TWO_PI = float(2.0 * np.pi)
INV_2PI = float(1.0 / (2.0 * np.pi))
MAGIC = float(1.5 * 2.0 ** 23)
CB = 0
CC = HPP * N
CX = 2 * HPP * N
CZ = CX + HPP * P
CT = CZ + HPP * P
CD = CT + HPP * (N // 2)
CL = CD + HPP
PASSW = CL + HPP

ENGS = ("sy", "pe", "dv", "sc", "gp")
SEMS = ("Ssy", "Spe", "Sdv", "Ssc", "Sgp", "Sgw", "Scc", "Sou", "Sx0", "Sx1",
        "SlA0", "SlA1", "SlB0", "SlB1", "Sxc", "Syc", "Swo0", "Swo1", "Ssh0",
        "Ssh1", "Sxw0", "Sxw1", "Sg0", "Sg1", "Seo", "Scl", "Sx2", "Sgt0", "Sgt1", "Scnt", "Scn2", "Seo1", "Syc1")


class Plan:
    def __init__(self, cnt):
        self.streams = {e: [] for e in ENGS}
        self.cnt = cnt

    def op(self, eng, fn, waits=(), inc=None, amt=1):
        ev = None
        if inc is not None:
            self.cnt[inc] += amt
            ev = (inc, self.cnt[inc])
        ws = tuple(w for w in waits if w is not None)
        self.streams[eng].append((fn, ws, (inc, amt) if inc else None))
        return ev

    def barrier(self):
        return [(s, self.cnt[s]) for s in SEMS if self.cnt[s] > 0]


def gate_all(pl, evs):
    for g_ in ENGS:
        pl.op(g_, lambda e: None, waits=evs)


def replay(nc, plan, sems):
    with nc.Block() as block:
        def mk(ename):
            def body(eng):
                seen = {}
                for fn, waits, inc in plan.streams[ename]:
                    for (s, v) in waits:
                        if seen.get(s, -1) < v:
                            eng.wait_ge(sems[s], v)
                            seen[s] = v
                    inst = fn(eng)
                    if inc is not None:
                        if inst is None:
                            inst = eng.nop()
                        inst.then_inc(sems[inc[0]], inc[1])
            return body
        for ename, attr in (("sy", "sync"), ("pe", "tensor"), ("dv", "vector"),
                            ("sc", "scalar"), ("gp", "gpsimd")):
            getattr(block, attr)(mk(ename))



def _perm_cols():
    return np.arange(N).reshape(N // 2, 2).T.reshape(-1)


def host_prepare(inputs):
    f32 = np.float32
    g = {k: np.asarray(v) for k, v in inputs.items()}
    x = np.ascontiguousarray(np.asarray(g["L0_attn_x"]).reshape(L, D).astype(f32, copy=False))
    perm = _perm_cols()

    WxT = g["L0_attn_Wx"].astype(f32).T
    WzT = g["L0_attn_Wz"].astype(f32).T
    WBT = g["L0_attn_WB"].astype(f32).T
    WCT = g["L0_attn_WC"].astype(f32).T
    WthT = g["L0_attn_Wth"].astype(f32).T
    WdtT = g["L0_attn_Wdt"].astype(f32).T
    WlamT = g["L0_attn_Wlam"].astype(f32).T
    WoutT = g["L0_attn_Wout"].astype(f32).T
    WoutP = np.ascontiguousarray(
        WoutT.reshape(32, 128, 4, 512).transpose(2, 0, 1, 3).reshape(4 * 4096, 512))
    A_log = g["L0_attn_A_log"].astype(f32)
    Bn_w = g["L0_attn_Bn_w"].astype(f32)[perm]
    Cn_w = g["L0_attn_Cn_w"].astype(f32)[perm]
    B_bias = g["L0_attn_B_bias"].astype(f32)[:, perm]
    C_bias = g["L0_attn_C_bias"].astype(f32)[:, perm]

    gate_wT = np.ascontiguousarray(g["L0_ffn_gate_w"].astype(f32).T)
    w_gate = g["L0_ffn_w_gate"].astype(f32)
    w_up = g["L0_ffn_w_up"].astype(f32)
    w_down = g["L0_ffn_w_down"].astype(f32)
    swg = np.ascontiguousarray(
        g["L0_ffn_sw_gate"].astype(f32).reshape(D, 22, 128).transpose(1, 0, 2).reshape(22 * D, 128))
    swu = np.ascontiguousarray(
        g["L0_ffn_sw_up"].astype(f32).reshape(D, 22, 128).transpose(1, 0, 2).reshape(22 * D, 128))
    swd = np.ascontiguousarray(
        g["L0_ffn_sw_down"].astype(f32).reshape(22, 128, 4, 512).transpose(2, 0, 1, 3).reshape(4 * IS_FF, 512))

    ident_f = np.eye(128, dtype=f32)
    kk, tt = np.meshgrid(np.arange(128), np.arange(128), indexing="ij")
    consts = dict(
        ident_f=ident_f,
        ident_b=ident_f.astype(ml_dtypes.bfloat16),
        L1incl=np.triu(np.ones((128, 128), dtype=f32)),
        L0strict=np.triu(np.ones((128, 128), dtype=f32), 1),
        Ushift=np.eye(128, k=-1, dtype=f32),
        onehot127=(np.arange(128) == 127).astype(f32).reshape(128, 1),
        onehot127b=(np.arange(128) == 127).astype(ml_dtypes.bfloat16).reshape(128, 1),
        ones_col=np.ones((128, 1), dtype=f32),
        halfpi=np.full((128, 1), np.pi / 2, dtype=f32),
        ones_row=np.ones((1, 128), dtype=f32),
        MBincl=np.where(kk <= tt, 0.0, BIGNEG).astype(f32),
        MBstrict=np.where(kk < tt, 0.0, BIGNEG).astype(f32),
        M127col=np.where(np.arange(128) == 127, BIGNEG, 0.0).astype(f32).reshape(128, 1),
        IOTA16=np.tile(np.arange(16, dtype=f32), (128, 1)),
        RIOTA16=np.tile(np.arange(15, -1, -1, dtype=f32), (128, 1)),
    )
    oh = np.zeros((HPP, HPP * 128), dtype=f32)
    for h in range(HPP):
        oh[h, h * 128:(h + 1) * 128] = 1.0
    consts["onehotHPP"] = oh

    per_core = []
    for c in range(NCORE):
        d = dict(consts)
        d["x_slice"] = np.ascontiguousarray(x[c * TPC:(c + 1) * TPC])
        blocks = []
        for p in range(2):
            hs = c * HP + p * HPP
            he = hs + HPP
            Bc = WBT[:, hs * N:he * N].reshape(D, HPP, N)[:, :, perm].reshape(D, HPP * N)
            Cc = WCT[:, hs * N:he * N].reshape(D, HPP, N)[:, :, perm].reshape(D, HPP * N)
            blocks.append(np.concatenate(
                [Bc, Cc, WxT[:, hs * P:he * P], WzT[:, hs * P:he * P],
                 WthT[:, hs * (N // 2):he * (N // 2)], WdtT[:, hs:he],
                 WlamT[:, hs:he]], axis=1))
        d["attnW"] = np.ascontiguousarray(np.concatenate(blocks, axis=0))
        d["WoutT"] = WoutP
        d["A_log_col"] = np.ascontiguousarray(A_log[c * HP:(c + 1) * HP].reshape(HP, 1))
        d["BCnw"] = np.ascontiguousarray(
            np.concatenate([np.tile(Bn_w, (128, HPP)), np.tile(Cn_w, (128, HPP))], axis=1))
        bc = []
        for p in range(2):
            hs = c * HP + p * HPP
            bc.append(np.tile(np.concatenate(
                [B_bias[hs:hs + HPP].reshape(HPP * N), C_bias[hs:hs + HPP].reshape(HPP * N)]
            ).reshape(1, 2 * HPP * N), (128, 1)))
        d["BCbias"] = np.ascontiguousarray(np.concatenate(bc, axis=1))
        d["gate_wT"] = gate_wT
        d["swg"] = swg
        d["swu"] = swu
        d["swd"] = swd
        d["wg2"] = np.ascontiguousarray(
            w_gate[2 * c:2 * c + 2].reshape(2, D, 11, 128).transpose(0, 2, 1, 3).reshape(2, 11 * D, 128))
        d["wu2"] = np.ascontiguousarray(
            w_up[2 * c:2 * c + 2].reshape(2, D, 11, 128).transpose(0, 2, 1, 3).reshape(2, 11 * D, 128))
        d["wd2"] = np.ascontiguousarray(
            w_down[2 * c:2 * c + 2].reshape(2, 11, 128, 4, 512).transpose(0, 3, 1, 2, 4).reshape(2, 4 * I_FF, 512))
        d["tokids"] = np.ascontiguousarray(
            (np.arange(TPC, dtype=f32) + c * TPC).reshape(NTT, 128).T)
        p_ = np.arange(128)
        d["ygidx"] = np.ascontiguousarray(
            (((c // 2) * 4096 + np.arange(32)[None, :] * 128 + p_[:, None]) * 2048
             + (c % 2) * TPC).astype(np.int32))
        ch_ = np.arange(16)
        d["clmidx"] = np.ascontiguousarray(
            ((((ch_ // 8 + 2 * c) * CAP) + (ch_ % 8) * 128)[None, :] + p_[:, None]
             ).astype(np.int32) * 2)
        d["coremask"] = np.ascontiguousarray(
            (np.arange(NCORE) < c).astype(f32).reshape(NCORE, 1))
        d["NEG16"] = np.full((128, 16), -1e30, dtype=f32)
        per_core.append(d)
    return per_core



def build_program(shapes_dtypes, upto=99, debug=(), nchunks_attn=NCH, two_pass=True):
    nc = bass.Bass("TRN2", target_bir_lowering=False, debug=False)
    din = {}
    for k, (shp, dt) in shapes_dtypes.items():
        if dt == ml_dtypes.bfloat16:
            bdt = BF16
        elif np.dtype(dt) == np.int32:
            bdt = I32
        else:
            bdt = F32
        din[k] = nc.dram_tensor(k, list(shp), bdt, kind="ExternalInput").ap()

    d_out = nc.dram_tensor("out", [TPC, D], F32, kind="ExternalOutput").ap()
    dbg_outs = {}

    def dbg_decl(name, shape, dt=F32):
        if name in debug:
            dbg_outs[name] = nc.dram_tensor("dbg_" + name, list(shp := shape), dt,
                                            kind="ExternalOutput").ap()
            return dbg_outs[name]
        return None

    xln_c = nc.dram_tensor("xln_c", [2, D, TPC // 2], BF16)
    xln_g = nc.dram_tensor("xln_g", [2, NCORE * D, TPC // 2], BF16)
    y_c = nc.dram_tensor("y_c", [4, HP * P, L // 4], BF16)
    y_g = nc.dram_tensor("y_g", [4, NCORE * HP * P, L // 4], BF16)
    x2ln_c = nc.dram_tensor("x2ln_c", [TPC, D], BF16)
    x2ln_g = nc.dram_tensor("x2ln_g", [NCORE * TPC, D], BF16)
    cnt_c = nc.dram_tensor("cnt_c", [1, E], F32)
    cnt_g = nc.dram_tensor("cnt_g", [NCORE, E], F32)
    clm_c = nc.dram_tensor("clm_c", [E * CAP, 2], F32)
    clm_g = nc.dram_tensor("clm_g", [E * CAP, 2], F32)
    eout_c = nc.dram_tensor("eout_c", [EPC * CAP, D], BF16)
    x2_d = nc.dram_tensor("x2_d", [TPC, D], F32)
    shps_d = nc.dram_tensor("shps_d", [TPC, D], F32)
    eout_g = nc.dram_tensor("eout_g", [NCORE * EPC * CAP, D], BF16)

    cnt = defaultdict(int)
    outer = contextlib.ExitStack()
    EO = outer.enter_context
    sems = {s: EO(nc.semaphore(s)) for s in SEMS}
    rg = [list(range(NCORE))]

    CONST_SPECS = [
        ("ident_f", [128, 128], F32), ("ident_b", [128, 128], BF16),
        ("onehot127b", [128, 1], BF16),
        ("L1incl", [128, 128], F32), ("L0strict", [128, 128], F32),
        ("Ushift", [128, 128], F32), ("onehot127", [128, 1], F32),
        ("ones_col", [128, 1], F32), ("ones_row", [1, 128], F32),
        ("halfpi", [128, 1], F32),
        ("MBincl", [128, 128], F32), ("MBstrict", [128, 128], F32),
        ("M127col", [128, 1], F32), ("onehotHPP", [HPP, HPP * 128], F32),
        ("IOTA16", [128, 16], F32), ("RIOTA16", [128, 16], F32),
        ("A_log_col", [HP, 1], F32), ("BCnw", [128, 2 * HPP * N], F32),
        ("BCbias", [128, 4 * HPP * N], F32), ("tokids", [128, NTT], F32),
        ("ygidx", [128, 32], I32), ("clmidx", [128, 16], I32),
        ("coremask", [NCORE, 1], F32), ("NEG16", [128, 16], F32),
    ]
    C = {nm: EO(nc.sbuf_tensor("c_" + nm, shp, dt)) for nm, shp, dt in CONST_SPECS}
    A_row = EO(nc.sbuf_tensor("A_row", [HP, 1], F32))
    A_rowT = EO(nc.sbuf_tensor("A_rowT", [1, HP], F32))
    A_bc = EO(nc.sbuf_tensor("A_bc", [128, HP], F32))
    pb = [EO(nc.psum_tensor(f"pb{i}", [128, 512], F32)) for i in range(5)]
    pbT = EO(nc.psum_tensor("pbT", [128, 2048], BF16))
    pb7 = EO(nc.psum_tensor("pb7", [128, 512], F32))

    idf = C["ident_f"]
    idb = C["ident_b"]
    rbnd = nc.gpsimd.alloc_register()

    pl = Plan(cnt)
    ev_c = None
    for nm in C:
        ev_c = pl.op("sy", lambda e, nm=nm: e.dma_start(out=C[nm][:], in_=din[nm]),
                     inc="Ssy", amt=16)
    ev = pl.op("sc", lambda e: e.activation(A_row[:], C["A_log_col"][:], AF.Exp),
               waits=[ev_c], inc="Ssc")
    ev = pl.op("dv", lambda e: e.tensor_scalar(A_row[:], A_row[:], -1.0, None, ALU.mult),
               waits=[ev], inc="Sdv")
    ev = pl.op("dv", lambda e: e.drain(), waits=[ev], inc="Sdv")
    evt = pl.op("pe", lambda e: e.transpose(pb[0][0:1, 0:HP], A_row[:], idf[0:HP, 0:HP]),
                waits=[ev], inc="Spe")
    ev = pl.op("dv", lambda e: e.tensor_copy(A_rowT[:], pb[0][0:1, 0:HP]),
               waits=[evt], inc="Sdv")
    evt = pl.op("pe", lambda e: e.matmul(pb[0][:, 128:128 + HP], C["ones_row"][:],
                                         A_rowT[:], start=True, stop=True),
                waits=[ev], inc="Spe")
    ev_abc = pl.op("dv", lambda e: e.tensor_copy(A_bc[:], pb[0][:, 128:128 + HP]),
                   waits=[evt], inc="Sdv")
    PH0 = pl.barrier()
    replay(nc, pl, sems)

    pl = Plan(cnt)
    ph1 = contextlib.ExitStack()
    EP = ph1.enter_context
    xt = [EP(nc.sbuf_tensor(f"p1xt{i}", [128, D], F32)) for i in range(2)]
    scr = EP(nc.sbuf_tensor("p1scr", [128, D], F32))
    scr2 = EP(nc.sbuf_tensor("p1scr2", [128, D], F32))
    lncol = EP(nc.sbuf_tensor("p1col", [128, 8], F32))
    xlnb = EP(nc.sbuf_tensor("p1xlnb", [128, D], BF16))
    fmst = EP(nc.sbuf_tensor("p1fm", [128, 16 * TPC], BF16))

    def emit_ln_tile(pl, src_tile_ap_fn, out_bf, i, ev_ld, scr, scr2, lncol,
                     out_free_ev=None):
        ev = pl.op("dv", lambda e: e.tensor_reduce(lncol[:, 0:1], src_tile_ap_fn(), AXX, ALU.add),
                   waits=[ev_ld], inc="Sdv")
        ev = pl.op("dv", lambda e: (e.drain(), e.tensor_scalar(
            lncol[:, 1:2], lncol[:, 0:1], 1.0 / D, None, ALU.mult))[1],
            waits=[ev], inc="Sdv")
        ev = pl.op("dv", lambda e: (e.drain(), e.tensor_scalar(
            scr[:], src_tile_ap_fn(), lncol[:, 1:2], None, ALU.subtract))[1],
            waits=[ev], inc="Sdv")
        ev = pl.op("sc", lambda e: e.activation(scr2[:], scr[:], AF.Square,
                                                accum_out=lncol[:, 2:3]),
                   waits=[ev], inc="Ssc")
        ev = pl.op("dv", lambda e: e.tensor_scalar(lncol[:, 3:4], lncol[:, 2:3],
                                                   1.0 / D, EPS_LN, ALU.mult, ALU.add),
                   waits=[ev], inc="Sdv")
        ev = pl.op("sc", lambda e: e.activation(lncol[:, 4:5], lncol[:, 3:4], AF.Ln),
                   waits=[ev], inc="Ssc")
        ev = pl.op("sc", lambda e: (e.drain(), e.activation(
            lncol[:, 5:6], lncol[:, 4:5], AF.Exp, scale=-0.5))[1],
            waits=[ev], inc="Ssc")
        ofe = out_free_ev if isinstance(out_free_ev, list) else [out_free_ev]
        ev = pl.op("dv", lambda e: e.tensor_scalar(out_bf[:], scr[:], lncol[:, 5:6],
                                                   None, ALU.mult),
                   waits=[ev] + list(ofe), inc="Sdv")
        return ev

    ev_xt_free = [None, None]
    ev_xlnb_free = None
    evT_slot = [None, None]
    last_evac = None
    for i in range(NTT):
        ev_ld = pl.op("sy", lambda e, i=i: e.dma_start(
            out=xt[i % 2][:], in_=din["x_slice"][i * 128:(i + 1) * 128, :]),
            waits=[PH0[0], ev_xt_free[i % 2]], inc="Sx0" if i % 2 == 0 else "Sx1", amt=16)
        ev_bf = emit_ln_tile(pl, (lambda i=i: xt[i % 2][:]), xlnb, i, ev_ld,
                             scr, scr2, lncol, out_free_ev=ev_xlnb_free)
        ev_xt_free[i % 2] = ev_bf
        for kc in range(16):
            bank = kc % 2
            slot = bank * 8 + (kc // 2) % 8
            evt = pl.op("pe", lambda e, kc=kc, slot=slot: e.transpose(
                pbT[:, slot * 128:(slot + 1) * 128],
                xlnb[:, kc * 128:(kc + 1) * 128], idb[:]),
                waits=[ev_bf, evT_slot[bank]], inc="Spe")
            evv = pl.op("dv", lambda e, kc=kc, i=i, slot=slot: e.tensor_copy(
                fmst[:, kc * TPC + i * 128: kc * TPC + (i + 1) * 128],
                pbT[:, slot * 128:(slot + 1) * 128]),
                waits=[evt], inc="Sdv")
            evT_slot[bank] = evv
            last_evac = evv
        ev_xlnb_free = evt

    ev_ag1h = [None, None]
    for hf in range(2):
        ev_fmdma = None
        for kc in range(16):
            ev_fmdma = pl.op("sy", lambda e, kc=kc, hf=hf: e.dma_start(
                out=xln_c[hf, kc * 128:(kc + 1) * 128, :],
                in_=fmst[:, kc * TPC + hf * 512: kc * TPC + hf * 512 + 512]),
                waits=[last_evac], inc="Sxc", amt=16)
        ev_ag1h[hf] = pl.op("gp", lambda e, hf=hf: e.collective_compute(
            "AllGather", ALU.bypass, rg,
            ins=[xln_c.ap()[hf].opt()], outs=[xln_g.ap()[hf].opt()]),
            waits=[ev_fmdma], inc="Scc")
    ev_ag1 = ev_ag1h[1]
    PH1 = pl.barrier()
    replay(nc, pl, sems)
    ph1.close()

    if "xln_g" in debug:
        pl = Plan(cnt)
        ph = contextlib.ExitStack()
        t = ph.enter_context(nc.sbuf_tensor("dbgt", [128, 4096], BF16))
        dbg_outs["xln_g"] = nc.dram_tensor("dbg_xln_g", [128, 4096], BF16,
                                           kind="ExternalOutput").ap()
        ev = pl.op("sy", lambda e: e.dma_start(out=t[:, 0:512],
                                               in_=xln_g[0, 0:128, :]),
                   waits=[ev_ag1], inc="Ssy", amt=16)
        ev = pl.op("sy", lambda e: e.dma_start(
            out=t[:, 2048:2560], in_=xln_g[0, 2048 * 3 + 128:2048 * 3 + 256, :]),
            waits=[ev_ag1], inc="Ssy", amt=16)
        ev = pl.op("sy", lambda e: e.dma_start(out=dbg_outs["xln_g"], in_=t[:]),
                   waits=[ev], inc="Ssy", amt=16)
        pl.op("sy", lambda e: None, waits=pl.barrier())
        replay(nc, pl, sems)
        ph.close()

    def finish_dummy():
        pl = Plan(cnt)
        ph = contextlib.ExitStack()
        z = ph.enter_context(nc.sbuf_tensor("zz", [128, D], F32))
        ev = pl.op("dv", lambda e: e.memset(z[:], 0.0), waits=pl.barrier(), inc="Sdv")
        evs = []
        for i in range(NTT):
            evs.append(pl.op("sy", lambda e, i=i: e.dma_start(
                out=d_out[i * 128:(i + 1) * 128, :], in_=z[:]),
                waits=[ev], inc="Sou", amt=16))
        pl.op("sy", lambda e: None, waits=[evs[-1]])
        replay(nc, pl, sems)
        ph.close()
        outer.close()
        return nc, dbg_outs

    if upto <= 1:
        return finish_dummy()

    nchunks = min(nchunks_attn, NCH)
    ph2 = contextlib.ExitStack()
    EP = ph2.enter_context
    aW = EP(nc.sbuf_tensor("aW", [128, 16 * PASSW], BF16))
    lhsA = EP(nc.sbuf_tensor("lhsA", [128, 2 * 16 * 256], BF16))
    lhsB = EP(nc.sbuf_tensor("lhsB", [128, 2 * 16 * 256], BF16))
    smalls = EP(nc.sbuf_tensor("smalls", [128, GRP * 48], F32))
    rb4 = EP(nc.sbuf_tensor("rb4", [128, 512], F32))
    acdtJs = EP(nc.sbuf_tensor("acdtJs", [1, 2 * 512], F32))
    thred = EP(nc.sbuf_tensor("thred", [128, 8 * 256], F32))
    csb = EP(nc.sbuf_tensor("csb", [128, GRP * 512], BF16))
    offrow = EP(nc.sbuf_tensor("offrow", [1, 256], F32))
    albr = EP(nc.sbuf_tensor("albr", [1, 32], F32))
    sw = EP(nc.sbuf_tensor("sw", [128, 9 * 32], F32))
    thx = EP(nc.sbuf_tensor("thx", [128, 8 * 264], F32))
    dtc = EP(nc.sbuf_tensor("dtc", [128, 32], F32))
    lamc = EP(nc.sbuf_tensor("lamc", [128, 32], F32))
    w0 = EP(nc.sbuf_tensor("w0", [128, 256], F32))
    tb0 = EP(nc.sbuf_tensor("tb0", [128, 256], F32))
    gw0 = EP(nc.sbuf_tensor("gw0", [128, 256], F32))
    gw1 = EP(nc.sbuf_tensor("gw1", [128, 256], F32))
    mixf = EP(nc.sbuf_tensor("mixf", [128, 2 * 1024], F32))
    scrB = EP(nc.sbuf_tensor("scrB", [128, 1024], F32))
    msq = EP(nc.sbuf_tensor("msq", [128, 32], F32))
    mixb = EP(nc.sbuf_tensor("mixb", [128, 2 * 1024], BF16))
    xhb = EP(nc.sbuf_tensor("xhb", [128, 2 * 256], BF16))
    zb = EP(nc.sbuf_tensor("zb", [128, 2 * 256], F32))
    rt = [EP(nc.sbuf_tensor(f"rt{i}", [128, 512], F32)) for i in range(4)]
    ex1 = rt[0]
    ex2 = rt[1]
    Bfm = EP(nc.sbuf_tensor("Bfm", [128, 2 * HPP * 128], BF16))
    Cfm = EP(nc.sbuf_tensor("Cfm", [128, HPP * 128], BF16))
    mcomb = EP(nc.sbuf_tensor("mcomb", [128, HPP * 128], BF16))
    lhsTw = EP(nc.sbuf_tensor("lhsTw", [128, HPP * 128], BF16))
    bndw = EP(nc.sbuf_tensor("bndw", [1, HPP * 128], BF16))
    bexp = EP(nc.sbuf_tensor("bexp", [1, HPP * 128], F32))
    w2b1 = EP(nc.sbuf_tensor("w2b1", [1, 4], F32))
    hbrow = EP(nc.sbuf_tensor("hbrow", [1, HPP * 128], BF16))
    hst = EP(nc.sbuf_tensor("hst", [128, 2 * HPP * P], BF16))
    y_tm = EP(nc.sbuf_tensor("y_tm", [128, HPP * P], F32))
    ygb = EP(nc.sbuf_tensor("ygb", [128, HPP * P], BF16))
    yfmg = EP(nc.sbuf_tensor("yfmg", [128, 2 * GRP * 128], BF16))
    bndst = EP(nc.sbuf_tensor("bndst", [1, 2 * 768], BF16))
    if nchunks < NCH:
        plz = Plan(cnt)
        zc = mixb
        evz0 = plz.op("dv", lambda e: e.memset(mixb[:, 0:2048], 0.0), inc="Sdv")
        evzd = None
        for rb in range(4):
            for cq in range(4):
                evzd = plz.op("sy", lambda e, rb=rb, cq=cq: e.dma_start(
                    out=y_c[cq, rb * 128:(rb + 1) * 128, :],
                    in_=mixb[:, 0:2048]), waits=[evz0], inc="Ssy", amt=16)
        plz.op("dv", lambda e: e.memset(mixb[:, 0:2048], 0.0), waits=[evzd], inc="Sdv")
        replay(nc, plz, sems)

    S_DT, S_ACDT, S_UG, S_UB, S_UBS, S_EA, S_AL, S_SE, S_W2, S_ADT = range(10)

    def scol(ci, q):
        return q * 64 + (ci % GRP) * 4

    evb_w = {}
    evb_r = {}
    evT_w = {}
    evT_r = {}
    BANK_T0, BANK_T1 = "TA", "TB"

    def pe_bank(pl, bank, fn, extra=(), start_new=True):
        ev = pl.op("pe", fn, waits=[evb_r.get(bank)] + list(extra) if start_new
                   else list(extra), inc="Spe")
        evb_w[bank] = ev
        return ev

    def rd_bank(pl, eng, bank, fn, extra=(), sem=None):
        ev = pl.op(eng, fn, waits=[evb_w.get(bank)] + list(extra),
                   inc=sem or ("Sdv" if eng == "dv" else "Ssc"))
        evb_r[bank] = ev
        return ev

    W_DT = [None]

    ev_aw = None
    PASS_HEADS = HPP

    def emit_pass(p, nch, ev_aw_free, ev_yfmg_free0=None, gate=None, p0_flush=None):
        flush_evs = []
        ag_evs = []
        ev_aw = None
        ev_lhsA_free = [None, None]
        ev_lhsB_free = [None, None]
        ev_lhsA_ld = {}
        ev_lhsB_ld = {}
        ev_off_w = None
        ev_smalls = {}
        ev_aarg = {}
        ev_cs = {}
        ev_mixb_rd = [[], []]
        ev_mixf_rd = [[], []]
        ev_xh_rd = [[], []]
        ev_z_rd = [None, None]
        ev_Bfm_rd = [[], []]
        ev_Cfm_rd = [[] for _ in range(HPP)]
        ev_h_w = {}
        ev_h_rd = {}
        ev_bnd_w = [None, None]
        ev_bnd_rd = [[], []]
        ev_gw_rd = [None]
        ev_thx_w = {}
        ev_thx_free = {}
        ev_ths_j = {}
        ev_dlc_free = []
        ev_q_pe_rd = [None]
        ev_ex_rd = [None]
        ev_bexp_rd = [None]
        ev_hbrow_rd = [None]
        ev_pb2bnd_rd = [None]
        ev_lhsTw_rd = [None]
        ev_mcomb_rd = [None]
        ev_yfmg_free = ev_yfmg_free0
        ev_yfm_last = None
        ev_proj_last = None
        ev_slot_free = {}
        ev_thred_rd = {}
        ev_csb_rd = {}
        ev_tsj_rd = None
        ev_r4_rd = [None]
        ev_aJ_rd = {}
        ev_scrB_rd = [None]
        evTb_r = [None, None]
        tcounter = [0, 0]

        def tslot(bank):
            if bank == 0:
                s = tcounter[0] % 8
            else:
                s = 8 + tcounter[1] % 2
            tcounter[bank] += 1
            return s

        def do_transpose(pl, src_ap_fn, waits):
            bank = (tcounter[0] + tcounter[1]) % 2
            s = tslot(bank)
            ev = pl.op("pe", lambda e, s=s: e.transpose(
                pbT[:, s * 128:(s + 1) * 128], src_ap_fn(), idb[:]),
                waits=list(waits) + [evTb_r[bank]], inc="Spe")
            return s, ev, bank

        def evac_T(pl, s, bank, ev, out_ap_fn, extra=()):
            evv = pl.op("dv", lambda e, s=s: e.tensor_copy(
                out_ap_fn(), pbT[:, s * 128:(s + 1) * 128]),
                waits=[ev] + list(extra), inc="Sdv")
            evTb_r[bank] = evv
            return evv

        def evac_T_sc(pl, s, bank, ev, out_ap_fn, extra=()):
            evv = pl.op("sc", lambda e, s=s: e.activation(
                out_ap_fn(), pbT[:, s * 128:(s + 1) * 128], AF.Copy),
                waits=[ev] + list(extra), inc="Ssc")
            evTb_r[bank] = evv
            return evv

        def stA(pl, ci):
            nonlocal ev_proj_last
            j = ci % GRP
            g2 = (ci // 2) % 2
            if ci % 2 == 0:
                c0 = (ci * TC) // TPC
                tcol = (ci * TC) % TPC
                ev_lhsA_ld[ci // 2] = pl.op("sy", lambda e, c0=c0, tcol=tcol, g2=g2: e.dma_start(
                    out=lhsA[:, g2 * 4096:(g2 + 1) * 4096].rearrange("p (kc t) -> p kc t", kc=16),
                    in_=xln_g[tcol // 512, c0 * D:(c0 + 1) * D,
                              tcol % 512:tcol % 512 + 256].rearrange(
                        "(kc q) t -> q kc t", q=128)),
                    waits=[ev_ag1h[tcol // 512], ev_lhsA_free[g2]],
                    inc="SlA0" if g2 == 0 else "SlA1", amt=16)
            ev_ld = ev_lhsA_ld[ci // 2]
            sub = ci % 2
            evp = None
            for kc in range(16):
                evp = pe_bank(pl, 2, lambda e, kc=kc, g2=g2, sub=sub: e.matmul(
                    pb[2][:, 0:264],
                    lhsA[:, g2 * 4096 + kc * 256 + sub * 128: g2 * 4096 + kc * 256 + (sub + 1) * 128],
                    aW[:, kc * PASSW + CT: kc * PASSW + CT + 264],
                    start=(kc == 0), stop=(kc == 15)),
                    extra=[ev_ld, ev_aw], start_new=(kc == 0))
            ev_lhsA_free[g2] = evp
            ev_proj_last = evp
            js = j % 8
            ev_thx_w[j] = rd_bank(pl, "dv", 2, lambda e, js=js: e.tensor_copy(
                thx[:, js * 264:(js + 1) * 264], pb[2][:, 0:264]),
                extra=[ev_thx_free.get(js)])

        def smalls_batch(pl, j0, j1):
            n = j1 - j0
            X = n * 4
            q = lambda k: smalls[:, k * 64 + j0 * 4: k * 64 + j1 * 4]
            sws = lambda k: sw[:, k * 32: k * 32 + X]
            thv = thx[:].rearrange("p (j c) -> p j c", j=8)
            evw = [ev_thx_w[j] for j in range(j0, j1)]
            gate = [ev_slot_free.get(j) for j in range(j0, j1)]
            Ab = A_bc[:, p * HPP:(p + 1) * HPP]
            Ab3 = Ab.unsqueeze(1).broadcast_to([128, n, HPP])
            j3 = lambda ap: ap.rearrange("p (j h) -> p j h", j=n)
            ev = pl.op("dv", lambda e: e.tensor_copy(
                j3(dtc[:, 0:X]), thv[:, 0:n, 256:260]),
                waits=evw + list(ev_dlc_free), inc="Sdv")
            ev = pl.op("dv", lambda e: (e.drain(), e.tensor_copy(
                j3(lamc[:, 0:X]), thv[:, 0:n, 260:264]))[1], waits=[ev], inc="Sdv")
            ev = pl.op("sc", lambda e: e.activation(sws(0), dtc[:, 0:X], AF.Abs), waits=[ev], inc="Ssc")
            ev = pl.op("sc", lambda e: (e.drain(), e.activation(sws(1), sws(0), AF.Exp, scale=-1.0))[1], waits=[ev], inc="Ssc")
            ev = pl.op("sc", lambda e: (e.drain(), e.activation(sws(2), sws(1), AF.Ln, bias=1.0))[1], waits=[ev], inc="Ssc")
            ev = pl.op("dv", lambda e: e.tensor_scalar(sws(3), dtc[:, 0:X], 0.0, None, ALU.max), waits=[ev], inc="Sdv")
            ev_dt = pl.op("dv", lambda e: (e.drain(), e.tensor_tensor(q(S_DT), sws(3), sws(2), ALU.add))[1],
                          waits=[ev] + gate + [ev_q_pe_rd[0]], inc="Sdv")
            ev_ths = pl.op("dv", lambda e: (e.drain(), e.tensor_tensor(
                thv[:, 0:n, 0:256].rearrange("p j (h i) -> p j h i", h=HPP),
                thv[:, 0:n, 0:256].rearrange("p j (h i) -> p j h i", h=HPP),
                j3(q(S_DT)).unsqueeze(3).broadcast_to([128, n, HPP, 64]), ALU.mult))[1],
                waits=[ev_dt], inc="Sdv")
            for j in range(j0, j1):
                ev_ths_j[j] = ev_ths
            ev_labs = pl.op("sc", lambda e: e.activation(sws(0), lamc[:, 0:X], AF.Abs), waits=[ev_dt], inc="Ssc")
            ev = pl.op("sc", lambda e: (e.drain(), e.activation(sws(1), sws(0), AF.Exp, scale=-1.0))[1], waits=[ev_labs], inc="Ssc")
            ev = pl.op("sc", lambda e: (e.drain(), e.activation(sws(2), sws(1), AF.Ln, bias=1.0))[1], waits=[ev], inc="Ssc")
            ev = pl.op("dv", lambda e: (e.drain(), e.tensor_scalar(sws(3), lamc[:, 0:X], 0.0, None, ALU.max))[1], waits=[ev], inc="Sdv")
            ev = pl.op("dv", lambda e: (e.drain(), e.tensor_tensor(sws(4), sws(3), sws(2), ALU.add))[1], waits=[ev], inc="Sdv")
            ev = pl.op("dv", lambda e: (e.drain(), e.tensor_scalar(sws(5), lamc[:, 0:X], -1.0, 0.0, ALU.mult, ALU.max))[1], waits=[ev], inc="Sdv")
            ev_dlc = pl.op("dv", lambda e: (e.drain(), e.tensor_tensor(sws(6), sws(5), sws(2), ALU.add))[1], waits=[ev], inc="Sdv")
            ev_dlc_free[:] = [ev_dlc, ev_labs]
            ev = pl.op("sc", lambda e: e.activation(sws(0), q(S_DT), AF.Ln), waits=[ev_dlc, ev_dt], inc="Ssc")
            ev_adt = pl.op("dv", lambda e: e.tensor_tensor(j3(q(S_ADT)), j3(q(S_DT)), Ab3, ALU.mult),
                           waits=[ev_dt, ev_abc], inc="Sdv")
            ev = pl.op("dv", lambda e: (e.drain(), e.tensor_tensor(sws(2), sws(0), sws(6), ALU.subtract))[1], waits=[ev, ev_adt], inc="Sdv")
            ev = pl.op("dv", lambda e: (e.drain(), e.tensor_tensor(sws(3), sws(0), sws(4), ALU.subtract))[1], waits=[ev], inc="Sdv")
            ev_lgb = pl.op("dv", lambda e: (e.drain(), e.tensor_tensor(sws(4), sws(3), q(S_ADT), ALU.add))[1], waits=[ev], inc="Sdv")
            ev_cdt = pe_bank(pl, 2, lambda e: e.matmul(pb[2][:, 264:264 + X], C["L1incl"][:], q(S_DT), start=True, stop=True), extra=[ev_dt])
            ev_acdt = rd_bank(pl, "dv", 2, lambda e: e.tensor_tensor(j3(q(S_ACDT)), j3(pb[2][:, 264:264 + X]), Ab3, ALU.mult))
            ev_ug = pl.op("dv", lambda e: (e.drain(), e.tensor_tensor(q(S_UG), sws(2), q(S_ACDT), ALU.subtract))[1], waits=[ev_lgb, ev_acdt], inc="Sdv")
            ev_ub = pl.op("dv", lambda e: (e.drain(), e.tensor_tensor(q(S_UB), sws(4), q(S_ACDT), ALU.subtract))[1], waits=[ev_ug], inc="Sdv")
            ev_sh = pe_bank(pl, 2, lambda e: e.matmul(pb[2][:, 296:296 + X], C["Ushift"][:], q(S_UB), start=True, stop=True), extra=[ev_ub])
            ev_ubs = rd_bank(pl, "dv", 2, lambda e: e.tensor_tensor(
                q(S_UBS), pb[2][:, 296:296 + X], C["M127col"][:, 0:1].broadcast_to([128, X]), ALU.add))
            ev_ea = pl.op("sc", lambda e: e.activation(q(S_EA), q(S_ACDT), AF.Exp),
                          waits=[ev_acdt] + gate, inc="Ssc")
            ev = pe_bank(pl, 2, lambda e: e.matmul(pb[2][0:1, 328:328 + X], C["onehot127"][:], q(S_ACDT), start=True, stop=True), extra=[ev_acdt])
            ev = rd_bank(pl, "dv", 2, lambda e: e.tensor_copy(albr[:, 0:X], pb[2][0:1, 328:328 + X]))
            ev = pe_bank(pl, 2, lambda e: e.matmul(pb[2][:, 360:360 + X], C["ones_row"][:], albr[:, 0:X], start=True, stop=True), extra=[ev])
            ev_q_pe_rd[0] = ev
            ev_al = rd_bank(pl, "dv", 2, lambda e: e.tensor_copy(q(S_AL), pb[2][:, 360:360 + X]))
            ev_se = pl.op("sc", lambda e: e.activation(q(S_SE), q(S_AL), AF.Exp),
                          waits=[ev_al] + gate, inc="Ssc")
            ev = pl.op("dv", lambda e: e.tensor_tensor(sws(5), q(S_UG), q(S_AL), ALU.add), waits=[ev_ug, ev_al], inc="Sdv")
            ev = pl.op("sc", lambda e: (e.drain(), e.activation(sws(6), sws(5), AF.Exp))[1], waits=[ev], inc="Ssc")
            ev2 = pl.op("dv", lambda e: e.tensor_tensor(sws(7), q(S_UBS), q(S_AL), ALU.add), waits=[ev_ubs, ev_al], inc="Sdv")
            ev2 = pl.op("sc", lambda e: (e.drain(), e.activation(sws(8), sws(7), AF.Exp))[1], waits=[ev2], inc="Ssc")
            ev_w2 = pl.op("dv", lambda e: e.tensor_tensor(q(S_W2), sws(6), sws(8), ALU.add), waits=[ev, ev2], inc="Sdv")
            evq = dict(w2=ev_w2, se=ev_se, ea=ev_ea, ubs=ev_ubs,
                       ug=ev_ug, ub=ev_ub, al=ev_al)
            for j in range(j0, j1):
                ev_smalls[j] = evq

        def ang_chunk(pl, ci):
            nonlocal ev_off_w
            j = ci % GRP
            js = j % 8
            tv = thx[:, js * 264: js * 264 + 256]
            ev_ths = ev_ths_j[j]
            ev = pe_bank(pl, 7, lambda e, tv=tv: e.matmul(pb7[:, 0:256], C["L1incl"][:], tv, start=True, stop=False), extra=[ev_ths])
            ev_ang = pe_bank(pl, 7, lambda e: e.matmul(pb7[:, 0:256], C["ones_row"][:], offrow[:], start=False, stop=True), extra=[ev_off_w], start_new=False)
            ev_osum = pe_bank(pl, 7, lambda e, tv=tv: e.matmul(
                pb7[0:1, 256:512], C["ones_col"][:], tv, start=True, stop=True),
                extra=[ev_ths], start_new=False)
            ev_thx_free[js] = ev_osum
            ev_off_w = rd_bank(pl, "dv", 7, lambda e: e.tensor_tensor(
                offrow[:], offrow[:], pb7[0:1, 256:512], ALU.add), extra=[ev_ang, ev_off_w])
            cbase = js * 256
            ev = rd_bank(pl, "dv", 7, lambda e: e.tensor_scalar(w0[:], pb7[:, 0:256], INV_2PI, MAGIC, ALU.mult, ALU.add))
            ev = pl.op("dv", lambda e: (e.drain(), e.tensor_scalar(w0[:], w0[:], MAGIC, None, ALU.subtract))[1], waits=[ev], inc="Sdv")
            ev_sa = rd_bank(pl, "dv", 7, lambda e, cbase=cbase: e.scalar_tensor_tensor(
                thred[:, cbase:cbase + 256], w0[:], -TWO_PI, pb7[:, 0:256], ALU.mult, ALU.add),
                extra=[ev, ev_thred_rd.get(js)])
            ev_sa = pl.op("dv", lambda e, cbase=cbase: (e.drain(), e.tensor_scalar(
                thred[:, cbase:cbase + 256], thred[:, cbase:cbase + 256],
                float(np.pi), -float(np.pi), ALU.min, ALU.max))[1],
                waits=[ev_sa], inc="Sdv")
            ev_aarg[j] = ev_sa

        def trig_half(pl, j0, j1):
            for j in range(j0, j1):
                sa = ev_aarg[j]
                js = j % 8
                ev_s = pl.op("sc", lambda e, j=j, js=js: e.activation(
                    csb[:, j * 512:j * 512 + 256], thred[:, js * 256:(js + 1) * 256], AF.Sin),
                    waits=[sa, ev_csb_rd.get(j)], inc="Ssc")
                ev = pl.op("sc", lambda e, js=js: (e.drain(), e.activation(
                    tb0[:], thred[:, js * 256:(js + 1) * 256], AF.Abs))[1],
                    waits=[sa], inc="Ssc")
                ev_c = pl.op("sc", lambda e, j=j: (e.drain(), e.activation(
                    csb[:, j * 512 + 256:j * 512 + 512], tb0[:], AF.Sin,
                    scale=-1.0, bias=C["halfpi"][:]))[1],
                    waits=[ev], inc="Ssc")
                ev_cs[j] = (ev_s, ev_c)
                ev_thred_rd[js] = ev_c

        def stA_tail(pl, cin, nxt_lo, nxt_hi):
            li = cin - nxt_lo
            n = nxt_hi - nxt_lo
            h1 = min(8, n)
            if li == h1 - 1:
                smalls_batch(pl, 0, h1)
                for cj in range(nxt_lo, nxt_lo + h1):
                    ang_chunk(pl, cj)
                trig_half(pl, 0, h1)
            if n > h1 and li == n - 1:
                smalls_batch(pl, 8, n)
                for cj in range(nxt_lo + 8, nxt_hi):
                    ang_chunk(pl, cj)
                trig_half(pl, 8, n)

        def stB_proj(pl, ci):
            nonlocal ev_proj_last
            par = ci % 2
            g2 = (ci // 2) % 2
            if ci % 2 == 0:
                c0 = (ci * TC) // TPC
                tcol = (ci * TC) % TPC
                ev_lhsB_ld[ci // 2] = pl.op("sy", lambda e, c0=c0, tcol=tcol, g2=g2: e.dma_start(
                    out=lhsB[:, g2 * 4096:(g2 + 1) * 4096].rearrange("p (kc t) -> p kc t", kc=16),
                    in_=xln_g[tcol // 512, c0 * D:(c0 + 1) * D,
                              tcol % 512:tcol % 512 + 256].rearrange(
                        "(kc q) t -> q kc t", q=128)),
                    waits=[ev_ag1h[tcol // 512], ev_lhsB_free[g2]],
                    inc="SlB0" if g2 == 0 else "SlB1", amt=16)
            ev_ld = ev_lhsB_ld[ci // 2]
            sub = ci % 2
            for bk2, coff in ((2, CX), (0, CB), (1, CC)):
                evx = None
                for kc in range(16):
                    evx = pe_bank(pl, bk2, lambda e, kc=kc, g2=g2, sub=sub, bk2=bk2, coff=coff: e.matmul(
                        pb[bk2][:, 0:512],
                        lhsB[:, g2 * 4096 + kc * 256 + sub * 128: g2 * 4096 + kc * 256 + (sub + 1) * 128],
                        aW[:, kc * PASSW + coff: kc * PASSW + coff + 512],
                        start=(kc == 0), stop=(kc == 15)),
                        extra=[ev_ld, ev_aw], start_new=(kc == 0))
            ev_lhsB_free[g2] = evx
            ev_proj_last = evx
            return par

        def stB_evac_xz(pl, ci):
            par = ci % 2
            evxh = rd_bank(pl, "dv", 2, lambda e, par=par: e.tensor_copy(
                xhb[:, par * 256:(par + 1) * 256], pb[2][:, 0:256]),
                extra=ev_xh_rd[par])
            evz = rd_bank(pl, "dv", 2, lambda e, par=par: e.tensor_copy(
                zb[:, par * 256:(par + 1) * 256], pb[2][:, 256:512]),
                extra=[ev_z_rd[par]])
            ev_xh_rd[par] = []
            return dict(evxh=evxh, evz=evz)

        def stB_evac_mix(pl, ci):
            par = ci % 2
            evB = rd_bank(pl, "sc", 0, lambda e, par=par: e.activation(
                mixf[:, par * 1024:par * 1024 + 512], pb[0][:, 0:512], AF.Copy),
                extra=ev_mixf_rd[par])
            evC = rd_bank(pl, "sc", 1, lambda e, par=par: e.activation(
                mixf[:, par * 1024 + 512:par * 1024 + 1024], pb[1][:, 0:512], AF.Copy))
            ev_mixf_rd[par] = []
            return dict(evB=evB, evC=evC)

        def stB_rest(pl, ci, dep):
            nonlocal ev_yfm_last
            par = ci % 2
            j = ci % GRP
            evB, evC, evxh, evz = dep["evB"], dep["evC"], dep["evxh"], dep["evz"]
            mo = par * 1024
            ev_sq = None
            for jj in range(8):
                ev_sq = pl.op("sc", lambda e, jj=jj, mo=mo: e.activation(
                    scrB[:, jj * 128:(jj + 1) * 128], mixf[:, mo + jj * 128: mo + (jj + 1) * 128],
                    AF.Square, accum_out=msq[:, jj:jj + 1]),
                    waits=[evB, evC, ev_scrB_rd[0]], inc="Ssc")
            ev_mixf_rd[par].append(ev_sq)
            ev = pl.op("dv", lambda e: e.tensor_scalar(msq[:, 8:16], msq[:, 0:8], 1.0 / N, EPS_RMS, ALU.mult, ALU.add), waits=[ev_sq], inc="Sdv")
            ev = pl.op("sc", lambda e: (e.drain(), e.activation(msq[:, 16:24], msq[:, 8:16], AF.Ln))[1], waits=[ev], inc="Ssc")
            ev = pl.op("sc", lambda e: (e.drain(), e.activation(msq[:, 24:32], msq[:, 16:24], AF.Exp, scale=-0.5))[1], waits=[ev], inc="Ssc")
            ev = pl.op("dv", lambda e, mo=mo: e.tensor_tensor(
                scrB[:].rearrange("p (jj i) -> p jj i", jj=8),
                mixf[:, mo:mo + 1024].rearrange("p (jj i) -> p jj i", jj=8),
                msq[:, 24:32].unsqueeze(2).broadcast_to([128, 8, 128]), ALU.mult), waits=[ev], inc="Sdv")
            ev_mixf_rd[par].append(ev)
            ev = pl.op("dv", lambda e: (e.drain(), e.tensor_tensor(scrB[:], scrB[:], C["BCnw"][:], ALU.mult))[1], waits=[ev], inc="Sdv")
            ev_pre = pl.op("dv", lambda e, p=p: (e.drain(), e.tensor_tensor(
                scrB[:], scrB[:], C["BCbias"][:, p * 1024:(p + 1) * 1024], ALU.add))[1], waits=[ev], inc="Sdv")
            evs_cs = ev_cs[j]
            x5 = lambda eo: scrB[:].rearrange("p (bc h eo i) -> p bc h eo i", bc=2, h=HPP, eo=2)[:, :, :, eo, :]
            cbc = lambda off: csb[:, j * 512 + off: j * 512 + off + 256].rearrange(
                "p (h i) -> p h i", h=HPP).unsqueeze(1).broadcast_to([128, 2, HPP, 64])
            m5 = lambda eo: mixb[:, par * 1024:(par + 1) * 1024].rearrange(
                "p (bc h eo i) -> p bc h eo i", bc=2, h=HPP, eo=2)[:, :, :, eo, :]
            r4v = lambda t: rt[t][:].rearrange("p (bc h i) -> p bc h i", bc=2, h=HPP)
            ev = pl.op("dv", lambda e: e.tensor_tensor(r4v(0), x5(0), cbc(256), ALU.mult), waits=[ev_pre, evs_cs[1]], inc="Sdv")
            ev = pl.op("dv", lambda e: e.tensor_tensor(r4v(1), x5(0), cbc(0), ALU.mult), waits=[ev, evs_cs[0]], inc="Sdv")
            ev = pl.op("dv", lambda e: e.tensor_tensor(r4v(2), x5(1), cbc(0), ALU.mult), waits=[ev], inc="Sdv")
            ev = pl.op("dv", lambda e: e.tensor_tensor(r4v(3), x5(1), cbc(256), ALU.mult), waits=[ev], inc="Sdv")
            ev = pl.op("dv", lambda e: (e.drain(), e.tensor_tensor(
                m5(0), r4v(0), r4v(2), ALU.subtract))[1],
                waits=[ev] + ev_mixb_rd[par], inc="Sdv")
            ev_mix = pl.op("dv", lambda e: (e.drain(), e.tensor_tensor(
                m5(1), r4v(1), r4v(3), ALU.add))[1], waits=[ev], inc="Sdv")
            ev_mixb_rd[par] = []
            ev_scrB_rd[0] = ev_mix
            ev_csb_rd[j] = ev_mix
            SDc = lambda off: smalls[:, scol(ci, off):scol(ci, off) + 4]
            sm = ev_smalls[j]
            ev_r4 = pl.op("dv", lambda e: e.tensor_tensor(
                rb4[:].rearrange("p (h t) -> p h t", h=HPP),
                SDc(S_ADT).unsqueeze(2).broadcast_to([128, HPP, 128]),
                C["L1incl"][:].unsqueeze(1).broadcast_to([128, HPP, 128]), ALU.mult),
                waits=[sm["ug"], ev_r4_rd[0]], inc="Sdv")
            evj = pe_bank(pl, 7, lambda e: e.matmul(pb7[0:1, 0:512], C["ones_col"][:], rb4[:], start=True, stop=True), extra=[ev_r4])
            ev_r4_rd[0] = evj
            ev_aJ = rd_bank(pl, "dv", 7, lambda e, par=par: e.tensor_copy(
                acdtJs[:, par * 512:(par + 1) * 512], pb7[0:1, 0:512]),
                extra=[ev_aJ_rd.get(par)])
            ev_bnd_reads = []
            ev_pb2bnd_rd_prev = ev_pb2bnd_rd[0]
            evB2s, evC2s, ev3s = [], [], []
            for h in range(HPP):
                sB, evtB, bB = do_transpose(pl, (lambda h=h, par=par: mixb[:, par * 1024 + h * 128: par * 1024 + (h + 1) * 128]), [ev_mix])
                ev_mixb_rd[par].append(evtB)
                evB2 = evac_T(pl, sB, bB, evtB, (lambda h=h, par=par: Bfm[:, (par * HPP + h) * 128:(par * HPP + h + 1) * 128]), extra=ev_Bfm_rd[par] if h == 0 else [])
                if h == 0:
                    ev_Bfm_rd[par] = []
                evB2s.append(evB2)
                sC, evtC, bC = do_transpose(pl, (lambda h=h, par=par: mixb[:, par * 1024 + 512 + h * 128: par * 1024 + 512 + (h + 1) * 128]), [ev_mix])
                ev_mixb_rd[par].append(evtC)
                evC2 = evac_T(pl, sC, bC, evtC, (lambda h=h: Cfm[:, h * 128:(h + 1) * 128]), extra=ev_Cfm_rd[h])
                ev_Cfm_rd[h] = []
                evC2s.append(evC2)
                ev3 = pe_bank(pl, 2, lambda e, h=h, par=par: e.matmul(
                    pb[2][0:1, h * 128:(h + 1) * 128],
                    Bfm[:, ((1 - par) * HPP + h) * 128 + 127: ((1 - par) * HPP + h) * 128 + 128],
                    Cfm[:, h * 128:(h + 1) * 128],
                    start=True, stop=True), extra=[evC2, ev_pb2bnd_rd_prev])
                ev_Bfm_rd[1 - par].append(ev3)
                ev_Cfm_rd[h].append(ev3)
                ev3s.append(ev3)
            ev1 = pe_bank(pl, 4, lambda e, par=par: e.matmul(
                pb[4][:, 0:512], C["ones_row"][:],
                acdtJs[0:1, par * 512:(par + 1) * 512],
                start=True, stop=True), extra=[ev_aJ])
            ev2l = None
            for h in range(HPP):
                ev2l = pe_bank(pl, 3, lambda e, h=h, par=par: e.matmul(
                    pb[3][:, h * 128:(h + 1) * 128],
                    Bfm[:, (par * HPP + h) * 128:(par * HPP + h + 1) * 128],
                    Cfm[:, h * 128:(h + 1) * 128],
                    start=True, stop=True), extra=[evB2s[h], evC2s[h]],
                    start_new=(h == 0))
                ev_Bfm_rd[par].append(ev2l)
                ev_Cfm_rd[h].append(ev2l)
            h4v = lambda t: t.rearrange("p (h i) -> p h i", h=HPP)
            col4 = lambda off: SDc(off).unsqueeze(2).broadcast_to([128, HPP, 128])
            mb4 = lambda nm: C[nm][:].unsqueeze(1).broadcast_to([128, HPP, 128])
            evd1 = rd_bank(pl, "dv", 4, lambda e: e.tensor_tensor(
                h4v(ex1[:]), h4v(pb[4][:, 0:512]), col4(S_UG), ALU.add),
                extra=[sm["ug"], ev1, ev_ex_rd[0]])
            evd1 = pl.op("dv", lambda e: (e.drain(), e.tensor_tensor(
                h4v(ex1[:]), h4v(ex1[:]), mb4("MBincl"), ALU.add))[1], waits=[evd1], inc="Sdv")
            eva1 = pl.op("sc", lambda e: e.activation(ex1[:], ex1[:], AF.Exp), waits=[evd1], inc="Ssc")
            evd2 = rd_bank(pl, "dv", 4, lambda e: e.tensor_tensor(
                h4v(ex2[:]), h4v(pb[4][:, 0:512]), col4(S_UBS), ALU.add),
                extra=[sm["ubs"], ev_ex_rd[0]])
            evd2 = pl.op("dv", lambda e: (e.drain(), e.tensor_tensor(
                h4v(ex2[:]), h4v(ex2[:]), mb4("MBstrict"), ALU.add))[1], waits=[evd2], inc="Sdv")
            eva2 = pl.op("sc", lambda e: (e.drain(), e.activation(ex2[:], ex2[:], AF.Exp))[1], waits=[evd2], inc="Ssc")
            evw = pl.op("dv", lambda e: e.tensor_tensor(ex1[:], ex1[:], ex2[:], ALU.add), waits=[eva1, eva2], inc="Sdv")
            evmc = rd_bank(pl, "dv", 3, lambda e: e.tensor_tensor(
                mcomb[:], ex1[:], pb[3][:, 0:512], ALU.mult),
                extra=[evw, ev2l, ev_mcomb_rd[0]])
            ev_ex_rd[0] = evmc
            evbx_dv = pl.op("dv", lambda e, par=par: e.tensor_tensor(
                bexp[:].rearrange("o (h t) -> o h t", h=HPP),
                acdtJs[0:1, par * 512:(par + 1) * 512].rearrange("o (h t) -> o h t", h=HPP),
                SDc(S_UB)[0:1, :].unsqueeze(2).broadcast_to([1, HPP, 128]), ALU.add),
                waits=[ev_aJ, sm["ub"], ev_bexp_rd[0]], inc="Sdv")
            evbx = pl.op("sc", lambda e: e.activation(bexp[:], bexp[:], AF.Exp), waits=[evbx_dv], inc="Ssc")
            evbw = rd_bank(pl, "dv", 2, lambda e: e.tensor_tensor(
                bndw[:], bexp[:], pb[2][0:1, 0:512], ALU.mult),
                extra=[evbx] + ev3s)
            ev_pb2bnd_rd[0] = evbw
            evlw = pl.op("dv", lambda e, par=par: e.tensor_tensor(
                h4v(lhsTw[:]), mixb[:, par * 1024:par * 1024 + 512].rearrange("p (h i) -> p h i", h=HPP),
                col4(S_W2), ALU.mult),
                waits=[ev_mix, sm["w2"], ev_lhsTw_rd[0]], inc="Sdv")
            ev_mixb_rd[par].append(evlw)
            evw2b = pl.op("dv", lambda e: e.tensor_tensor(
                w2b1[:], SDc(S_UB)[0:1, :], SDc(S_AL)[0:1, :], ALU.add),
                waits=[sm["ub"], sm["al"]], inc="Sdv")
            evw2b = pl.op("sc", lambda e: (e.drain(), e.activation(w2b1[:], w2b1[:], AF.Exp))[1], waits=[evw2b], inc="Ssc")
            evhb = pl.op("dv", lambda e, par=par: e.tensor_tensor(
                hbrow[:].rearrange("o (h t) -> o h t", h=HPP),
                bndst[0:1, par * 768:par * 768 + 512].rearrange("o (h t) -> o h t", h=HPP),
                w2b1[0:1, :].unsqueeze(2).broadcast_to([1, HPP, 128]), ALU.mult),
                waits=[evw2b, ev_bnd_w[par], ev_hbrow_rd[0]], inc="Sdv")
            ev_bnd_reads.append(evhb)
            evy = None
            evh = None
            for h in range(HPP):
                evy = pe_bank(pl, 3, lambda e, h=h, par=par: e.matmul(
                    pb[3][:, h * 64:(h + 1) * 64], mcomb[:, h * 128:(h + 1) * 128],
                    xhb[:, par * 256 + h * 64: par * 256 + (h + 1) * 64],
                    start=True, stop=False), extra=[evmc, evxh],
                    start_new=(h == 0))
                evy = pe_bank(pl, 3, lambda e, h=h, par=par: e.matmul(
                    pb[3][:, h * 64:(h + 1) * 64], bndw[0:1, h * 128:(h + 1) * 128],
                    bndst[0:1, par * 768 + 512 + h * 64:par * 768 + 512 + (h + 1) * 64],
                    start=False, stop=True), extra=[evbw, ev_bnd_w[par]], start_new=False)
                ev_bnd_reads.append(evy)
                evh = pe_bank(pl, 3, lambda e, h=h, par=par: e.matmul(
                    pb[3][:, 256 + h * 64: 256 + (h + 1) * 64], lhsTw[:, h * 128:(h + 1) * 128],
                    xhb[:, par * 256 + h * 64: par * 256 + (h + 1) * 64],
                    start=True, stop=False), extra=[evlw], start_new=False)
                evh = pe_bank(pl, 3, lambda e, h=h, par=par: e.matmul(
                    pb[3][:, 256 + h * 64: 256 + (h + 1) * 64], hbrow[0:1, h * 128:(h + 1) * 128],
                    bndst[0:1, par * 768 + 512 + h * 64:par * 768 + 512 + (h + 1) * 64],
                    start=False, stop=True), extra=[evhb], start_new=False)
                ev_bnd_reads.append(evh)
            ev_mcomb_rd[0] = evy
            ev_lhsTw_rd[0] = evh
            ev_xh_rd[par].extend([evy, evh])
            ev_bexp_rd[0] = evy
            ev_hbrow_rd[0] = evh
            evy2 = None
            for h in range(HPP):
                evy2 = pe_bank(pl, 4, lambda e, h=h, par=par: e.matmul(
                    pb[4][:, h * 64:(h + 1) * 64], Cfm[:, h * 128:(h + 1) * 128],
                    hst[:, par * 256 + h * P:par * 256 + (h + 1) * P],
                    start=True, stop=True), extra=[evC2s[h], ev_h_w.get(par)],
                    start_new=(h == 0))
                ev_Cfm_rd[h].append(evy2)
            ev_h_rd.setdefault(par, []).append(evy2)
            ysc4 = scrB[:, 0:256]
            hupd = scrB[:, 256:512]
            evye = rd_bank(pl, "dv", 4, lambda e: e.tensor_tensor(
                ysc4.rearrange("p (h i) -> p h i", h=HPP),
                pb[4][:, 0:256].rearrange("p (h i) -> p h i", h=HPP),
                SDc(S_EA).unsqueeze(2).broadcast_to([128, HPP, 64]), ALU.mult),
                extra=[evy2, sm["ea"]])
            evye2 = rd_bank(pl, "dv", 3, lambda e: e.tensor_tensor(
                y_tm[:], ysc4, pb[3][:, 0:256], ALU.add),
                extra=[evye, evy])
            ev_y_evac = [evye2]
            evhu = pl.op("dv", lambda e, par=par: e.tensor_tensor(
                hupd.rearrange("p (h i) -> p h i", h=HPP),
                hst[:, par * 256:(par + 1) * 256].rearrange("p (h i) -> p h i", h=HPP),
                SDc(S_SE).unsqueeze(2).broadcast_to([128, HPP, P]), ALU.mult),
                waits=[sm["se"], ev_h_w.get(par), evye2], inc="Sdv")
            evhe = rd_bank(pl, "dv", 3, lambda e, par=par: e.tensor_tensor(
                hst[:, (1 - par) * 256:(2 - par) * 256], hupd, pb[3][:, 256:512], ALU.add),
                extra=[evhu, evh, ev_h_w.get(1 - par)] + ev_h_rd.get(1 - par, []))
            ev_h_w[1 - par] = evhe
            ev_h_rd[1 - par] = []
            ev_h_rd.setdefault(par, []).append(evhu)
            ev_scrB_rd[0] = evhe
            ev_aJ_rd[par] = ev1
            evsc_g = pl.op("sc", lambda e, par=par: e.activation(gw0[:], zb[:, par * 256:(par + 1) * 256], AF.Exp, scale=-1.0), waits=[evz, ev_gw_rd[0]], inc="Ssc")
            ev = pl.op("dv", lambda e: e.tensor_scalar(gw0[:], gw0[:], 1.0, None, ALU.add), waits=[evsc_g], inc="Sdv")
            ev = pl.op("dv", lambda e: (e.drain(), e.reciprocal(gw1[:], gw0[:]))[1], waits=[ev], inc="Sdv")
            ev = pl.op("dv", lambda e, par=par: (e.drain(), e.tensor_tensor(gw0[:], zb[:, par * 256:(par + 1) * 256], gw1[:], ALU.mult))[1], waits=[ev], inc="Sdv")
            ev_z_rd[par] = ev
            evg = pl.op("dv", lambda e: (e.drain(), e.tensor_tensor(ygb[:], y_tm[:], gw0[:], ALU.mult))[1], waits=[ev] + ev_y_evac, inc="Sdv")
            ev_gw_rd[0] = evg
            ev_slot_free[j] = evg
            cl = j
            for ft in range(2):
                s, evt, bkt = do_transpose(pl, (lambda ft=ft: ygb[:, ft * 128:(ft + 1) * 128]), [evg])
                evv = evac_T(pl, s, bkt, evt, (lambda ft=ft, cl=cl: yfmg[:, ft * GRP * 128 + cl * 128: ft * GRP * 128 + (cl + 1) * 128]), extra=[ev_yfmg_free])
                ev_yfm_last = evv
            evbe1 = pe_bank(pl, 2, lambda e, par=par: e.matmul(
                pb[2][0:1, 0:512], C["onehot127b"][:],
                mixb[:, par * 1024:par * 1024 + 512], start=True, stop=True),
                extra=[ev_mix, ev_pb2bnd_rd[0]])
            ev_mixb_rd[par].append(evbe1)
            evbe2 = pe_bank(pl, 7, lambda e, par=par: e.matmul(
                pb7[0:1, 0:256], C["onehot127b"][:],
                xhb[:, par * 256:(par + 1) * 256], start=True, stop=True),
                extra=[evxh])
            ev_xh_rd[par].append(evbe2)
            evbc = rd_bank(pl, "dv", 2, lambda e, par=par: e.tensor_copy(
                bndst[0:1, (1 - par) * 768:(1 - par) * 768 + 512], pb[2][0:1, 0:512]),
                extra=ev_bnd_rd[1 - par] + [ev_bnd_w[1 - par]])
            evbc = rd_bank(pl, "dv", 7, lambda e, par=par: e.tensor_copy(
                bndst[0:1, (1 - par) * 768 + 512:(1 - par) * 768 + 768], pb7[0:1, 0:256]),
                extra=[evbc])
            ev_bnd_w[1 - par] = evbc
            ev_bnd_rd[1 - par] = []
            ev_bnd_rd[par] = ev_bnd_reads

        plw = Plan(cnt)
        gate_all(plw, PH1 if gate is None else gate)
        evw = plw.op("gp", lambda e, p=p: e.dma_start(
            out=aW[:].rearrange("p (kc c) -> p kc c", kc=16),
            in_=din["attnW"][p * D:(p + 1) * D, :].rearrange("(kc q) c -> q kc c", q=128)),
            waits=[PH0[0], ev_aw_free], inc="Sgw", amt=16)
        ev_aw = evw
        for t in (hst, Bfm, bndst, smalls, sw):
            plw.op("gp", lambda e, t=t: e.memset(t[:], 0.0), inc="Sgp")
        plw.op("dv", lambda e: e.memset(offrow[:], 0.0), waits=[ev_off_w] if ev_off_w else [], inc="Sdv")
        ev_ms = plw.op("gp", lambda e: e.nop(), inc="Sgp")
        replay(nc, plw, sems)
        for par0 in range(2):
            ev_bnd_w[par0] = ev_ms
            ev_h_w[par0] = ev_ms
        ev_Bfm_rd[0] = [ev_ms]
        ev_Bfm_rd[1] = [ev_ms]

        ngr = (nch + GRP - 1) // GRP
        pl = Plan(cnt)
        n0 = min(GRP, nch)
        for ci in range(0, n0):
            stA(pl, ci)
            stA_tail(pl, ci, 0, n0)
        replay(nc, pl, sems)
        for g in range(ngr):
            pl = Plan(cnt)
            lo, hi = g * GRP, min((g + 1) * GRP, nch)
            nxt_lo, nxt_hi = (g + 1) * GRP, min((g + 2) * GRP, nch)
            stB_proj(pl, lo)
            pend = dict(stB_evac_xz(pl, lo), **stB_evac_mix(pl, lo))
            for ci in range(lo, hi):
                nxt = {}
                if ci + 1 < hi:
                    stB_proj(pl, ci + 1)
                    nxt.update(stB_evac_xz(pl, ci + 1))
                stB_rest(pl, ci, pend)
                if ci + 1 < hi:
                    nxt.update(stB_evac_mix(pl, ci + 1))
                pend = nxt
                cin = nxt_lo + (ci - lo)
                if cin < nxt_hi:
                    stA(pl, cin)
                    stA_tail(pl, cin, nxt_lo, nxt_hi)
            ev_fl = None
            for ft in range(2):
                ev_fl = pl.op("sy", lambda e, ft=ft, g=g, p=p: e.dma_start(
                    out=y_c[g, (p * 256 + ft * 128):(p * 256 + (ft + 1) * 128),
                            0:(hi - lo) * 128],
                    in_=yfmg[:, ft * GRP * 128:ft * GRP * 128 + (hi - lo) * 128]),
                    waits=[ev_yfm_last], inc="Syc", amt=16)
            ev_yfmg_free = ev_fl
            flush_evs.append(ev_fl)
            if p0_flush is not None and g < len(p0_flush):
                ag_evs.append(pl.op("gp", lambda e, g=g: e.collective_compute(
                    "AllGather", ALU.bypass, rg,
                    ins=[y_c.ap()[g].opt()], outs=[y_g.ap()[g].opt()]),
                    waits=[ev_fl, p0_flush[g]], inc="Scc"))
            replay(nc, pl, sems)
        return ev_yfmg_free, ev_proj_last, flush_evs, ag_evs

    ev_f0, ev_pj0, fl0, _ = emit_pass(0, nchunks, None)
    if two_pass:
        PB1 = [(s, cnt[s]) for s in SEMS if cnt[s] > 0]
        ev_f1, _, fl1, ags = emit_pass(1, nchunks, ev_pj0, ev_yfmg_free0=ev_f0,
                                       gate=PB1, p0_flush=fl0)
    else:
        ev_f1 = ev_f0
        ags = []

    pl = Plan(cnt)
    if ags:
        ev_ag2 = ags[-1]
    else:
        for g4 in range(4):
            ev_ag2 = pl.op("gp", lambda e, g4=g4: e.collective_compute(
                "AllGather", ALU.bypass, rg,
                ins=[y_c.ap()[g4].opt()], outs=[y_g.ap()[g4].opt()]),
                waits=[ev_f0, ev_f1], inc="Scc")
    if "mix" in debug:
        ng = min(GRP, nchunks)
        for nm, t, pp, w in (("mixb", mixb, 128, 2048), ("xhb", xhb, 128, 512),
                             ("zbv", zb, 128, 512), ("csb", csb, 128, ng * 512),
                             ("smalls", smalls, 128, ng * 48),
                             ("thred", thred, 128, ng * 256),
                             ("ytm", y_tm, 128, HPP * P), ("tsb", thx, 128, 256),
                             ("offrow", offrow, 1, 256), ("Bfm", Bfm, 128, 1024)):
            dt_ = t[:].dtype
            dbg_outs[nm] = nc.dram_tensor("dbg_" + nm, [pp, w], dt_,
                                          kind="ExternalOutput").ap()
            pl.op("sy", lambda e, nm=nm, t=t, pp=pp, w=w: e.dma_start(
                out=dbg_outs[nm], in_=t[0:pp, 0:w]), waits=pl.barrier(), inc="Ssy", amt=16)
        pl.op("sy", lambda e: None, waits=pl.barrier())
    PH2 = pl.barrier()
    replay(nc, pl, sems)
    ph2.close()

    if "y_g" in debug:
        pl = Plan(cnt)
        ph = contextlib.ExitStack()
        t = ph.enter_context(nc.sbuf_tensor("dbgy", [128, 8192], BF16))
        dbg_outs["y_g"] = nc.dram_tensor("dbg_y_g", [128, 8192], BF16,
                                         kind="ExternalOutput").ap()
        ev = pl.op("sy", lambda e: e.dma_start(out=t[:, 0:2048], in_=y_g[0, 0:128, :]),
                   waits=[ev_ag2], inc="Ssy", amt=16)
        ev = pl.op("sy", lambda e: e.dma_start(out=t[:, 4096:6144], in_=y_g[0, 256:384, :]),
                   waits=[ev_ag2, ev], inc="Ssy", amt=16)
        ev = pl.op("sy", lambda e: e.dma_start(out=dbg_outs["y_g"], in_=t[:]),
                   waits=[ev], inc="Ssy", amt=16)
        pl.op("sy", lambda e: None, waits=pl.barrier())
        replay(nc, pl, sems)
        ph.close()

    if upto <= 2:
        return finish_dummy()

    ph34 = contextlib.ExitStack()
    x2fm = ph34.enter_context(nc.sbuf_tensor("x2fm", [128, 16 * TPC], BF16))
    gslots = ph34.enter_context(nc.sbuf_tensor("gslots", [128, NTT * 4], I32))
    ph3 = contextlib.ExitStack()
    EP = ph3.enter_context
    ygsb = EP(nc.sbuf_tensor("ygsb", [128, 32 * TPC], BF16))
    wosb = EP(nc.sbuf_tensor("wosb", [128, 32 * 512], BF16))
    xts = [EP(nc.sbuf_tensor(f"p3x{i}", [128, D], F32)) for i in range(2)]
    x2t = EP(nc.sbuf_tensor("x2t", [128, D], F32))
    scr3 = EP(nc.sbuf_tensor("scr3", [128, D], F32))
    scr3b = EP(nc.sbuf_tensor("scr3b", [128, D], F32))
    ln3 = EP(nc.sbuf_tensor("ln3", [128, 8], F32))
    x2lnb = EP(nc.sbuf_tensor("x2lnb", [128, D], BF16))
    gwb = EP(nc.sbuf_tensor("gwb", [128, 16 * 16], BF16))
    glog = EP(nc.sbuf_tensor("glog", [128, 16], F32))
    mx4 = EP(nc.sbuf_tensor("mx4", [128, 8], F32))
    idx4 = EP(nc.sbuf_tensor("idx4", [128, 4], F32))
    w4 = EP(nc.sbuf_tensor("w4", [128, 8], F32))
    rnk4 = EP(nc.sbuf_tensor("rnk4", [128, 4], F32))
    tt16 = EP(nc.sbuf_tensor("tt16", [128, 16], F32))
    tt16b = EP(nc.sbuf_tensor("tt16b", [128, 16], F32))
    macc = EP(nc.sbuf_tensor("macc", [128, 16], F32))
    mu8 = EP(nc.sbuf_tensor("mu8", [128, 16], mybir.dt.uint8))
    ku8 = EP(nc.sbuf_tensor("ku8", [128, 4], mybir.dt.uint8))
    ohall = EP(nc.sbuf_tensor("ohall", [128, NTT * 64], F32))
    ohSall = EP(nc.sbuf_tensor("ohSall", [128, NTT * 16], F32))
    gslf = EP(nc.sbuf_tensor("gslf", [128, 4], F32))
    cntsb = EP(nc.sbuf_tensor("cntsb", [NCORE, E], F32))
    base_r = EP(nc.sbuf_tensor("base_r", [1, E], F32))
    offs_r = EP(nc.sbuf_tensor("offs_r", [1, E], F32))
    snt = EP(nc.sbuf_tensor("snt", [128, 256], F32))
    clsrc = EP(nc.sbuf_tensor("clsrc", [128, 8], F32))

    evT3b = [None, None]

    pl = Plan(cnt)
    gate_all(pl, PH2)
    ev_yg = None
    yg_flat = y_g.ap().flatten().unsqueeze(1)
    for kc in range(32):
        ev_yg = pl.op("gp", lambda e, kc=kc: e.indirect_dma_start(
            ygsb[:, kc * TPC:(kc + 1) * TPC], None,
            yg_flat, IndirectOffsetOnAxis(C["ygidx"][:, kc:kc + 1], 0),
            oob_is_err=True),
            waits=[ev_ag2], inc="Sg0", amt=16)
    ev_wo_used = [None]
    ev_x_used = [None, None]
    ev_xij = {}
    ev_x2dw = None
    for jg in range(4):
        sl = 0
        evw_h = []
        for hf, eng in ((0, "gp"), (1, "gp")):
            evw_h.append(pl.op(eng, lambda e, jg=jg, hf=hf: e.dma_start(
                out=wosb[:, hf * 8192:(hf + 1) * 8192].rearrange("p (kc c) -> p kc c", kc=16),
                in_=din["WoutT"][jg * 4096 + hf * 2048: jg * 4096 + (hf + 1) * 2048, :]
                .rearrange("(kc q) c -> q kc c", q=128)),
                waits=[ev_wo_used[0]], inc="Swo0" if jg % 2 == 0 else "Swo1", amt=16))
        ev_wold = evw_h
        evm_last = None
        for i in range(NTT):
            xsl = (jg * NTT + i) % 2
            ev_xij[(jg, i)] = pl.op("sy", lambda e, i=i, jg=jg, xsl=xsl: e.dma_start(
                out=xts[xsl][:, 0:512],
                in_=din["x_slice"][i * 128:(i + 1) * 128, jg * 512:(jg + 1) * 512]),
                waits=[ev_x_used[xsl]], inc="Sx0" if xsl == 0 else "Sx1", amt=16)
            bnk = (jg * NTT + i) % 3
            evm = None
            for kc in range(32):
                evm = pe_bank(pl, bnk, lambda e, i=i, jg=jg, kc=kc, bnk=bnk: e.matmul(
                    pb[bnk][:, 0:512],
                    ygsb[:, kc * TPC + i * 128: kc * TPC + (i + 1) * 128],
                    wosb[:, kc * 512:(kc + 1) * 512],
                    start=(kc == 0), stop=(kc == 31)),
                    extra=[ev_yg] + ev_wold, start_new=(kc == 0))
            evm_last = evm
            ev = rd_bank(pl, "dv", bnk, lambda e, i=i, jg=jg, xsl=xsl, bnk=bnk: e.tensor_tensor(
                x2t[:, 0:512], pb[bnk][:, 0:512], xts[xsl][:, 0:512], ALU.add),
                extra=[ev_xij[(jg, i)], ev_x2dw])
            ev_x_used[xsl] = ev
            ev_x2dw = pl.op("sy", lambda e, i=i, jg=jg: e.dma_start(
                out=x2_d[i * 128:(i + 1) * 128, jg * 512:(jg + 1) * 512],
                in_=x2t[:, 0:512]), waits=[ev], inc="Sx2", amt=16)
        ev_wo_used[0] = evm_last
    ev_x2r = [ev_x_used[0], ev_x_used[1]]
    ev_x2fm = {}
    ev_x2c = None
    ev_bf_prev = None
    ev_x2fm_last = None
    for i in range(NTT):
        xsl = i % 2
        ev_ld = pl.op("sy", lambda e, i=i, xsl=xsl: e.dma_start(
            out=xts[xsl][:], in_=x2_d[i * 128:(i + 1) * 128, :]),
            waits=[ev_x2r[xsl], (("Sx2", cnt["Sx2"]) if i == 0 else None)],
            inc="Sxw0" if xsl == 0 else "Sxw1", amt=16)
        ev_bf = emit_ln_tile(pl, (lambda xsl=xsl: xts[xsl][:]), x2lnb, i, ev_ld,
                             scr3, scr3b, ln3,
                             out_free_ev=[ev_x2fm.get(i - 1), ev_x2c])
        ev_x2r[xsl] = ev_bf
        ev_x2c = pl.op("sy", lambda e, i=i: e.dma_start(
            out=x2ln_c[i * 128:(i + 1) * 128, :], in_=x2lnb[:]),
            waits=[ev_bf, ev_x2c], inc="Sxc", amt=16)
        last_t = None
        for kc in range(16):
            bank = kc % 2
            slot = bank * 8 + (kc // 2) % 8
            evt = pl.op("pe", lambda e, kc=kc, slot=slot: e.transpose(
                pbT[:, slot * 128:(slot + 1) * 128],
                x2lnb[:, kc * 128:(kc + 1) * 128], idb[:]),
                waits=[ev_bf, evT3b[bank]], inc="Spe")
            evv = pl.op("dv", lambda e, kc=kc, i=i, slot=slot: e.tensor_copy(
                x2fm[:, kc * TPC + i * 128: kc * TPC + (i + 1) * 128],
                pbT[:, slot * 128:(slot + 1) * 128]),
                waits=[evt], inc="Sdv")
            evT3b[bank] = evv
            last_t = evt
            ev_x2fm_last = evv
        ev_x2fm[i] = last_t
    for i in range(NTT):
        ev_x2fm[i] = ev_x2fm_last
    ev_ag3 = pl.op("gp", lambda e: e.collective_compute(
        "AllGather", ALU.bypass, rg, ins=[x2ln_c.ap().opt()], outs=[x2ln_g.ap().opt()]),
        waits=[ev_x2c], inc="Scc")
    replay(nc, pl, sems)

    def sntbig4():
        return snt[:, 0:4]

    pl = Plan(cnt)
    ev_gsl = {}
    ev_ogl = {}
    ev_gw = pl.op("gp", lambda e, bnk=bnk: e.dma_start(
        out=gwb[:].rearrange("p (kc e) -> p kc e", kc=16),
        in_=din["gate_wT"].rearrange("(kc p) e -> p kc e", p=128)),
        inc="Sgw", amt=16)
    ev_cnt_mm = {}
    for i in range(NTT):
        bnk = 3 + i % 2
        evm = None
        for kc in range(16):
            evm = pe_bank(pl, bnk, lambda e, i=i, kc=kc, bnk=bnk: e.matmul(
                pb[bnk][:, 0:16],
                x2fm[:, kc * TPC + i * 128: kc * TPC + (i + 1) * 128],
                gwb[:, kc * 16:(kc + 1) * 16],
                start=(kc == 0), stop=(kc == 15)),
                extra=[ev_gw, ev_x2fm[i]], start_new=(kc == 0))
        ev = rd_bank(pl, "dv", bnk, lambda e, bnk=bnk: e.tensor_copy(glog[:], pb[bnk][:, 0:16]),
                     extra=[ev_gsl.get(i - 2)])
        for k in range(4):
            ev = pl.op("dv", lambda e, k=k: (e.drain(), e.tensor_reduce(
                mx4[:, k:k + 1], glog[:], AXX, ALU.max))[1], waits=[ev], inc="Sdv")
            ev = pl.op("dv", lambda e, k=k: (e.drain(), e.tensor_scalar(
                tt16[:], glog[:], mx4[:, k:k + 1], None, ALU.is_ge))[1], waits=[ev], inc="Sdv")
            ev = pl.op("dv", lambda e: (e.drain(), e.tensor_tensor(
                tt16b[:], tt16[:], C["RIOTA16"][:], ALU.mult))[1], waits=[ev], inc="Sdv")
            ev = pl.op("dv", lambda e, k=k: (e.drain(), e.tensor_reduce(
                idx4[:, k:k + 1], tt16b[:], AXX, ALU.max))[1], waits=[ev], inc="Sdv")
            ev = pl.op("dv", lambda e, k=k: (e.drain(), e.tensor_scalar(
                idx4[:, k:k + 1], idx4[:, k:k + 1], -1.0, 15.0, ALU.mult, ALU.add))[1],
                waits=[ev], inc="Sdv")
            if k < 3:
                ev = pl.op("dv", lambda e, k=k: (e.drain(), e.tensor_scalar(
                    mu8[:], glog[:], mx4[:, k:k + 1], None, ALU.is_ge))[1], waits=[ev], inc="Sdv")
                ev = pl.op("dv", lambda e: (e.drain(), e.copy_predicated(
                    glog[:], mu8[:], C["NEG16"][:]))[1], waits=[ev], inc="Sdv")
        ev = pl.op("dv", lambda e: (e.drain(), e.tensor_scalar(
            w4[:, 0:4], mx4[:, 0:4], mx4[:, 0:1], None, ALU.subtract))[1], waits=[ev], inc="Sdv")
        ev = pl.op("sc", lambda e: e.activation(w4[:, 4:8], w4[:, 0:4], AF.Exp), waits=[ev], inc="Ssc")
        ev = pl.op("dv", lambda e: e.tensor_reduce(w4[:, 0:1], w4[:, 4:8], AXX, ALU.add), waits=[ev], inc="Sdv")
        ev = pl.op("dv", lambda e: (e.drain(), e.reciprocal(w4[:, 1:2], w4[:, 0:1]))[1], waits=[ev], inc="Sdv")
        ev = pl.op("dv", lambda e: (e.drain(), e.tensor_scalar(
            w4[:, 4:8], w4[:, 4:8], w4[:, 1:2], None, ALU.mult))[1], waits=[ev], inc="Sdv")
        ev_w4 = ev
        ev = pl.op("dv", lambda e, i=i: (e.drain(), e.tensor_tensor(
            ohall[:, i * 64:(i + 1) * 64].rearrange("p (k e) -> p k e", k=4),
            idx4[:].unsqueeze(2).broadcast_to([128, 4, 16]),
            C["IOTA16"][:].unsqueeze(1).broadcast_to([128, 4, 16]), ALU.is_equal))[1],
            waits=[ev], inc="Sdv")
        ev = pl.op("dv", lambda e, i=i: (e.drain(), e.tensor_reduce(
            ohSall[:, i * 16:(i + 1) * 16].unsqueeze(2),
            ohall[:, i * 64:(i + 1) * 64].rearrange("p (k e) -> p e k", k=4),
            AXX, ALU.add))[1], waits=[ev], inc="Sdv")
        ev_ogl[i] = ev
        ev_gsl[i] = ev
    evc = None
    for i in range(NTT):
        evc = pe_bank(pl, 7, lambda e, i=i: e.matmul(
            pb7[0:1, 0:16], C["ones_col"][:], ohSall[:, i * 16:(i + 1) * 16],
            start=(i == 0), stop=(i == NTT - 1)), extra=[ev_ogl[i]], start_new=(i == 0))
    evc = rd_bank(pl, "dv", 7, lambda e: e.tensor_copy(base_r[:], pb7[0:1, 0:16]))
    evc = pl.op("sy", lambda e: e.dma_start(out=cnt_c.ap(), in_=base_r[:]),
                waits=[evc], inc="Scnt", amt=16)
    ev_ag4 = pl.op("gp", lambda e: e.collective_compute(
        "AllGather", ALU.bypass, rg, ins=[cnt_c.ap().opt()], outs=[cnt_g.ap().opt()]),
        waits=[evc], inc="Scc")
    pl.op("gp", lambda e: e.reg_mov(rbnd, E * CAP - 1))
    evsn0 = pl.op("dv", lambda e: e.memset(snt[:], SENT), inc="Sdv")
    evsn = pl.op("sy", lambda e: e.dma_start(
        out=clm_c.ap().rearrange("(a b) c -> a (b c)", a=128), in_=snt[:]),
        waits=[evsn0], inc="Ssy", amt=16)
    evb = pl.op("sy", lambda e: e.dma_start(out=cntsb[:], in_=cnt_g.ap()),
                waits=[ev_ag4], inc="Scn2", amt=16)
    evb = pe_bank(pl, 2, lambda e, bnk=bnk: e.matmul(pb[2][0:1, 0:16], C["coremask"][:],
                                            cntsb[:], start=True, stop=True),
                  extra=[evb])
    evb = rd_bank(pl, "dv", 2, lambda e, bnk=bnk: e.tensor_copy(offs_r[:], pb[2][0:1, 0:16]))
    ev_offs = evb
    ev_clsc = []
    for i in range(NTT):
        bnk = 3 + i % 2
        ev1 = pe_bank(pl, bnk, lambda e, i=i, bnk=bnk: e.matmul(
            pb[bnk][:, 32:48], C["L0strict"][:], ohSall[:, i * 16:(i + 1) * 16],
            start=True, stop=False), extra=[ev_ogl[i]])
        ev1 = pe_bank(pl, bnk, lambda e, bnk=bnk: e.matmul(
            pb[bnk][:, 32:48], C["ones_row"][:], offs_r[:],
            start=False, stop=True), extra=[ev_offs], start_new=False)
        evcm = rd_bank(pl, "dv", bnk, lambda e, bnk=bnk: e.tensor_copy(macc[:], pb[bnk][:, 32:48]))
        ev2 = pe_bank(pl, bnk, lambda e, i=i, bnk=bnk: e.matmul(
            pb[bnk][0:1, 48:64], C["ones_col"][:], ohSall[:, i * 16:(i + 1) * 16],
            start=True, stop=True), extra=[evcm], start_new=False)
        ev_offs = rd_bank(pl, "dv", bnk, lambda e, bnk=bnk: e.tensor_tensor(
            offs_r[:], offs_r[:], pb[bnk][0:1, 48:64], ALU.add), extra=[ev_offs, ev2])
        ev = evcm
        for k in range(4):
            ohk = lambda kk, ii: ohall[:, ii * 64 + kk * 16: ii * 64 + (kk + 1) * 16]
            ev = pl.op("dv", lambda e, k=k, i=i: (e.drain(), e.tensor_tensor(
                tt16[:], ohk(k, i), macc[:], ALU.mult))[1], waits=[ev], inc="Sdv")
            ev = pl.op("dv", lambda e, k=k: (e.drain(), e.tensor_reduce(
                rnk4[:, k:k + 1], tt16[:], AXX, ALU.add))[1], waits=[ev], inc="Sdv")
            if k < 3:
                ev = pl.op("dv", lambda e, k=k, i=i: (e.drain(), e.tensor_tensor(
                    macc[:], macc[:], ohk(k, i), ALU.add))[1], waits=[ev], inc="Sdv")
        ev = pl.op("dv", lambda e, i=i: (e.drain(), e.tensor_tensor(
            ohall[:, i * 64:(i + 1) * 64].rearrange("p (k e) -> p k e", k=4),
            ohall[:, i * 64:(i + 1) * 64].rearrange("p (k e) -> p k e", k=4),
            C["IOTA16"][:].unsqueeze(1).broadcast_to([128, 4, 16]), ALU.mult))[1],
            waits=[ev], inc="Sdv")
        ev = pl.op("dv", lambda e, i=i: (e.drain(), e.tensor_reduce(
            idx4[:].unsqueeze(2), ohall[:, i * 64:(i + 1) * 64].rearrange("p (k e) -> p k e", k=4),
            AXX, ALU.add))[1], waits=[ev], inc="Sdv")
        ev = pl.op("dv", lambda e: (e.drain(), e.scalar_tensor_tensor(
            gslf[:], idx4[:], 1024.0, rnk4[:], ALU.mult, ALU.add))[1], waits=[ev], inc="Sdv")
        ev = pl.op("dv", lambda e: (e.drain(), e.tensor_scalar(
            ku8[:], rnk4[:], float(CAP), None, ALU.is_ge))[1], waits=[ev], inc="Sdv")
        ev = pl.op("dv", lambda e: (e.drain(), e.copy_predicated(
            gslf[:], ku8[:], sntbig4()))[1],
            waits=[ev, evsn0], inc="Sdv")
        ev = pl.op("dv", lambda e, i=i: (e.drain(), e.tensor_copy(
            gslots[:, i * 4:(i + 1) * 4], gslf[:]))[1], waits=[ev], inc="Sdv")
        ev_gsl[i] = ev
        ev = pl.op("dv", lambda e, i=i: (e.drain(), e.tensor_copy(
            clsrc[:].rearrange("p (k two) -> p two k", two=2)[:, 0, :],
            C["tokids"][:, i:i + 1].broadcast_to([128, 4])))[1],
            waits=[ev, ev_w4, (ev_clsc[-1] if ev_clsc else None)], inc="Sdv")
        ev = pl.op("dv", lambda e: (e.drain(), e.tensor_copy(
            clsrc[:].rearrange("p (k two) -> p two k", two=2)[:, 1, :], w4[:, 4:8]))[1],
            waits=[ev], inc="Sdv")
        for k in range(4):
            evs = pl.op("gp", lambda e, i=i, k=k: e.indirect_dma_start(
                clm_c.ap(), IndirectOffsetOnAxis(gslots[:, i * 4 + k:i * 4 + k + 1], 0),
                clsrc[:, 2 * k:2 * k + 2], None,
                bounds_check=rbnd, oob_is_err=False),
                waits=[ev, ev_gsl[i], evsn], inc="Scl", amt=16)
            ev_clsc.append(evs)
    ev_ar = pl.op("gp", lambda e: e.collective_compute(
        "AllReduce", ALU.min, rg, ins=[clm_c.ap().opt()], outs=[clm_g.ap().opt()]),
        waits=[evsn] + ev_clsc[-1:], inc="Scc")
    PH3 = pl.barrier()
    replay(nc, pl, sems)

    if upto <= 3:
        ph3.close()
        return finish_dummy()

    B3 = pl.barrier()
    ph3.close()
    gate5 = [(s, v) for (s, v) in B3 if s != "Scc"]
    ph5 = contextlib.ExitStack()
    EP = ph5.enter_context
    clmt = EP(nc.sbuf_tensor("clmt", [128, 32], F32))
    toki = EP(nc.sbuf_tensor("toki", [128, 16], I32))
    wbufc = EP(nc.sbuf_tensor("wbufc", [128, 16], F32))
    xg_tm = EP(nc.sbuf_tensor("xg_tm", [128, 8 * D], BF16))
    xg_fm = EP(nc.sbuf_tensor("xg_fm", [128, 16 * CAP], BF16))
    ew1 = EP(nc.sbuf_tensor("ew1", [128, 2 * 16 * 128], BF16))
    ew2 = EP(nc.sbuf_tensor("ew2", [128, 2 * 16 * 128], BF16))
    midE = EP(nc.sbuf_tensor("midE", [128, 11 * CAP], BF16))
    ewd = EP(nc.sbuf_tensor("ewd", [128, 2 * 11 * 512], BF16))
    ge_sb = EP(nc.sbuf_tensor("ge_sb", [128, CAP], F32))
    ee_sb = EP(nc.sbuf_tensor("ee_sb", [128, CAP], F32))
    re_sb = EP(nc.sbuf_tensor("re_sb", [128, CAP], F32))
    eo_sb = EP(nc.sbuf_tensor("eo_sb", [128, 2 * 512], BF16))

    pl = Plan(cnt)
    gate_all(pl, gate5)
    clm_flat = clm_g.ap().flatten().unsqueeze(1)
    ev_clm = None
    for j in range(16):
        ev_clm = pl.op("gp", lambda e, j=j: e.indirect_dma_start(
            clmt[:, 2 * j:2 * j + 2], None,
            clm_flat, IndirectOffsetOnAxis(C["clmidx"][:, j:j + 1], 0),
            oob_is_err=True),
            waits=[ev_ar], inc="Sg0", amt=16)
    ev = pl.op("dv", lambda e: e.tensor_copy(
        toki[:], clmt[:].rearrange("p (j two) -> p two j", two=2)[:, 0, :]),
        waits=[ev_clm], inc="Sdv")
    ev_tok = pl.op("dv", lambda e: (e.drain(), e.tensor_copy(
        wbufc[:], clmt[:].rearrange("p (j two) -> p two j", two=2)[:, 1, :]))[1],
        waits=[ev], inc="Sdv")
    pl.op("gp", lambda e: e.reg_mov(rbnd, L - 1))
    ev_eoutw = None
    ev_xg_lastT = None
    ev_xgfm_rd = None
    ev_midE_rd = None
    ev_ewd = [None, None]
    ev_eo_used = [None, None]
    for el in range(EPC):
        ev_ms5 = pl.op("gp", lambda e: e.memset(xg_tm[:], 0.0),
                       waits=[ev_xg_lastT], inc="Sgp")
        ev_xg = None
        for ch in range(8):
            ev_xg = pl.op("gp", lambda e, el=el, ch=ch: e.indirect_dma_start(
                xg_tm[:, ch * D:(ch + 1) * D], None,
                x2ln_g.ap(), IndirectOffsetOnAxis(toki[:, el * 8 + ch: el * 8 + ch + 1], 0),
                bounds_check=rbnd, oob_is_err=False),
                waits=[ev_tok, ev_ms5, ev_ag3], inc="Sg1", amt=16)
        ev_fm = None
        for ch in range(8):
            for kc in range(16):
                bank = (ch * 16 + kc) % 2
                slot = bank * 8 + ((ch * 16 + kc) // 2) % 8
                evt = pl.op("pe", lambda e, ch=ch, kc=kc, slot=slot: e.transpose(
                    pbT[:, slot * 128:(slot + 1) * 128],
                    xg_tm[:, ch * D + kc * 128: ch * D + (kc + 1) * 128], idb[:]),
                    waits=[ev_xg, evT3b[bank]], inc="Spe")
                evv = pl.op("dv", lambda e, ch=ch, kc=kc, slot=slot: e.tensor_copy(
                    xg_fm[:, kc * CAP + ch * 128: kc * CAP + (ch + 1) * 128],
                    pbT[:, slot * 128:(slot + 1) * 128]),
                    waits=[evt, ev_xgfm_rd], inc="Sdv")
                evT3b[bank] = evv
                ev_fm = evv
                ev_xg_lastT = evt
        ev_ew1u = [None, None]
        ev_ew2u = [None, None]
        ev_resb_rd = [None, None]
        ev_midl = None
        for it in range(11):
            sl = it % 2
            evg = pl.op("gp", lambda e, el=el, it=it, sl=sl: e.dma_start(
                out=ew1[:, sl * 2048:(sl + 1) * 2048].rearrange("p (kc c) -> p kc c", kc=16),
                in_=din["wg2"][el, it * D:(it + 1) * D, :].rearrange("(kc q) c -> q kc c", q=128)),
                waits=[ev_ew1u[sl]], inc="Ssh0" if sl == 0 else "Ssh1", amt=16)
            evu = pl.op("gp", lambda e, el=el, it=it, sl=sl: e.dma_start(
                out=ew2[:, sl * 2048:(sl + 1) * 2048].rearrange("p (kc c) -> p kc c", kc=16),
                in_=din["wu2"][el, it * D:(it + 1) * D, :].rearrange("(kc q) c -> q kc c", q=128)),
                waits=[ev_ew2u[sl]], inc="Sgt0" if sl == 0 else "Sgt1", amt=16)
            for jt in range(2):
                bgate = jt % 2
                bup = 2 + jt % 2
                for kc in range(16):
                    pe_bank(pl, bgate, lambda e, it=it, jt=jt, kc=kc, sl=sl, bgate=bgate, bup=bup: e.matmul(
                        pb[bgate][:, 0:512],
                        ew1[:, sl * 2048 + kc * 128: sl * 2048 + (kc + 1) * 128],
                        xg_fm[:, kc * CAP + jt * 512: kc * CAP + (jt + 1) * 512],
                        start=(kc == 0), stop=(kc == 15)),
                        extra=[evg, ev_fm], start_new=(kc == 0))
                for kc in range(16):
                    pe_bank(pl, bup, lambda e, it=it, jt=jt, kc=kc, sl=sl, bup=bup: e.matmul(
                        pb[bup][:, 0:512],
                        ew2[:, sl * 2048 + kc * 128: sl * 2048 + (kc + 1) * 128],
                        xg_fm[:, kc * CAP + jt * 512: kc * CAP + (jt + 1) * 512],
                        start=(kc == 0), stop=(kc == 15)),
                        extra=[evu, ev_fm], start_new=(kc == 0))
                co = jt * 512
                ev = rd_bank(pl, "sc", bgate, lambda e, co=co, bgate=bgate: e.activation(
                    re_sb[:, co:co + 512], pb[bgate][:, 0:512], AF.Silu),
                    extra=[ev_resb_rd[jt]])
                ev = rd_bank(pl, "dv", bup, lambda e, it=it, co=co, bup=bup: e.tensor_tensor(
                    midE[:, it * CAP + co: it * CAP + co + 512], re_sb[:, co:co + 512],
                    pb[bup][:, 0:512], ALU.mult), extra=[ev, ev_midE_rd])
                ev_resb_rd[jt] = ev
                ev_midl = ev
            ev_ew1u[sl] = ev_midl
            ev_ew2u[sl] = ev_midl
        ev_xgfm_rd = ev_midl
        for jg in range(4):
            sl = jg % 2
            evw = pl.op("gp", lambda e, el=el, jg=jg, sl=sl: e.dma_start(
                out=ewd[:, sl * 5632:(sl + 1) * 5632].rearrange("p (ki c) -> p ki c", ki=11),
                in_=din["wd2"][el, jg * I_FF:(jg + 1) * I_FF, :].rearrange("(ki q) c -> q ki c", q=128)),
                waits=[ev_ewd[sl]], inc="Seo" if sl == 0 else "Seo1", amt=16)
            evm_last = None
            for i in range(8):
                bnk = (jg * 8 + i) % 3
                evm = None
                for ki in range(11):
                    evm = pe_bank(pl, bnk, lambda e, i=i, jg=jg, ki=ki, sl=sl, bnk=bnk: e.matmul(
                        pb[bnk][:, 0:512],
                        midE[:, ki * CAP + i * 128: ki * CAP + (i + 1) * 128],
                        ewd[:, sl * 5632 + ki * 512: sl * 5632 + (ki + 1) * 512],
                        start=(ki == 0), stop=(ki == 10)),
                        extra=[evw, ev_midl], start_new=(ki == 0))
                evm_last = evm
                esl = (jg * 8 + i) % 2
                ev = rd_bank(pl, "dv", bnk, lambda e, el=el, i=i, esl=esl, bnk=bnk: e.tensor_scalar(
                    eo_sb[:, esl * 512:(esl + 1) * 512], pb[bnk][:, 0:512],
                    wbufc[:, el * 8 + i: el * 8 + i + 1], None, ALU.mult),
                    extra=[ev_tok, ev_eo_used[esl]])
                ev_eoutw = pl.op("sy", lambda e, el=el, i=i, jg=jg, esl=esl: e.dma_start(
                    out=eout_c.ap()[el * CAP + i * 128: el * CAP + (i + 1) * 128,
                                    jg * 512:(jg + 1) * 512],
                    in_=eo_sb[:, esl * 512:(esl + 1) * 512]),
                    waits=[ev], inc="Syc" if esl == 0 else "Syc1", amt=16)
                ev_eo_used[esl] = ev_eoutw
            ev_ewd[sl] = evm_last
        ev_midE_rd = evm_last
    scc_pre_ag5 = cnt["Scc"]
    ev_ag5 = pl.op("gp", lambda e: e.collective_compute(
        "AllGather", ALU.bypass, rg, ins=[eout_c.ap().opt()], outs=[eout_g.ap().opt()]),
        waits=[ev_eo_used[0], ev_eo_used[1]], inc="Scc")
    B5 = pl.barrier()
    replay(nc, pl, sems)
    ph5.close()

    gate4 = ([(s, v) for (s, v) in B5 if s != "Scc"]
             + ([("Scc", scc_pre_ag5)] if scc_pre_ag5 > 0 else []))
    ph4 = contextlib.ExitStack()
    EP = ph4.enter_context
    swt = EP(nc.sbuf_tensor("swt", [128, 2 * 16 * 128], BF16))
    swt2 = EP(nc.sbuf_tensor("swt2", [128, 2 * 16 * 128], BF16))
    mid2 = EP(nc.sbuf_tensor("mid2", [128, 22 * TPC], BF16))
    g_sb = EP(nc.sbuf_tensor("g_sb", [128, TPC], F32))
    e_sb = EP(nc.sbuf_tensor("e_sb", [128, TPC], F32))
    r_sb = EP(nc.sbuf_tensor("r_sb", [128, TPC], F32))
    swd_sb = EP(nc.sbuf_tensor("swd_sb", [128, 2 * 22 * 512], BF16))

    NIT = IS_FF // 128
    pl = Plan(cnt)
    gate_all(pl, gate4)
    ev_swu = [None, None]
    ev_swu2 = [None, None]
    ev_rsb_rd = [None, None]
    ev_mid = {}
    for it in range(NIT):
        sl = it % 2
        evg = pl.op("gp", lambda e, it=it, sl=sl: e.dma_start(
            out=swt[:, sl * 2048:(sl + 1) * 2048].rearrange("p (kc c) -> p kc c", kc=16),
            in_=din["swg"][it * D:(it + 1) * D, :].rearrange("(kc q) c -> q kc c", q=128)),
            waits=[ev_swu[sl]], inc="Ssh0" if sl == 0 else "Ssh1", amt=16)
        evu = pl.op("gp", lambda e, it=it, sl=sl: e.dma_start(
            out=swt2[:, sl * 2048:(sl + 1) * 2048].rearrange("p (kc c) -> p kc c", kc=16),
            in_=din["swu"][it * D:(it + 1) * D, :].rearrange("(kc q) c -> q kc c", q=128)),
            waits=[ev_swu2[sl]], inc="Sgt0" if sl == 0 else "Sgt1", amt=16)
        evac_done = []
        for jt in range(2):
            bg = (2 * it + jt) % 2
            bgate = (2 * it + jt) % 2
            bup = 2 + (2 * it + jt) % 2
            evmg = None
            evmu = None
            for kc in range(16):
                evmg = pe_bank(pl, bgate, lambda e, it=it, jt=jt, kc=kc, sl=sl, bgate=bgate, bup=bup: e.matmul(
                    pb[bgate][:, 0:512],
                    swt[:, sl * 2048 + kc * 128: sl * 2048 + (kc + 1) * 128],
                    x2fm[:, kc * TPC + jt * 512: kc * TPC + (jt + 1) * 512],
                    start=(kc == 0), stop=(kc == 15)),
                    extra=[evg, ev_x2fm_last], start_new=(kc == 0))
            for kc in range(16):
                evmu = pe_bank(pl, bup, lambda e, it=it, jt=jt, kc=kc, sl=sl, bup=bup: e.matmul(
                    pb[bup][:, 0:512],
                    swt2[:, sl * 2048 + kc * 128: sl * 2048 + (kc + 1) * 128],
                    x2fm[:, kc * TPC + jt * 512: kc * TPC + (jt + 1) * 512],
                    start=(kc == 0), stop=(kc == 15)),
                    extra=[evu, ev_x2fm_last], start_new=(kc == 0))
            co = jt * 512
            ev = rd_bank(pl, "sc", bgate, lambda e, co=co, bgate=bgate: e.activation(
                r_sb[:, co:co + 512], pb[bgate][:, 0:512], AF.Silu),
                extra=[ev_rsb_rd[jt]])
            ev = rd_bank(pl, "dv", bup, lambda e, it=it, co=co, bup=bup: e.tensor_tensor(
                mid2[:, it * TPC + co: it * TPC + co + 512], r_sb[:, co:co + 512],
                pb[bup][:, 0:512], ALU.mult), extra=[ev])
            ev_rsb_rd[jt] = ev
            evac_done.append(ev)
        ev_swu[sl] = evac_done[-1]
        ev_swu2[sl] = evac_done[-1]
        ev_mid[it] = evac_done[-1]
    ev_mid_all = ev_mid[NIT - 1]
    replay(nc, pl, sems)

    pl = Plan(cnt)
    ev_swd = [None, None]
    shstage = g_sb
    x2sb = e_sb
    ev_x2u = [None, None]
    ev_shw = None
    for jg in range(4):
        sl = jg % 2
        evw = pl.op("gp", lambda e, jg=jg, sl=sl: e.dma_start(
            out=swd_sb[:, sl * 11264:(sl + 1) * 11264].rearrange("p (ki c) -> p ki c", ki=NIT),
            in_=din["swd"][jg * IS_FF:(jg + 1) * IS_FF, :].rearrange("(ki q) c -> q ki c", q=128)),
            waits=[ev_swd[sl]], inc="Ssh0" if sl == 0 else "Ssh1", amt=16)
        evm_last = None
        for i in range(NTT):
            bnk = (jg * NTT + i) % 3
            sl2 = (jg * NTT + i) % 2
            ev_x2ld = pl.op("sy", lambda e, i=i, jg=jg, sl2=sl2: e.dma_start(
                out=x2sb[:, sl2 * 512:(sl2 + 1) * 512],
                in_=x2_d[i * 128:(i + 1) * 128, jg * 512:(jg + 1) * 512]),
                waits=[ev_x2u[sl2]], inc="Sxw0" if sl2 == 0 else "Sxw1", amt=16)
            evm = None
            for ki in range(NIT):
                evm = pe_bank(pl, bnk, lambda e, i=i, jg=jg, ki=ki, sl=sl, bnk=bnk: e.matmul(
                    pb[bnk][:, 0:512],
                    mid2[:, ki * TPC + i * 128: ki * TPC + (i + 1) * 128],
                    swd_sb[:, sl * 11264 + ki * 512: sl * 11264 + (ki + 1) * 512],
                    start=(ki == 0), stop=(ki == NIT - 1)),
                    extra=[evw, ev_mid_all], start_new=(ki == 0))
            evm_last = evm
            ev = rd_bank(pl, "dv", bnk, lambda e, bnk=bnk, sl2=sl2: e.tensor_tensor(
                shstage[:, 0:512], pb[bnk][:, 0:512], x2sb[:, sl2 * 512:(sl2 + 1) * 512],
                ALU.add), extra=[ev_shw, ev_x2ld])
            ev_x2u[sl2] = ev
            ev_shw = pl.op("sy", lambda e, i=i, jg=jg: e.dma_start(
                out=shps_d[i * 128:(i + 1) * 128, jg * 512:(jg + 1) * 512],
                in_=shstage[:, 0:512]), waits=[ev], inc="Sx2", amt=16)
        ev_swd[sl] = evm_last
    EV_SH_DONE = ("Sx2", cnt["Sx2"])
    B4 = pl.barrier()
    replay(nc, pl, sems)
    ph4.close()

    ph6 = contextlib.ExitStack()
    EP = ph6.enter_context
    gth = [EP(nc.sbuf_tensor(f"gth{k}", [128, D], BF16)) for k in range(8)]
    shrt = [EP(nc.sbuf_tensor(f"shrt{k}", [128, D], F32)) for k in range(2)]
    s1t = [EP(nc.sbuf_tensor(f"s1t{k}", [128, D], F32)) for k in range(2)]
    s2t = [EP(nc.sbuf_tensor(f"s2t{k}", [128, D], F32)) for k in range(2)]
    outt = [EP(nc.sbuf_tensor(f"outt{k}", [128, D], F32)) for k in range(2)]

    pl = Plan(cnt)
    gate_all(pl, B4)
    pl.op("gp", lambda e: e.reg_mov(rbnd, NCORE * EPC * CAP - 1))
    ev_out_dma = None
    ev_prev = {}
    for i in range(NTT):
        b = i % 2
        base = b * 4
        evs = [None] * 4
        for k in range(4):
            ev_msk = pl.op("dv", lambda e, kk=base + k: e.memset(gth[kk][:], 0.0),
                           waits=[ev_prev.get(f"g{base + k}")], inc="Sdv")
            evs[k] = pl.op("gp", lambda e, i=i, k=k, kk=base + k: e.indirect_dma_start(
                gth[kk][:], None, eout_g.ap(),
                IndirectOffsetOnAxis(gslots[:, i * 4 + k: i * 4 + k + 1], 0),
                bounds_check=rbnd, oob_is_err=False),
                waits=[ev_msk, ev_ag5, ev_gsl[i]],
                inc=["Sg1", "Swo0", "Swo1", "Seo"][k], amt=16)
        ev_shl = pl.op("sy", lambda e, i=i, b=b: e.dma_start(
            out=shrt[b][:], in_=shps_d[i * 128:(i + 1) * 128, :]),
            waits=[ev_prev.get(f"sh{b}")], inc="Sxw1", amt=16)
        ev = pl.op("dv", lambda e, b=b: e.tensor_tensor(
            s1t[b][:], gth[base][:], gth[base + 1][:], ALU.add),
            waits=[evs[0], evs[1], ev_prev.get(f"s1{b}")], inc="Sdv")
        ev_prev[f"g{base}"] = ev
        ev_prev[f"g{base + 1}"] = ev
        ev2 = pl.op("dv", lambda e, b=b: e.tensor_tensor(
            s2t[b][:], gth[base + 2][:], gth[base + 3][:], ALU.add),
            waits=[evs[2], evs[3], ev_prev.get(f"s2{b}")], inc="Sdv")
        ev_prev[f"g{base + 2}"] = ev2
        ev_prev[f"g{base + 3}"] = ev2
        ev3 = pl.op("dv", lambda e, b=b: (e.drain(), e.tensor_tensor(
            s1t[b][:], s1t[b][:], s2t[b][:], ALU.add))[1],
            waits=[ev, ev2], inc="Sdv")
        ev_prev[f"s2{b}"] = ev3
        ev5 = pl.op("dv", lambda e, b=b: e.tensor_tensor(
            outt[b][:], s1t[b][:], shrt[b][:], ALU.add),
            waits=[ev3, ev_shl, ev_prev.get(f"out{b}")], inc="Sdv")
        ev_prev[f"s1{b}"] = ev5
        ev_prev[f"sh{b}"] = ev5
        ev_out_dma = pl.op("sy", lambda e, i=i, b=b: e.dma_start(
            out=d_out[i * 128:(i + 1) * 128, :], in_=outt[b][:]),
            waits=[ev5], inc="Sou", amt=16)
        ev_prev[f"out{b}"] = ev_out_dma
    pl.op("sy", lambda e: None, waits=[ev_out_dma])
    replay(nc, pl, sems)
    ph6.close()
    ph34.close()
    outer.close()
    return nc, dbg_outs



def kernel(**inputs):
    per_core = host_prepare(inputs)
    shapes = {k: (v.shape, v.dtype) for k, v in per_core[0].items()}
    nc, _ = build_program(shapes)
    res = run_bass_kernel_spmd(nc, per_core, core_ids=list(range(NCORE)))
    outs = [np.asarray(r["out"]) for r in res.results]
    return np.concatenate(outs, axis=0).reshape(1, L, D)
